# version 11
# baseline (speedup 1.0000x reference)
import numpy as np
import concourse.bass as bass
import concourse.mybir as mybir
from concourse.bass_utils import run_bass_kernel_spmd

F32 = mybir.dt.float32
BF16 = mybir.dt.bfloat16
I32 = mybir.dt.int32
AF = mybir.ActivationFunctionType
ALU = mybir.AluOpType
AX = mybir.AxisListType

D = 4096
B = 2
N = 4096
TPC = 1024
NC = 8
HD = 128
HPC = 4
ES = HPC * HD
NCH = TPC // 128
RG = [[0, 1, 2, 3, 4, 5, 6, 7]]
ALPHA = float(1.0 / np.sqrt(HD))
INV_D = float(1.0 / D)
MEAN_W = float(2.0 ** -24)

_STAGE = 8


class Sem:
    def __init__(self, h):
        self.h = h
        self.n = 0


def build_nc(stage):
    dump = stage if stage in (21, 22, 23, 24) else None
    if dump in (21, 22):
        stage = 2
    elif dump in (23, 24):
        stage = 5
    nc = bass.Bass("TRN2", target_bir_lowering=False, debug=False)

    x_in = nc.dram_tensor("x_my", [TPC, D], F32, kind="ExternalInput").ap()
    wq_in = nc.dram_tensor("wq_my", [D, ES], F32, kind="ExternalInput").ap()
    wk_in = nc.dram_tensor("wk_my", [D, ES], F32, kind="ExternalInput").ap()
    wv_in = nc.dram_tensor("wv_my", [D, ES], F32, kind="ExternalInput").ap()
    bq_in = nc.dram_tensor("bq_my", [1, ES], F32, kind="ExternalInput").ap()
    bk_in = nc.dram_tensor("bk_my", [1, ES], F32, kind="ExternalInput").ap()
    bv_in = nc.dram_tensor("bv_my", [1, ES], F32, kind="ExternalInput").ap()
    gq_in = nc.dram_tensor("gq", [1, HD], F32, kind="ExternalInput").ap()
    gk_in = nc.dram_tensor("gk", [1, HD], F32, kind="ExternalInput").ap()
    wfs_in = nc.dram_tensor("wfs_my", [ES, D], F32, kind="ExternalInput").ap()
    wos_in = nc.dram_tensor("wos_my", [ES, D], F32, kind="ExternalInput").ap()
    bf_in = nc.dram_tensor("bf", [1, D], F32, kind="ExternalInput").ap()
    bo_in = nc.dram_tensor("bo", [1, D], F32, kind="ExternalInput").ap()
    out_t = nc.dram_tensor("out", [TPC, D], F32, kind="ExternalOutput").ap()

    xA = nc.dram_tensor("xA", [TPC, D], F32, kind="Internal").ap()
    xB = nc.dram_tensor("xB", [TPC, D], F32, kind="Internal").ap()
    xq_d = nc.dram_tensor("xq_d", [TPC, D], BF16, kind="Internal").ap()
    qn_d = nc.dram_tensor("qn_d", [N, ES], BF16, kind="Internal").ap()
    kn_d = nc.dram_tensor("kn_d", [N, ES], BF16, kind="Internal").ap()
    wqkv_d = nc.dram_tensor("wqkv_d", [D, 3 * ES], BF16, kind="Internal").ap()

    ws_i = nc.dram_tensor("ws_i", [1, 8], F32, kind="Internal").ap()
    ws_o = nc.dram_tensor("ws_o", [1, 8], F32, kind="Internal").ap()
    wint_i = nc.dram_tensor("wint_i", [ES, D], BF16, kind="Internal").ap()
    wintT_i = nc.dram_tensor("wintT_i", [D, ES], BF16, kind="Internal").ap()
    wintT_o = nc.dram_tensor("wintT_o", [NC, D, ES], BF16,
                             kind="Internal").ap()
    wom_d = nc.dram_tensor("wom_d", [ES, D], BF16, kind="Internal").ap()
    h2_i = [nc.dram_tensor(f"h2_i{j}", [512, D], BF16, kind="Internal").ap()
            for j in range(2)]
    h2_o = [nc.dram_tensor(f"h2_o{j}", [NC, 512, D], BF16, kind="Internal").ap()
            for j in range(2)]
    at_d = nc.dram_tensor("at_d", [ES, B * N], BF16, kind="Internal").ap()
    rs_i = nc.dram_tensor("rs_i", [B * N, D], BF16, kind="Internal").ap()
    rs_o = nc.dram_tensor("rs_o", [TPC, D], BF16, kind="Internal").ap()

    from contextlib import ExitStack
    with ExitStack() as es:
        ARENA_KB = 176
        arena_t = es.enter_context(
            nc.sbuf_tensor("arena", [128, ARENA_KB * 1024], mybir.dt.uint8))
        _ar = arena_t.ap()

        _pers = [0]

        def P(dt, cols):
            sz = cols * {F32: 4, BF16: 2, I32: 4}[dt]
            off = (_pers[0] + 63) // 64 * 64
            _pers[0] = off + sz
            assert _pers[0] <= 28 * 1024
            return _ar[:, off:off + sz].bitcast(dt)

        def R(off_kb, dt, cols):
            off = int(off_kb * 1024)
            sz = cols * {F32: 4, BF16: 2, I32: 4}[dt]
            assert 28 * 1024 <= off and off + sz <= ARENA_KB * 1024
            return _ar[:, off:off + sz].bitcast(dt)

        bo_bc = P(BF16, D)
        bf_bc = P(BF16, D)
        bq_bc = P(F32, ES)
        bk_bc = P(F32, ES)
        bv_bc = P(F32, ES)
        gq_bc = P(F32, HD)
        gk_bc = P(F32, HD)
        ones_bf = P(BF16, 32)
        onesr_f = P(F32, 128)
        onescol = P(F32, 16)
        rx1 = P(F32, 16)
        rx2 = P(F32, 16)
        wsrow = P(F32, 16)
        ws_bc = P(F32, 16)
        mwc_bc = P(F32, 16)
        lrow = P(F32, 512)
        st1 = P(F32, 16)
        st2 = P(F32, 16)
        st3 = P(F32, 16)
        stp = [P(F32, 32), P(F32, 32)]
        stw_t = P(F32, 256)
        st3b = [P(F32, 16), P(F32, 16)]


        xt = [R(28, F32, D), R(44, F32, D)]
        lnt = R(60, F32, D)
        scr = R(76, F32, D)
        xa = [R(92, F32, D), R(108, F32, D)]
        tq = R(124, F32, D)
        i32t = R(140, I32, D)
        xqbf = [R(156, BF16, D), R(164, BF16, D)]
        h2st = [R(156, BF16, D), R(164, BF16, D)]
        xqT = R(28, BF16, 32 * 1024)
        at_sb = R(28, BF16, 32 * 1024)
        wring = [R(92 + i, BF16, 512) for i in range(8)]
        yt_t = [R(100, F32, 512), R(102, F32, 512)]
        g_t = [R(104, F32, 512), R(106, F32, 512)]
        xo = [R(108, F32, 512), R(110, F32, 512)]
        xr = [R(112 + 2 * i, F32, 512) for i in range(4)]
        wq_sb = R(28, BF16, 32 * ES)
        wk_sb = R(60, BF16, 32 * ES)
        wv_sb = R(92, BF16, 32 * ES)
        V_sb = R(124, BF16, 32 * ES)
        hring = [R(156 + i, BF16, 512) for i in range(8)]
        qtmp = R(164, F32, 512)
        sqq = R(166, F32, 512)
        qnst = [R(168, BF16, 512), R(169, BF16, 512)]
        atst = [R(170, BF16, 512), R(171, BF16, 512)]
        rbc_sb = R(172, F32, 512)
        qT = R(28, BF16, HPC * N)
        kT = R(60, BF16, HPC * N)
        Eb = [R(92, BF16, 16 * 512), R(108, BF16, 16 * 512)]
        at4 = [R(28 + 16 * e, BF16, B * N) for e in range(4)]
        wo_sb = R(124, BF16, HPC * D)
        stg6 = [R(92, BF16, D), R(100, BF16, D)]
        rsb_t = R(156, BF16, 2048)
        xbt_t = R(160, F32, 2048)
        x3st = R(168, F32, 2048)

        ps2a = es.enter_context(nc.psum_tensor("ps2a", [128, 1024], F32))
        ps2b = es.enter_context(nc.psum_tensor("ps2b", [128, 1024], F32))
        pav = es.enter_context(nc.psum_tensor("pav", [128, 512], F32))
        pl = es.enter_context(nc.psum_tensor("pl", [128, 512], F32))
        prb = es.enter_context(nc.psum_tensor("prb", [128, 512], F32))
        pex = es.enter_context(nc.psum_tensor("pex", [128, 512], F32))
        PS8 = [ps2a.ap()[:, 0:512], ps2a.ap()[:, 512:1024],
               ps2b.ap()[:, 0:512], ps2b.ap()[:, 512:1024],
               pav.ap()[:], pl.ap()[:], prb.ap()[:], pex.ap()[:]]

        S = {}
        for nm in ["prep", "x", "w", "t", "h2", "stx", "stq", "sth", "stw",
                   "cc", "pe", "vs", "sv", "vg", "sg", "sf", "out",
                   "vv", "sc", "wqkv"]:
            S[nm] = Sem(es.enter_context(nc.semaphore("s_" + nm)))
        WR = [Sem(es.enter_context(nc.semaphore(f"s_wr{i}"))) for i in range(8)]
        HR = [Sem(es.enter_context(nc.semaphore(f"s_hr{i}"))) for i in range(8)]

        prog = {e: [] for e in ["sync", "gpsimd", "vector", "scalar", "tensor"]}

        def em(eng, fn):
            prog[eng].append(fn)

        def wt(eng, sem, val):
            if val > 0:
                em(eng, lambda e, s=sem, v=val: e.wait_ge(s.h, v))

        def load(sem, dst, src, transpose=False, serial=True, queue="sync"):
            if serial and sem.n > 0:
                wt(queue, sem, sem.n)
            def fn(e, d=dst, r=src, t=transpose, s=sem):
                e.dma_start(out=d, in_=r, transpose=t).then_inc(s.h, 16)
            em(queue, fn)
            sem.n += 16
            return sem.n

        def store(sem, dst, src, serial=True, queue="gpsimd"):
            if serial and sem.n > 0:
                wt(queue, sem, sem.n)
            def fn(e, d=dst, r=src, s=sem):
                e.dma_start(out=d, in_=r).then_inc(s.h, 16)
            em(queue, fn)
            sem.n += 16
            return sem.n

        SELF = {"vector": "vv", "scalar": "sc"}

        def op(eng, name, *args, inc=None, bar=False, **kw):
            sem = inc if inc is not None else (S[SELF[eng]] if bar else None)
            def fn(e, name=name, args=args, kw=kw, sem=sem):
                ins = getattr(e, name)(*args, **kw)
                if sem is not None:
                    ins.then_inc(sem.h, 1)
            em(eng, fn)
            if sem is not None:
                sem.n += 1
                if bar:
                    wt(eng, sem, sem.n)
                return sem.n
            return None

        def mm(out, lhsT, rhs, start, stop, inc=None):
            return op("tensor", "matmul", out, lhsT, rhs,
                      start=start, stop=stop, inc=inc)

        def cc(kind, alu, i_, o_):
            def fn(e, k=kind, a=alu, i=i_, o=o_):
                e.collective_compute(k, a, replica_groups=RG, ins=[i],
                                     outs=[o]).then_inc(S["cc"].h, 1)
            em("gpsimd", fn)
            S["cc"].n += 1
            return S["cc"].n

        load(S["prep"], bq_bc[0:1, :], bq_in, serial=False)
        load(S["prep"], bk_bc[0:1, :], bk_in, serial=False)
        load(S["prep"], bv_bc[0:1, :], bv_in, serial=False)
        load(S["prep"], gq_bc[0:1, :], gq_in, serial=False)
        load(S["prep"], gk_bc[0:1, :], gk_in, serial=False)
        load(S["prep"], lnt[0:1, 0:D], bo_in, serial=False)
        n_all = load(S["prep"], scr[0:1, 0:D], bf_in, serial=False)
        n_bq = n_bk = n_bv = n_gq = n_gk = n_bo = n_bf = n_all

        em("vector", lambda e: e.memset(ones_bf[:, 0:1], 1.0))
        em("vector", lambda e: e.memset(onesr_f[0:1, :], 1.0))
        em("vector", lambda e: e.memset(onescol[:, 0:1], 1.0))

        wt("vector", S["prep"], n_gq)
        op("vector", "tensor_scalar", gq_bc[0:1, :], gq_bc[0:1, :], ALPHA,
           None, op0=ALU.mult, inc=S["vs"], bar=True)

        def bcast(row_ap, dst_ap, cols, dep_n):
            for j in range((cols + 511) // 512):
                c0, c1 = 512 * j, min(512 * (j + 1), cols)
                wt("tensor", S["prep"], dep_n)
                wt("tensor", S["vs"], S["vs"].n)
                pe_n = mm(prb.ap()[:, 0:c1 - c0], onesr_f[0:1, :],
                          row_ap[0:1, c0:c1], True, True, inc=S["pe"])
                wt("vector", S["pe"], pe_n)
                op("vector", "tensor_copy", dst_ap[:, c0:c1],
                   prb.ap()[:, 0:c1 - c0], inc=S["vs"])

        bcast(bq_bc, bq_bc, ES, n_bq)
        bcast(bk_bc, bk_bc, ES, n_bk)
        bcast(bv_bc, bv_bc, ES, n_bv)
        bcast(gq_bc, gq_bc, HD, n_gq)
        bcast(gk_bc, gk_bc, HD, n_gk)
        bcast(lnt, bo_bc, D, n_bo)
        bcast(scr, bf_bc, D, n_bf)

        p0_xt_g = [0, 0]
        for i in range(4):
            stg = xt[i % 2]
            wt("sync", S["vv"], p0_xt_g[i % 2])
            n_l = load(S["prep"], stg, wfs_in[128 * i:128 * (i + 1), :])
            wt("vector", S["prep"], n_l)
            p0_xt_g[i % 2] = op("vector", "tensor_reduce", st1[:, i:i + 1],
                                stg, AX.X, ALU.add,
                                apply_absolute_value=True, inc=S["vv"],
                                bar=True)
        v_n = op("vector", "tensor_reduce", st1[:, 4:5], st1[:, 0:4], AX.X,
                 ALU.add, inc=S["vs"], bar=True)
        wt("tensor", S["vs"], v_n)
        pe_n = mm(pex.ap()[0:1, 0:1], st1[:, 4:5], onescol[:, 0:1], True, True,
                  inc=S["pe"])
        wt("vector", S["pe"], pe_n)
        op("vector", "tensor_copy", wsrow[0:1, 0:1], pex.ap()[0:1, 0:1])
        vg_n = op("vector", "memset", wsrow[0:1, 1:8], 0.0, inc=S["vg"])
        wt("gpsimd", S["vg"], vg_n)
        st_n = store(S["stw"], ws_i, wsrow[0:1, 0:8])
        wt("gpsimd", S["stw"], st_n)
        cc_ws = cc("AllReduce", ALU.add, ws_i, ws_o)

        store(S["stq"], wqkv_d[:, 0:ES], wq_in)
        store(S["stq"], wqkv_d[:, ES:2 * ES], wk_in)
        st_wqkv = store(S["stq"], wqkv_d[:, 2 * ES:3 * ES], wv_in)
        st_wom = store(S["stw"], wom_d, wos_in)

        wt("sync", S["cc"], cc_ws)
        n_ws = load(S["x"], wsrow[0:1, 8:16], ws_o)
        wt("vector", S["x"], n_ws)
        op("vector", "tensor_scalar", wsrow[0:1, 0:1], wsrow[0:1, 8:9],
           MEAN_W, 1e-5, op0=ALU.mult, op1=ALU.max, bar=True)
        op("vector", "tensor_scalar", wsrow[0:1, 1:2], wsrow[0:1, 0:1],
           float(1.0 / 127.0), None, op0=ALU.mult, bar=True)
        v_n = op("vector", "reciprocal", wsrow[0:1, 2:3], wsrow[0:1, 0:1],
                 inc=S["vs"], bar=True)
        wt("tensor", S["vs"], v_n)
        mm(pex.ap()[:, 0:1], onesr_f[0:1, :], wsrow[0:1, 2:3], True, True)
        pe_n = mm(pex.ap()[:, 1:2], onesr_f[0:1, :], wsrow[0:1, 1:2], True,
                  True, inc=S["pe"])
        wt("vector", S["pe"], pe_n)
        op("vector", "tensor_copy", ws_bc[:, 0:1], pex.ap()[:, 0:1])
        WS_READY = op("vector", "tensor_copy", mwc_bc[:, 0:1], pex.ap()[:, 1:2],
                      inc=S["vs"])

        def ln_quant_phase(x_src, xa_dst, rx_sb, guards, chunks=None):
            cl = list(chunks) if chunks is not None else list(range(NCH))
            c0, ncl = cl[0], len(cl)
            C = lambda b: stw_t[:, 8 * b + c0:8 * b + c0 + ncl]
            Cc = lambda b, c: stw_t[:, 8 * b + c:8 * b + c + 1]
            p1v = [0] * 2
            for c in cl:
                sl = c % 2
                wt("sync", S["vv"], p1v[sl])
                if c == cl[0]:
                    for gs, gv in guards:
                        wt("sync", gs, gv)
                n_x = load(S["x"], xt[sl], x_src[128 * c:128 * (c + 1), :])
                wt("vector", S["x"], n_x)
                op("vector", "tensor_reduce", Cc(0, c), xt[sl], AX.X, ALU.add)
                op("vector", "tensor_reduce", Cc(2, c), xt[sl], AX.X, ALU.max)
                op("vector", "tensor_reduce", Cc(3, c), xt[sl], AX.X, ALU.min)
                op("vector", "tensor_mul", scr, xt[sl], xt[sl], bar=True)
                p1v[sl] = op("vector", "tensor_reduce", Cc(1, c), scr, AX.X,
                             ALU.add, inc=S["vv"], bar=True)
            op("vector", "tensor_scalar", C(4), C(0), INV_D, None,
               op0=ALU.mult, bar=True)
            op("vector", "tensor_mul", C(5), C(4), C(4), bar=True)
            op("vector", "scalar_tensor_tensor", C(6), C(1), INV_D, C(5),
               op0=ALU.mult, op1=ALU.subtract, bar=True)
            v_n = op("vector", "tensor_scalar", C(5), C(6), 1e-5, None,
                     op0=ALU.add, inc=S["vs"], bar=True)
            wt("scalar", S["vs"], v_n)
            s_n = op("scalar", "activation", C(15), C(5), AF.Sqrt,
                     inc=S["sv"])
            wt("vector", S["sv"], s_n)
            op("vector", "reciprocal", C(7), C(15), bar=True)
            op("vector", "tensor_scalar", C(8), C(7), 0.5, 1.0,
               op0=ALU.mult, op1=ALU.add, bar=True)
            op("vector", "tensor_mul", C(5), C(4), C(7), bar=True)
            op("vector", "tensor_scalar", C(9), C(5), -0.5, None,
               op0=ALU.mult, bar=True)
            op("vector", "tensor_mul", C(15), C(6), C(7), bar=True)
            op("vector", "tensor_mul", C(15), C(15), C(7), bar=True)
            v_n = op("vector", "tensor_scalar", C(15), C(15), 1e-5, None,
                     op0=ALU.add, inc=S["vs"], bar=True)
            wt("scalar", S["vs"], v_n)
            s_n = op("scalar", "activation", C(15), C(15), AF.Sqrt,
                     inc=S["sv"])
            wt("vector", S["sv"], s_n)
            op("vector", "reciprocal", C(10), C(15), bar=True)
            op("vector", "tensor_sub", C(2), C(2), C(4), bar=True)
            op("vector", "tensor_sub", C(3), C(4), C(3), bar=True)
            op("vector", "tensor_max", C(2), C(2), C(3), bar=True)
            op("vector", "tensor_mul", C(2), C(2), C(7), bar=True)
            op("vector", "tensor_mul", C(11), C(2), C(10), bar=True)
            op("vector", "tensor_scalar", C(11), C(11), 1e-5, None,
               op0=ALU.max, bar=True)
            op("vector", "reciprocal", C(12), C(11), bar=True)
            op("vector", "tensor_scalar", C(12), C(12), 127.0, None,
               op0=ALU.mult, bar=True)
            op("vector", "tensor_mul", C(12), C(10), C(12), bar=True)
            op("vector", "tensor_mul", C(13), C(7), C(12), bar=True)
            op("vector", "tensor_mul", C(14), C(5), C(12), bar=True)
            op("vector", "tensor_scalar", C(14), C(14), -1.0, None,
               op0=ALU.mult, bar=True)
            v_st = op("vector", "tensor_scalar", rx_sb[:, c0:c0 + ncl], C(11),
                      mwc_bc[:, 0:1], None, op0=ALU.mult, inc=S["vs"],
                      bar=True)
            xa_g = [0] * 2
            xq_g = [0] * 2
            tq_g = [0, 0]
            p3_sv = [0, 0]
            tq2 = [tq, scr]
            for j, c in enumerate(cl):
                sl = c % 2
                if j < 2:
                    wt("sync", S["vv"], p1v[sl])
                else:
                    wt("sync", S["sv"], p3_sv[sl])
                n_x = load(S["x"], xt[sl], x_src[128 * c:128 * (c + 1), :])
                wt("scalar", S["x"], n_x)
                if j == 0:
                    wt("scalar", S["vs"], v_st)
                wt("scalar", S["stx"], xa_g[sl])
                s_xa = op("scalar", "activation", xa[sl], xt[sl], AF.Identity,
                          bias=Cc(9, c), scale=Cc(8, c), inc=S["sg"])
                wt("gpsimd", S["sg"], s_xa)
                xa_g[sl] = store(S["stx"], xa_dst[128 * c:128 * (c + 1), :],
                                 xa[sl])
                wt("scalar", S["vv"], tq_g[sl])
                s_t = op("scalar", "activation", tq2[sl], xt[sl], AF.Identity,
                         bias=Cc(14, c), scale=Cc(13, c), inc=S["sv"])
                p3_sv[sl] = s_t
                wt("vector", S["sv"], s_t)
                tq_g[sl] = op("vector", "tensor_copy", i32t, tq2[sl],
                              inc=S["vv"], bar=True)
                wt("vector", S["stq"], xq_g[sl])
                vg_n = op("vector", "tensor_copy", xqbf[sl], i32t, inc=S["vg"])
                wt("gpsimd", S["vg"], vg_n)
                xq_g[sl] = store(S["stq"], xq_d[128 * c:128 * (c + 1), :],
                                 xqbf[sl])
            return None

        P0_PE = S["pe"].n
        wint_st = 0
        p2_xt_g = [p0_xt_g[0], p0_xt_g[1]]
        p2_tq_g = 0
        for i in range(4):
            sl = i % 2
            if i < 2:
                wt("sync", S["vv"], p0_xt_g[sl])
            else:
                wt("sync", S["sv"], p2_xt_g[sl])
            n_w = load(S["x"], xt[sl], wfs_in[128 * i:128 * (i + 1), :])
            wt("scalar", S["x"], n_w)
            if i == 0:
                wt("scalar", S["vs"], WS_READY)
            wt("scalar", S["vv"], p2_tq_g)
            s_n = op("scalar", "activation", tq, xt[sl], AF.Copy,
                     scale=ws_bc[:, 0:1], inc=S["sv"])
            p2_xt_g[sl] = s_n
            wt("vector", S["sv"], s_n)
            p2_tq_g = op("vector", "tensor_copy", i32t, tq, inc=S["vv"],
                         bar=True)
            wt("vector", S["stw"], wint_st)
            vg_n = op("vector", "tensor_scalar", xqbf[sl], i32t, 1.0, -1.0,
                      op0=ALU.min, op1=ALU.max, inc=S["vg"])
            wt("gpsimd", S["vg"], vg_n)
            wint_st = store(S["stw"], wint_i[128 * i:128 * (i + 1), :],
                            xqbf[sl])
        P2A_SV = S["sv"].n
        P2A_VV = S["vv"].n
        P2A_VG = S["vg"].n
        wtT = [R(60, BF16, ES), R(68, BF16, ES)]
        wt_stl = []
        for k in range(32):
            sl = k % 2
            wt("sync", S["stw"], wint_st if k < 2 else wt_stl[k - 2])
            n_l = load(WR[sl], wtT[sl], wint_i[0:ES, 128 * k:128 * (k + 1)],
                       transpose=True, serial=False)
            wt("gpsimd", WR[sl], n_l)
            wt_stl.append(store(S["stw"], wintT_i[128 * k:128 * (k + 1), :],
                                wtT[sl]))
        wt("gpsimd", S["stw"], wt_stl[-1])
        cc_wint = cc("AllGather", ALU.bypass, wintT_i, wintT_o)
        wintT_flat = wintT_o.rearrange("r p d -> (r p) d")
        P2_STW = S["stw"].n

        g_p1 = [(S["vs"], WS_READY), (S["sv"], P2A_SV), (S["vv"], P2A_VV),
                (S["vg"], P2A_VG), (S["stw"], wint_st), (S["pe"], P0_PE)]
        ln_quant_phase(x_in, xA, rx1, g_p1)
        P1_STX = S["stx"].n
        P1_STQ = S["stq"].n
        P1_SV = S["sv"].n
        P1_VG = S["vg"].n

        wring_g = [0] * 8
        widx = [0]
        xr_g = [0] * 4
        xo_g = {"stx": [0, 0], "out": [0, 0]}

        def ffn_like_phase(dst_sem_name, lhsT_sb, rhs_src_fn, rhs_transpose,
                           res_src, dst, epi_fn, guards_sync, guards_tensor,
                           per_k_guards=None, lhs_k_guards=None,
                           ts_list=(0, 1), og_list=None):
            st_sem = S[dst_sem_name]
            first = [True]
            bank_g = [S["vs"].n] * 2
            for gs, gv in guards_tensor:
                wt("tensor", gs, gv)
            ts0 = list(ts_list)[0]
            ogl = list(og_list) if og_list is not None else list(range(8))
            og0 = ogl[0]
            for ts in ts_list:
                for og in ogl:
                    bg = og % 2
                    wt("tensor", S["vs"], bank_g[bg])
                    pe_group = 0
                    for k in range(32):
                        sl = widx[0] % 8
                        wt("sync", S["pe"], wring_g[sl])
                        if first[0]:
                            for gs, gv in guards_sync:
                                wt("sync", gs, gv)
                            first[0] = False
                        if per_k_guards is not None and ts == ts0 and og < 2:
                            wt("sync", S["stw"], per_k_guards[k])
                        n_w = load(WR[sl], wring[sl], rhs_src_fn(og, k),
                                   transpose=rhs_transpose, serial=False)
                        if (lhs_k_guards is not None and ts == ts0
                                and og == og0):
                            wt("tensor", *lhs_k_guards[k])
                        wt("tensor", WR[sl], n_w)
                        for tt in range(4):
                            t0 = 1024 * k + 512 * ts + 128 * tt
                            pe_group = mm(PS8[4 * bg + tt],
                                          lhsT_sb[:, t0:t0 + 128],
                                          wring[sl][:], k == 0, k == 31,
                                          inc=S["pe"] if tt == 3 else None)
                        wring_g[sl] = pe_group
                        widx[0] += 1
                    last_vs = 0
                    for tt in range(4):
                        tg = 4 * ts + tt
                        rows = slice(128 * tg, 128 * (tg + 1))
                        cols = slice(512 * og, 512 * (og + 1))
                        xsl = (2 * og + tt) % 4
                        wt("sync", S["vg"], xr_g[xsl])
                        n_xr = load(S["x"], xr[xsl], res_src[rows, cols])
                        last_vs = epi_fn(bg, tt, tg, rows, cols, xsl, n_xr,
                                         pe_group, st_sem, dst)
                    bank_g[bg] = last_vs

        def make_ffn_epi(rx_sb, st_sem_name):
            g_guard = [0, 0]

            def epi(bg, tt, tg, rows, cols, xsl, n_xr, pe_group, st_sem, dst):
                ysl = tt % 2
                wt("vector", S["pe"], pe_group)
                wt("vector", S["sv"], g_guard[ysl])
                s_n = op("vector", "scalar_tensor_tensor", yt_t[ysl],
                         PS8[4 * bg + tt], rx_sb[:, tg:tg + 1], bf_bc[:, cols],
                         op0=ALU.mult, op1=ALU.add, inc=S["vs"])
                wt("scalar", S["vs"], s_n)
                s_g = op("scalar", "activation", g_t[ysl], yt_t[ysl], AF.Gelu,
                         inc=S["sv"])
                g_guard[ysl] = s_g
                osl = tt % 2
                wt("vector", S["sv"], s_g)
                wt("vector", S["x"], n_xr)
                wt("vector", st_sem, xo_g[st_sem_name][osl])
                vg_n = op("vector", "scalar_tensor_tensor", xo[osl], g_t[ysl],
                          0.5, xr[xsl], op0=ALU.mult, op1=ALU.add, inc=S["vg"])
                xr_g[xsl] = vg_n
                wt("gpsimd", S["vg"], vg_n)
                xo_g[st_sem_name][osl] = store(st_sem, dst[rows, cols],
                                               xo[osl])
                return s_n
            return epi

        def ffn_phase(rx_sb, res_src, dst, st_sem_name, guards):
            for gs, gv in guards:
                wt("sync", gs, gv)
            xqt_k = []
            for k in range(32):
                xqt_k.append((HR[k % 4],
                              load(HR[k % 4], xqT[:, 1024 * k:1024 * (k + 1)],
                                   xq_d[0:TPC, 128 * k:128 * (k + 1)],
                                   transpose=True, serial=True)))
            ffn_like_phase(
                st_sem_name, xqT,
                lambda og, k: wintT_flat[4096 * og + 128 * k:
                                         4096 * og + 128 * (k + 1), :],
                False, res_src, dst, make_ffn_epi(rx_sb, st_sem_name),
                [(S["cc"], cc_wint)], [], lhs_k_guards=xqt_k)

        if stage >= 3:
            ffn_phase(rx1, xA, xB, "stx",
                      [(S["stq"], P1_STQ), (S["stx"], P1_STX)])
        P3_PE = S["pe"].n
        P3_STX = S["stx"].n
        P3_SV = S["sv"].n
        P3_VG = S["vg"].n

        cc_h2 = [0, 0]
        P4_STH = 0
        if stage >= 5:
            h2_g = [0] * 2
            CB = lambda bk: stw_t[:, 8 * bk:8 * (bk + 1)]
            CBc = lambda bk, cc_: stw_t[:, 8 * bk + cc_:8 * bk + cc_ + 1]
            p4v = [0] * 2
            for c in range(NCH):
                sl = c % 2
                wt("sync", S["vv"], p4v[sl])
                if c == 0:
                    wt("sync", S["pe"], P3_PE)
                    wt("sync", S["stx"], P3_STX)
                    wt("sync", S["sv"], S["sv"].n)
                    wt("sync", S["vg"], S["vg"].n)
                n_x = load(S["x"], xt[sl], xB[128 * c:128 * (c + 1), :])
                wt("vector", S["x"], n_x)
                op("vector", "tensor_reduce", CBc(0, c), xt[sl], AX.X,
                   ALU.add)
                op("vector", "tensor_mul", scr, xt[sl], xt[sl], bar=True)
                p4v[sl] = op("vector", "tensor_reduce", CBc(1, c), scr, AX.X,
                             ALU.add, inc=S["vv"], bar=True)
            op("vector", "tensor_scalar", CB(4), CB(0), INV_D, None,
               op0=ALU.mult, bar=True)
            op("vector", "tensor_mul", CB(5), CB(4), CB(4), bar=True)
            op("vector", "scalar_tensor_tensor", CB(6), CB(1), INV_D, CB(5),
               op0=ALU.mult, op1=ALU.subtract, bar=True)
            v_n = op("vector", "tensor_scalar", CB(5), CB(6), 1e-5, None,
                     op0=ALU.add, inc=S["vs"], bar=True)
            wt("scalar", S["vs"], v_n)
            s_n = op("scalar", "activation", CB(3), CB(5), AF.Sqrt,
                     inc=S["sv"])
            wt("vector", S["sv"], s_n)
            op("vector", "reciprocal", CB(7), CB(3), bar=True)
            v_st4 = op("vector", "scalar_tensor_tensor", CB(9), CB(4), -1.0,
                       CB(7), op0=ALU.mult, op1=ALU.mult, inc=S["vs"],
                       bar=True)
            p4_sg = [0, 0]
            for c in range(NCH):
                sl = c % 2
                if c < 2:
                    wt("sync", S["vv"], p4v[sl])
                else:
                    wt("sync", S["sg"], p4_sg[sl])
                n_x = load(S["x"], xt[sl], xB[128 * c:128 * (c + 1), :])
                wt("scalar", S["x"], n_x)
                if c == 0:
                    wt("scalar", S["vs"], v_st4)
                wt("scalar", S["sth"], h2_g[sl])
                sg_n = op("scalar", "activation", h2st[sl], xt[sl],
                          AF.Identity, bias=CBc(9, c), scale=CBc(7, c),
                          inc=S["sg"])
                p4_sg[sl] = sg_n
                wt("gpsimd", S["sg"], sg_n)
                h2_g[sl] = store(
                    S["sth"], h2_i[c // 4][128 * (c % 4):128 * (c % 4) + 128, :],
                    h2st[sl])
                if c == 3:
                    wt("gpsimd", S["sth"], S["sth"].n)
                    cc_h2[0] = cc("AllGather", ALU.bypass, h2_i[0], h2_o[0])
                if c == 7:
                    wt("gpsimd", S["sth"], S["sth"].n)
                    cc_h2[1] = cc("AllGather", ALU.bypass, h2_i[1], h2_o[1])
            P4_STH = S["sth"].n

        hring_g = [0] * 8
        qn_g = [0] * 2
        at_g = [0] * 2
        cc_at = [0, 0]
        if stage >= 5:
            Ehalf_g = [0, 0]
            pair_g = [0, 0]
            pl_g = 0
            pavrb_g = 0
            rank1_g = 0
            for b in range(1 if dump == 24 else B):
                wt("sync", S["pe"], S["pe"].n)
                wt("sync", S["sf"], S["sf"].n)
                wt("sync", S["sv"], S["sv"].n)
                wt("sync", S["vg"], S["vg"].n)
                wt("sync", S["stq"], st_wqkv if b == 0 else S["stq"].n)
                if b == 0 and stage >= 6:
                    wt("sync", S["sth"], P4_STH)
                n_wqkv = 0
                for (w_sb, off) in ((wq_sb, 0), (wk_sb, ES), (wv_sb, 2 * ES)):
                    for k in range(32):
                        n_wqkv = load(S["wqkv"], w_sb[:, ES * k:ES * (k + 1)],
                                      wqkv_d[128 * k:128 * (k + 1),
                                             off:off + ES], serial=False)
                wt("tensor", S["wqkv"], n_wqkv)
                wt("sync", S["cc"], cc_h2[b])
                if b == 0:
                    wt("tensor", S["cc"], cc_h2[1])

                qkv_bank_g = S["vs"].n
                v_bank_g = S["vs"].n
                qkv_sq_g = S["sv"].n
                for tc in range(8):
                    r = tc
                    Xh = b
                    wt("tensor", S["vs"], qkv_bank_g)
                    pe_qk = 0
                    for k in range(32):
                        sl = k % 8
                        wt("sync", S["pe"], hring_g[sl])
                        n_h = load(HR[sl], hring[sl],
                                   h2_o[Xh][r][0:512, 128 * k:128 * (k + 1)],
                                   transpose=True, serial=False)
                        wt("tensor", HR[sl], n_h)
                        for tt in range(4):
                            mm(PS8[tt], hring[sl][:, 128 * tt:128 * (tt + 1)],
                               wq_sb[:, ES * k:ES * (k + 1)], k == 0, k == 31)
                        for tt in range(4):
                            pe_qk = mm(PS8[4 + tt],
                                       hring[sl][:, 128 * tt:128 * (tt + 1)],
                                       wk_sb[:, ES * k:ES * (k + 1)],
                                       k == 0, k == 31,
                                       inc=S["pe"] if tt == 3 else None)
                        hring_g[sl] = pe_qk
                    vs_qk = 0
                    vs_q4 = 0
                    sq_guard = qkv_sq_g
                    for qk in range(2):
                        bias_bc = bq_bc if qk == 0 else bk_bc
                        g_bc = gq_bc if qk == 0 else gk_bc
                        dstd = qn_d if qk == 0 else kn_d
                        for tt in range(4):
                            bank = PS8[4 * qk + tt]
                            wt("vector", S["pe"], pe_qk)
                            wt("vector", S["sv"], sq_guard)
                            s_n = op("vector", "scalar_tensor_tensor", qtmp,
                                     bank, 1.0, bias_bc, op0=ALU.mult,
                                     op1=ALU.add, inc=S["vs"], bar=True)
                            vs_qk = s_n
                            wt("scalar", S["vs"], s_n)
                            for hh in range(4):
                                cseg = slice(128 * hh, 128 * (hh + 1))
                                sq_guard = op(
                                    "scalar", "activation", sqq[:, cseg],
                                    qtmp[:, cseg], AF.Square,
                                    accum_out=st3[:, hh:hh + 1],
                                    inc=S["sv"] if hh == 3 else None)
                            wt("vector", S["sv"], sq_guard)
                            v_n = op("vector", "tensor_scalar", st3[:, 0:4],
                                     st3[:, 0:4], float(1.0 / HD), 1e-6,
                                     op0=ALU.mult, op1=ALU.add, inc=S["vs"],
                                     bar=True)
                            wt("scalar", S["vs"], v_n)
                            s_n = op("scalar", "activation", st3[:, 4:8],
                                     st3[:, 0:4], AF.Sqrt, inc=S["sv"])
                            wt("vector", S["sv"], s_n)
                            op("vector", "reciprocal", st3[:, 8:12],
                               st3[:, 4:8], bar=True)
                            qsl = (2 * qk + tt) % 2
                            wt("vector", S["stq"], qn_g[qsl])
                            vg_n = 0
                            for hh in range(4):
                                cseg = slice(128 * hh, 128 * (hh + 1))
                                vg_n = op("vector", "scalar_tensor_tensor",
                                          qnst[qsl][:, cseg], qtmp[:, cseg],
                                          st3[:, 8 + hh:9 + hh], g_bc,
                                          op0=ALU.mult, op1=ALU.mult,
                                          inc=S["vg"] if hh == 3 else None)
                            wt("gpsimd", S["vg"], vg_n)
                            trow = 512 * tc + 128 * tt
                            qn_g[qsl] = store(S["stq"],
                                              dstd[trow:trow + 128, :],
                                              qnst[qsl])
                        if qk == 0:
                            vs_q4 = vs_qk
                    qkv_sq_g = sq_guard
                    wt("tensor", S["vs"], vs_q4)
                    pe_v = 0
                    for k in range(32):
                        sl = k % 8
                        wt("sync", S["pe"], hring_g[sl])
                        n_h = load(HR[sl], hring[sl],
                                   h2_o[Xh][r][0:512, 128 * k:128 * (k + 1)],
                                   transpose=True, serial=False)
                        wt("tensor", HR[sl], n_h)
                        for tt in range(4):
                            pe_v = mm(PS8[tt],
                                      hring[sl][:, 128 * tt:128 * (tt + 1)],
                                      wv_sb[:, ES * k:ES * (k + 1)],
                                      k == 0, k == 31,
                                      inc=S["pe"] if tt == 3 else None)
                        hring_g[sl] = pe_v
                    for tt in range(4):
                        wt("vector", S["pe"], pe_v)
                        cseg = slice(512 * (4 * tc + tt),
                                     512 * (4 * tc + tt + 1))
                        qkv_bank_g = op("vector", "scalar_tensor_tensor",
                                        V_sb[:, cseg], PS8[tt], 1.0, bv_bc,
                                        op0=ALU.mult, op1=ALU.add, inc=S["vs"])
                QKV_STQ = S["stq"].n
                QKV_PE = S["pe"].n
                QKV_VS = S["vs"].n

                wt("sync", S["stq"], QKV_STQ)
                wt("sync", S["pe"], QKV_PE)
                n_qt = 0
                for (dsrc, dsb) in ((qn_d, qT), (kn_d, kT)):
                    for h in range(HPC):
                        n_qt = load(S["t"], dsb[:, N * h:N * (h + 1)],
                                    dsrc[0:N, 128 * h:128 * (h + 1)],
                                    transpose=True, serial=False)
                wt("tensor", S["t"], n_qt)
                wt("tensor", S["vs"], QKV_VS)

                for h in range(HPC):
                    for qc in range(8):
                        qcols = slice(N * h + 512 * qc,
                                      N * h + 512 * (qc + 1))
                        av_n = 0
                        for half in range(2):
                            exp_n = 0
                            for g_ in range(8):
                                pr = g_ % 2
                                ps_pair = ps2a if pr == 0 else ps2b
                                wt("tensor", S["sf"], pair_g[pr])
                                pe_n = 0
                                for j in range(2):
                                    kt = 16 * half + 2 * g_ + j
                                    kcol = slice(N * h + 128 * kt,
                                                 N * h + 128 * (kt + 1))
                                    pe_n = mm(
                                        ps_pair.ap()[:, 512 * j:512 * (j + 1)],
                                        kT[:, kcol], qT[:, qcols], True, True,
                                        inc=S["pe"] if j == 1 else None)
                                wt("scalar", S["pe"], pe_n)
                                wt("scalar", S["pe"], Ehalf_g[half])
                                ecol = slice(1024 * g_, 1024 * (g_ + 1))
                                exp_n = op("scalar", "activation",
                                           Eb[half][:, ecol], ps_pair.ap()[:],
                                           AF.Exp, inc=S["sf"])
                                pair_g[pr] = exp_n
                            wt("tensor", S["sf"], exp_n)
                            if half == 0:
                                wt("tensor", S["vg"], pavrb_g)
                                wt("tensor", S["vs"], pl_g)
                            for kt16 in range(16):
                                kt = 16 * half + kt16
                                erhs = Eb[half][:, 512 * kt16:512 * (kt16 + 1)]
                                vcol = slice(512 * kt + 128 * h,
                                             512 * kt + 128 * (h + 1))
                                mm(pav.ap()[:], V_sb[:, vcol], erhs,
                                   kt == 0, kt == 31)
                                av_n = mm(pl.ap()[0:1, :], ones_bf[:, 0:1],
                                          erhs, kt == 0, kt == 31,
                                          inc=S["pe"] if kt16 == 15 else None)
                            Ehalf_g[half] = av_n
                        wt("vector", S["pe"], av_n)
                        wt("vector", S["pe"], rank1_g)
                        pl_g = op("vector", "reciprocal", lrow[0:1, :],
                                  pl.ap()[0:1, :], inc=S["vs"])
                        wt("tensor", S["vs"], pl_g)
                        rank1_g = mm(prb.ap()[:], onesr_f[0:1, :],
                                     lrow[0:1, :], True, True, inc=S["pe"])
                        asl = qc % 2
                        wt("vector", S["pe"], rank1_g)
                        op("vector", "tensor_copy", rbc_sb, prb.ap()[:],
                           bar=True)
                        wt("vector", S["sth"], at_g[asl])
                        vg_n = op("vector", "tensor_mul", atst[asl],
                                  pav.ap()[:], rbc_sb, inc=S["vg"])
                        pavrb_g = vg_n
                        wt("gpsimd", S["vg"], vg_n)
                        gcol = 4096 * b + 512 * qc
                        at_g[asl] = store(
                            S["sth"],
                            at_d[128 * h:128 * (h + 1), gcol:gcol + 512],
                            atst[asl])

        cc_rs = [0, 0]
        if stage >= 6:
            wt("sync", S["sth"], S["sth"].n)
            wt("sync", S["pe"], S["pe"].n)
            wt("sync", S["sf"], S["sf"].n)
            wt("sync", S["vg"], S["vg"].n)
            wt("sync", S["stw"], st_wom)
            n_at = 0
            for e in range(4):
                n_at = load(S["t"], at4[e], at_d[128 * e:128 * (e + 1), :],
                            serial=False)
            n_wo = 0
            for e in range(4):
                n_wo = load(S["wqkv"], wo_sb[:, D * e:D * (e + 1)],
                            wom_d[128 * e:128 * (e + 1), :], serial=False)
            wt("tensor", S["t"], n_at)
            wt("tensor", S["wqkv"], n_wo)
            wt("tensor", S["vs"], S["vs"].n)
            rnd_g = [S["vs"].n] * 2
            stg_g = [0, 0]
            rs_st = 0
            for T in range(64):
                ssl = T % 2
                vs_cp = 0
                for rnd in range(2):
                    gr = (2 * T + rnd) % 2
                    wt("tensor", S["vs"], rnd_g[gr])
                    pe_n = 0
                    for e in range(4):
                        for oc in range(4):
                            o = 4 * rnd + oc
                            pe_n = mm(PS8[4 * gr + oc],
                                      at4[e][:, 128 * T:128 * (T + 1)],
                                      wo_sb[:, D * e + 512 * o:D * e + 512 * (o + 1)],
                                      e == 0, e == 3,
                                      inc=S["pe"] if oc == 3 else None)
                    for oc in range(4):
                        o = 4 * rnd + oc
                        wt("vector", S["pe"], pe_n)
                        if rnd == 0 and oc == 0:
                            wt("vector", S["stx"], stg_g[ssl])
                        vs_cp = op("vector", "tensor_copy",
                                   stg6[ssl][:, 512 * o:512 * (o + 1)],
                                   PS8[4 * gr + oc], inc=S["vs"])
                    rnd_g[gr] = vs_cp
                wt("gpsimd", S["vs"], vs_cp)
                stg_g[ssl] = store(S["stx"], rs_i[128 * T:128 * (T + 1), :],
                                   stg6[ssl])
                if T == 31:
                    wt("gpsimd", S["stx"], S["stx"].n)
                    cc_rs[0] = cc("ReduceScatter", ALU.add,
                                  rs_i[0:4096, :], rs_o[0:512, :])
                if T == 63:
                    wt("gpsimd", S["stx"], S["stx"].n)
                    cc_rs[1] = cc("ReduceScatter", ALU.add,
                                  rs_i[4096:8192, :], rs_o[512:1024, :])
            P6A_PE = S["pe"].n

            wint8_rhs = lambda og, k: wintT_flat[4096 * og + 128 * k:
                                                 4096 * og + 128 * (k + 1), :]
            xbstage = [R(28 + 8 * j, F32, 2048) for j in range(8)]
            wt("sync", S["pe"], P6A_PE)
            n_xb0 = 0
            for j in range(8):
                c2, hf = divmod(j, 2)
                n_xb0 = load(S["x"], xbstage[j],
                             xB[128 * c2:128 * (c2 + 1),
                                2048 * hf:2048 * (hf + 1)], serial=False)
            wt("sync", S["cc"], cc_rs[0])
            x3_st_g = 0
            for j in range(8):
                c2, hf = divmod(j, 2)
                rows = slice(128 * c2, 128 * (c2 + 1))
                cols = slice(2048 * hf, 2048 * (hf + 1))
                wt("sync", S["vg"], S["vg"].n)
                n_r = load(S["x"], rsb_t, rs_o[rows, cols])
                wt("vector", S["x"], n_r)
                wt("vector", S["stx"], x3_st_g)
                op("vector", "scalar_tensor_tensor", x3st, rsb_t, 1.0,
                   xbstage[j], op0=ALU.mult, op1=ALU.add, bar=True)
                vg_n = op("vector", "tensor_add", x3st, x3st,
                          bo_bc[:, cols], inc=S["vg"])
                wt("gpsimd", S["vg"], vg_n)
                x3_st_g = store(S["stx"], xA[rows, cols], x3st)
            P6B0_STX = S["stx"].n
            if stage >= 8:
                g_p7a = [(S["stx"], P6B0_STX), (S["pe"], P6A_PE),
                         (S["sv"], S["sv"].n), (S["vg"], S["vg"].n),
                         (S["stq"], S["stq"].n), (S["stw"], S["stw"].n),
                         (S["sth"], S["sth"].n), (S["sf"], S["sf"].n)]
                ln_quant_phase(xA, xB, rx2, g_p7a, chunks=range(0, 4))
                epi8 = make_ffn_epi(rx2, "out")
                g_p8a = [(S["stq"], S["stq"].n), (S["stx"], S["stx"].n)]
                for gs, gv in g_p8a:
                    wt("sync", gs, gv)
                xqt_a = []
                for k in range(32):
                    xqt_a.append((HR[k % 4],
                                  load(HR[k % 4],
                                       xqT[:, 1024 * k:1024 * k + 512],
                                       xq_d[0:512, 128 * k:128 * (k + 1)],
                                       transpose=True, serial=True)))
                ffn_like_phase("out", xqT, wint8_rhs, False, xB, out_t, epi8,
                               [], [], lhs_k_guards=xqt_a, ts_list=(0,),
                               og_list=range(0, 4))
            wt("sync", S["cc"], cc_rs[1])
            wt("sync", S["stq"], S["stq"].n)
            x3_st_g1 = 0
            for c2 in range(4):
                rows = slice(512 + 128 * c2, 512 + 128 * (c2 + 1))
                for hf in range(2):
                    cols = slice(2048 * hf, 2048 * (hf + 1))
                    wt("sync", S["vg"], S["vg"].n)
                    n_r = load(S["x"], rsb_t, rs_o[rows, cols])
                    n_x2 = load(S["x"], xbt_t, xB[rows, cols])
                    wt("vector", S["x"], n_x2)
                    wt("vector", S["stx"], x3_st_g1)
                    op("vector", "scalar_tensor_tensor", x3st, rsb_t, 1.0,
                       xbt_t, op0=ALU.mult, op1=ALU.add, bar=True)
                    vg_n = op("vector", "tensor_add", x3st, x3st,
                              bo_bc[:, cols], inc=S["vg"])
                    wt("gpsimd", S["vg"], vg_n)
                    x3_st_g1 = store(S["stx"], xA[rows, cols], x3st)
        P6_STX = S["stx"].n
        P6_PE = S["pe"].n

        if stage >= 8:
            g_p7b = [(S["stx"], P6_STX), (S["pe"], P6_PE),
                     (S["sv"], S["sv"].n), (S["vg"], S["vg"].n),
                     (S["stq"], S["stq"].n), (S["stw"], S["stw"].n),
                     (S["sth"], S["sth"].n), (S["sf"], S["sf"].n)]
            ln_quant_phase(xA, xB, rx2, g_p7b, chunks=range(4, 8))
            wt("sync", S["stq"], S["stq"].n)
            xqt_c = []
            for k in range(32):
                xqt_c.append((HR[k % 4],
                              load(HR[k % 4],
                                   xqT[:, 1024 * k:1024 * k + 512],
                                   xq_d[0:512, 128 * k:128 * (k + 1)],
                                   transpose=True, serial=True)))
            ffn_like_phase("out", xqT, wint8_rhs, False, xB, out_t, epi8,
                           [], [], lhs_k_guards=xqt_c, ts_list=(0,),
                           og_list=range(4, 8))
            g_p8b = [(S["stq"], S["stq"].n), (S["stx"], S["stx"].n)]
            for gs, gv in g_p8b:
                wt("sync", gs, gv)
            xqt_b = []
            for k in range(32):
                xqt_b.append((HR[k % 4],
                              load(HR[k % 4],
                                   xqT[:, 1024 * k + 512:1024 * (k + 1)],
                                   xq_d[512:1024, 128 * k:128 * (k + 1)],
                                   transpose=True, serial=True)))
            ffn_like_phase("out", xqT, wint8_rhs, False, xB, out_t, epi8,
                           [], [], lhs_k_guards=xqt_b, ts_list=(1,))

        if dump:
            for nm in ["stx", "stq", "sth", "stw", "cc", "vg", "sg"]:
                wt("gpsimd", S[nm], S[nm].n)
            if dump == 21:
                store(S["out"], out_t[:, 0:D // 2].bitcast(BF16), xq_d)
            elif dump == 22:
                store(S["out"], out_t[0:512, 0:D // 2].bitcast(BF16), wint_i)
            elif dump == 23:
                store(S["out"], out_t[0:512, 0:D // 2].bitcast(BF16)
                      .rearrange("p (a b) -> p a b", a=2)
                      .rearrange("p a b -> (p a) b") if False else
                      out_t.bitcast(BF16)[0:512, 0:B * N], at_d)
            elif dump == 24:
                store(S["out"], out_t.bitcast(BF16)[0:1024, 0:ES],
                      qn_d[1280:2304])
        elif stage < 8:
            src = {1: xA, 3: xB, 6: xA}[stage]
            for nm in ["stx", "stq", "sth", "stw", "cc", "vg", "sg"]:
                wt("gpsimd", S[nm], S[nm].n)
            store(S["out"], out_t, src)

        wt("sync", S["out"], S["out"].n)
        wt("gpsimd", S["out"], S["out"].n)

        block = es.enter_context(nc.Block())
        @block.sync
        def _(e):
            for fn in prog["sync"]:
                fn(e)

        @block.gpsimd
        def _(e):
            for fn in prog["gpsimd"]:
                fn(e)

        @block.vector
        def _(e):
            for fn in prog["vector"]:
                fn(e)

        @block.scalar
        def _(e):
            for fn in prog["scalar"]:
                fn(e)

        @block.tensor
        def _(e):
            for fn in prog["tensor"]:
                fn(e)

    return nc


def kernel(L0_attn_hidden_states, L0_attn_Wq, L0_attn_bq, L0_attn_Wk,
           L0_attn_bk, L0_attn_Wv, L0_attn_bv, L0_attn_Wo, L0_attn_bo,
           L0_attn_gq, L0_attn_gk, L0_ffn_W, L0_ffn_b):
    x = np.ascontiguousarray(
        np.asarray(L0_attn_hidden_states, dtype=np.float32).reshape(B * N, D))
    Wq = np.asarray(L0_attn_Wq, np.float32)
    Wk = np.asarray(L0_attn_Wk, np.float32)
    Wv = np.asarray(L0_attn_Wv, np.float32)
    Wo = np.asarray(L0_attn_Wo, np.float32)
    Wf = np.asarray(L0_ffn_W, np.float32)
    bq = np.asarray(L0_attn_bq, np.float32).reshape(1, D)
    bk = np.asarray(L0_attn_bk, np.float32).reshape(1, D)
    bv = np.asarray(L0_attn_bv, np.float32).reshape(1, D)
    bo = np.asarray(L0_attn_bo, np.float32).reshape(1, D)
    bf_ = np.asarray(L0_ffn_b, np.float32).reshape(1, D)
    gq = np.asarray(L0_attn_gq, np.float32).reshape(1, HD)
    gk = np.asarray(L0_attn_gk, np.float32).reshape(1, HD)

    in_maps = []
    for c in range(NC):
        esl = slice(ES * c, ES * (c + 1))
        myrows = np.r_[512 * c:512 * (c + 1), N + 512 * c:N + 512 * (c + 1)]
        in_maps.append({
            "x_my": np.ascontiguousarray(x[myrows]),
            "wq_my": np.ascontiguousarray(Wq[:, esl]),
            "wk_my": np.ascontiguousarray(Wk[:, esl]),
            "wv_my": np.ascontiguousarray(Wv[:, esl]),
            "bq_my": np.ascontiguousarray(bq[:, esl]),
            "bk_my": np.ascontiguousarray(bk[:, esl]),
            "bv_my": np.ascontiguousarray(bv[:, esl]),
            "gq": gq, "gk": gk,
            "wfs_my": np.ascontiguousarray(Wf[esl, :]),
            "wos_my": np.ascontiguousarray(Wo[esl, :]),
            "bf": bf_, "bo": bo,
        })
    nc = build_nc(_STAGE)
    res = run_bass_kernel_spmd(nc, in_maps, core_ids=list(range(NC)))
    out = np.empty((B * N, D), np.float32)
    for c in range(NC):
        oc = res.results[c]["out"]
        out[512 * c:512 * (c + 1)] = oc[0:512]
        out[N + 512 * c:N + 512 * (c + 1)] = oc[512:1024]
    return out.reshape(B, N, D)


# version 12
# speedup vs baseline: 1.0184x; 1.0184x over previous
import numpy as np
import concourse.bass as bass
import concourse.mybir as mybir
from concourse.bass_utils import run_bass_kernel_spmd

F32 = mybir.dt.float32
BF16 = mybir.dt.bfloat16
I32 = mybir.dt.int32
AF = mybir.ActivationFunctionType
ALU = mybir.AluOpType
AX = mybir.AxisListType

D = 4096
B = 2
N = 4096
TPC = 1024
NC = 8
HD = 128
HPC = 4
ES = HPC * HD
NCH = TPC // 128
RG = [[0, 1, 2, 3, 4, 5, 6, 7]]
ALPHA = float(1.0 / np.sqrt(HD))
INV_D = float(1.0 / D)
MEAN_W = float(2.0 ** -24)

_STAGE = 8


class Sem:
    def __init__(self, h):
        self.h = h
        self.n = 0


def build_nc(stage):
    dump = stage if stage in (21, 22, 23, 24) else None
    if dump in (21, 22):
        stage = 2
    elif dump in (23, 24):
        stage = 5
    nc = bass.Bass("TRN2", target_bir_lowering=False, debug=False)

    x_in = nc.dram_tensor("x_my", [TPC, D], F32, kind="ExternalInput").ap()
    wq_in = nc.dram_tensor("wq_my", [D, ES], F32, kind="ExternalInput").ap()
    wk_in = nc.dram_tensor("wk_my", [D, ES], F32, kind="ExternalInput").ap()
    wv_in = nc.dram_tensor("wv_my", [D, ES], F32, kind="ExternalInput").ap()
    bq_in = nc.dram_tensor("bq_my", [1, ES], F32, kind="ExternalInput").ap()
    bk_in = nc.dram_tensor("bk_my", [1, ES], F32, kind="ExternalInput").ap()
    bv_in = nc.dram_tensor("bv_my", [1, ES], F32, kind="ExternalInput").ap()
    gq_in = nc.dram_tensor("gq", [1, HD], F32, kind="ExternalInput").ap()
    gk_in = nc.dram_tensor("gk", [1, HD], F32, kind="ExternalInput").ap()
    wfs_in = nc.dram_tensor("wfs_my", [ES, D], F32, kind="ExternalInput").ap()
    wos_in = nc.dram_tensor("wos_my", [ES, D], F32, kind="ExternalInput").ap()
    bf_in = nc.dram_tensor("bf", [1, D], F32, kind="ExternalInput").ap()
    bo_in = nc.dram_tensor("bo", [1, D], F32, kind="ExternalInput").ap()
    out_t = nc.dram_tensor("out", [TPC, D], F32, kind="ExternalOutput").ap()

    xA = nc.dram_tensor("xA", [TPC, D], F32, kind="Internal").ap()
    xB = nc.dram_tensor("xB", [TPC, D], F32, kind="Internal").ap()
    xq_d = nc.dram_tensor("xq_d", [TPC, D], BF16, kind="Internal").ap()
    qn_d = nc.dram_tensor("qn_d", [N, ES], BF16, kind="Internal").ap()
    kn_d = nc.dram_tensor("kn_d", [N, ES], BF16, kind="Internal").ap()
    wqkv_d = nc.dram_tensor("wqkv_d", [D, 3 * ES], BF16, kind="Internal").ap()

    ws_i = nc.dram_tensor("ws_i", [1, 8], F32, kind="Internal").ap()
    ws_o = nc.dram_tensor("ws_o", [1, 8], F32, kind="Internal").ap()
    wint_i = nc.dram_tensor("wint_i", [ES, D], BF16, kind="Internal").ap()
    wintT_i = nc.dram_tensor("wintT_i", [D, ES], BF16, kind="Internal").ap()
    wintT_o = nc.dram_tensor("wintT_o", [NC, D, ES], BF16,
                             kind="Internal").ap()
    wom_d = nc.dram_tensor("wom_d", [ES, D], BF16, kind="Internal").ap()
    h2_i = [nc.dram_tensor(f"h2_i{j}", [512, D], BF16, kind="Internal").ap()
            for j in range(2)]
    h2_o = [nc.dram_tensor(f"h2_o{j}", [NC, 512, D], BF16, kind="Internal").ap()
            for j in range(2)]
    at_d = nc.dram_tensor("at_d", [ES, B * N], BF16, kind="Internal").ap()
    rs_i = nc.dram_tensor("rs_i", [B * N, D], BF16, kind="Internal").ap()
    rs_o = nc.dram_tensor("rs_o", [TPC, D], BF16, kind="Internal").ap()

    from contextlib import ExitStack
    with ExitStack() as es:
        ARENA_KB = 176
        arena_t = es.enter_context(
            nc.sbuf_tensor("arena", [128, ARENA_KB * 1024], mybir.dt.uint8))
        _ar = arena_t.ap()

        _pers = [0]

        def P(dt, cols):
            sz = cols * {F32: 4, BF16: 2, I32: 4}[dt]
            off = (_pers[0] + 63) // 64 * 64
            _pers[0] = off + sz
            assert _pers[0] <= 28 * 1024
            return _ar[:, off:off + sz].bitcast(dt)

        def R(off_kb, dt, cols):
            off = int(off_kb * 1024)
            sz = cols * {F32: 4, BF16: 2, I32: 4}[dt]
            assert 28 * 1024 <= off and off + sz <= ARENA_KB * 1024
            return _ar[:, off:off + sz].bitcast(dt)

        bo_bc = P(BF16, D)
        bf_bc = P(BF16, D)
        bq_bc = P(F32, ES)
        bk_bc = P(F32, ES)
        bv_bc = P(F32, ES)
        gq_bc = P(F32, HD)
        gk_bc = P(F32, HD)
        ones_bf = P(BF16, 32)
        onesr_f = P(F32, 128)
        onescol = P(F32, 16)
        rx1 = P(F32, 16)
        rx2 = P(F32, 16)
        wsrow = P(F32, 16)
        ws_bc = P(F32, 16)
        mwc_bc = P(F32, 16)
        lrow = P(F32, 512)
        st1 = P(F32, 16)
        st2 = P(F32, 16)
        st3 = P(F32, 16)
        stp = [P(F32, 32), P(F32, 32)]
        stw_t = P(F32, 256)
        st3b = [P(F32, 16), P(F32, 16)]


        xt = [R(28, F32, D), R(44, F32, D)]
        lnt = R(60, F32, D)
        scr = R(76, F32, D)
        xa = [R(92, F32, D), R(108, F32, D)]
        tq = R(124, F32, D)
        i32t = R(140, I32, D)
        xqbf = [R(156, BF16, D), R(164, BF16, D)]
        h2st = [R(156, BF16, D), R(164, BF16, D)]
        xqT = R(28, BF16, 32 * 1024)
        at_sb = R(28, BF16, 32 * 1024)
        wring = [R(92 + i, BF16, 512) for i in range(8)]
        yt_t = [R(100, F32, 512), R(102, F32, 512)]
        g_t = [R(104, F32, 512), R(106, F32, 512)]
        xo = [R(108, F32, 512), R(110, F32, 512)]
        xr = [R(112 + 2 * i, F32, 512) for i in range(4)]
        wq_sb = R(28, BF16, 32 * ES)
        wk_sb = R(60, BF16, 32 * ES)
        wv_sb = R(92, BF16, 32 * ES)
        V_sb = R(124, BF16, 32 * ES)
        hring = [R(156 + i, BF16, 512) for i in range(8)]
        qtmp = R(164, F32, 512)
        sqq = R(166, F32, 512)
        qnst = [R(168, BF16, 512), R(169, BF16, 512)]
        atst = [R(170, BF16, 512), R(171, BF16, 512)]
        rbc_sb = R(172, F32, 512)
        qT = R(28, BF16, HPC * N)
        kT = R(60, BF16, HPC * N)
        Eb = [R(92, BF16, 16 * 512), R(108, BF16, 16 * 512)]
        at4 = [R(28 + 16 * e, BF16, B * N) for e in range(4)]
        wo_sb = R(124, BF16, HPC * D)
        stg6 = [R(92, BF16, D), R(100, BF16, D)]
        rsb_t = R(156, BF16, 2048)
        xbt_t = R(160, F32, 2048)
        x3st = R(168, F32, 2048)

        ps2a = es.enter_context(nc.psum_tensor("ps2a", [128, 1024], F32))
        ps2b = es.enter_context(nc.psum_tensor("ps2b", [128, 1024], F32))
        pav = es.enter_context(nc.psum_tensor("pav", [128, 512], F32))
        pl = es.enter_context(nc.psum_tensor("pl", [128, 512], F32))
        prb = es.enter_context(nc.psum_tensor("prb", [128, 512], F32))
        pex = es.enter_context(nc.psum_tensor("pex", [128, 512], F32))
        PS8 = [ps2a.ap()[:, 0:512], ps2a.ap()[:, 512:1024],
               ps2b.ap()[:, 0:512], ps2b.ap()[:, 512:1024],
               pav.ap()[:], pl.ap()[:], prb.ap()[:], pex.ap()[:]]

        S = {}
        for nm in ["prep", "x", "w", "t", "h2", "stx", "stq", "sth", "stw",
                   "cc", "pe", "vs", "sv", "vg", "sg", "sf", "out",
                   "vv", "sc", "wqkv"]:
            S[nm] = Sem(es.enter_context(nc.semaphore("s_" + nm)))
        WR = [Sem(es.enter_context(nc.semaphore(f"s_wr{i}"))) for i in range(8)]
        HR = [Sem(es.enter_context(nc.semaphore(f"s_hr{i}"))) for i in range(8)]

        prog = {e: [] for e in ["sync", "gpsimd", "vector", "scalar", "tensor"]}

        def em(eng, fn):
            prog[eng].append(fn)

        def wt(eng, sem, val):
            if val > 0:
                em(eng, lambda e, s=sem, v=val: e.wait_ge(s.h, v))

        def load(sem, dst, src, transpose=False, serial=True, queue="sync"):
            if serial and sem.n > 0:
                wt(queue, sem, sem.n)
            def fn(e, d=dst, r=src, t=transpose, s=sem):
                e.dma_start(out=d, in_=r, transpose=t).then_inc(s.h, 16)
            em(queue, fn)
            sem.n += 16
            return sem.n

        def store(sem, dst, src, serial=True, queue="gpsimd"):
            if serial and sem.n > 0:
                wt(queue, sem, sem.n)
            def fn(e, d=dst, r=src, s=sem):
                e.dma_start(out=d, in_=r).then_inc(s.h, 16)
            em(queue, fn)
            sem.n += 16
            return sem.n

        SELF = {"vector": "vv", "scalar": "sc"}

        def op(eng, name, *args, inc=None, bar=False, **kw):
            sem = inc if inc is not None else (S[SELF[eng]] if bar else None)
            def fn(e, name=name, args=args, kw=kw, sem=sem):
                ins = getattr(e, name)(*args, **kw)
                if sem is not None:
                    ins.then_inc(sem.h, 1)
            em(eng, fn)
            if sem is not None:
                sem.n += 1
                if bar:
                    wt(eng, sem, sem.n)
                return sem.n
            return None

        def mm(out, lhsT, rhs, start, stop, inc=None):
            return op("tensor", "matmul", out, lhsT, rhs,
                      start=start, stop=stop, inc=inc)

        def cc(kind, alu, i_, o_):
            def fn(e, k=kind, a=alu, i=i_, o=o_):
                e.collective_compute(k, a, replica_groups=RG, ins=[i],
                                     outs=[o]).then_inc(S["cc"].h, 1)
            em("gpsimd", fn)
            S["cc"].n += 1
            return S["cc"].n

        load(S["prep"], bq_bc[0:1, :], bq_in, serial=False)
        load(S["prep"], bk_bc[0:1, :], bk_in, serial=False)
        load(S["prep"], bv_bc[0:1, :], bv_in, serial=False)
        load(S["prep"], gq_bc[0:1, :], gq_in, serial=False)
        load(S["prep"], gk_bc[0:1, :], gk_in, serial=False)
        load(S["prep"], lnt[0:1, 0:D], bo_in, serial=False)
        n_all = load(S["prep"], scr[0:1, 0:D], bf_in, serial=False)
        n_bq = n_bk = n_bv = n_gq = n_gk = n_bo = n_bf = n_all

        em("vector", lambda e: e.memset(ones_bf[:, 0:1], 1.0))
        em("vector", lambda e: e.memset(onesr_f[0:1, :], 1.0))
        em("vector", lambda e: e.memset(onescol[:, 0:1], 1.0))

        wt("vector", S["prep"], n_gq)
        op("vector", "tensor_scalar", gq_bc[0:1, :], gq_bc[0:1, :], ALPHA,
           None, op0=ALU.mult, inc=S["vs"], bar=True)

        def bcast(row_ap, dst_ap, cols, dep_n):
            for j in range((cols + 511) // 512):
                c0, c1 = 512 * j, min(512 * (j + 1), cols)
                wt("tensor", S["prep"], dep_n)
                wt("tensor", S["vs"], S["vs"].n)
                pe_n = mm(prb.ap()[:, 0:c1 - c0], onesr_f[0:1, :],
                          row_ap[0:1, c0:c1], True, True, inc=S["pe"])
                wt("vector", S["pe"], pe_n)
                op("vector", "tensor_copy", dst_ap[:, c0:c1],
                   prb.ap()[:, 0:c1 - c0], inc=S["vs"])

        bcast(bq_bc, bq_bc, ES, n_bq)
        bcast(bk_bc, bk_bc, ES, n_bk)
        bcast(bv_bc, bv_bc, ES, n_bv)
        bcast(gq_bc, gq_bc, HD, n_gq)
        bcast(gk_bc, gk_bc, HD, n_gk)
        bcast(lnt, bo_bc, D, n_bo)
        bcast(scr, bf_bc, D, n_bf)

        p0_xt_g = [0, 0]
        for i in range(4):
            stg = xt[i % 2]
            wt("sync", S["vv"], p0_xt_g[i % 2])
            n_l = load(S["prep"], stg, wfs_in[128 * i:128 * (i + 1), :])
            wt("vector", S["prep"], n_l)
            p0_xt_g[i % 2] = op("vector", "tensor_reduce", st1[:, i:i + 1],
                                stg, AX.X, ALU.add,
                                apply_absolute_value=True, inc=S["vv"],
                                bar=True)
        v_n = op("vector", "tensor_reduce", st1[:, 4:5], st1[:, 0:4], AX.X,
                 ALU.add, inc=S["vs"], bar=True)
        wt("tensor", S["vs"], v_n)
        pe_n = mm(pex.ap()[0:1, 0:1], st1[:, 4:5], onescol[:, 0:1], True, True,
                  inc=S["pe"])
        wt("vector", S["pe"], pe_n)
        op("vector", "tensor_copy", wsrow[0:1, 0:1], pex.ap()[0:1, 0:1])
        vg_n = op("vector", "memset", wsrow[0:1, 1:8], 0.0, inc=S["vg"])
        wt("gpsimd", S["vg"], vg_n)
        st_n = store(S["stw"], ws_i, wsrow[0:1, 0:8])
        wt("gpsimd", S["stw"], st_n)
        cc_ws = cc("AllReduce", ALU.add, ws_i, ws_o)

        store(S["stq"], wqkv_d[:, 0:ES], wq_in)
        store(S["stq"], wqkv_d[:, ES:2 * ES], wk_in)
        st_wqkv = store(S["stq"], wqkv_d[:, 2 * ES:3 * ES], wv_in)
        st_wom = store(S["stw"], wom_d, wos_in)

        wt("sync", S["cc"], cc_ws)
        n_ws = load(S["x"], wsrow[0:1, 8:16], ws_o)
        wt("vector", S["x"], n_ws)
        op("vector", "tensor_scalar", wsrow[0:1, 0:1], wsrow[0:1, 8:9],
           MEAN_W, 1e-5, op0=ALU.mult, op1=ALU.max, bar=True)
        op("vector", "tensor_scalar", wsrow[0:1, 1:2], wsrow[0:1, 0:1],
           float(1.0 / 127.0), None, op0=ALU.mult, bar=True)
        v_n = op("vector", "reciprocal", wsrow[0:1, 2:3], wsrow[0:1, 0:1],
                 inc=S["vs"], bar=True)
        wt("tensor", S["vs"], v_n)
        mm(pex.ap()[:, 0:1], onesr_f[0:1, :], wsrow[0:1, 2:3], True, True)
        pe_n = mm(pex.ap()[:, 1:2], onesr_f[0:1, :], wsrow[0:1, 1:2], True,
                  True, inc=S["pe"])
        wt("vector", S["pe"], pe_n)
        op("vector", "tensor_copy", ws_bc[:, 0:1], pex.ap()[:, 0:1])
        WS_READY = op("vector", "tensor_copy", mwc_bc[:, 0:1], pex.ap()[:, 1:2],
                      inc=S["vs"])

        def ln_quant_phase(x_src, xa_dst, rx_sb, guards, chunks=None):
            cl = list(chunks) if chunks is not None else list(range(NCH))
            c0, ncl = cl[0], len(cl)
            C = lambda b: stw_t[:, 8 * b + c0:8 * b + c0 + ncl]
            Cc = lambda b, c: stw_t[:, 8 * b + c:8 * b + c + 1]
            p1v = [0] * 2
            for c in cl:
                sl = c % 2
                wt("sync", S["vv"], p1v[sl])
                if c == cl[0]:
                    for gs, gv in guards:
                        wt("sync", gs, gv)
                n_x = load(S["x"], xt[sl], x_src[128 * c:128 * (c + 1), :])
                wt("vector", S["x"], n_x)
                op("vector", "tensor_reduce", Cc(0, c), xt[sl], AX.X, ALU.add)
                op("vector", "tensor_reduce", Cc(2, c), xt[sl], AX.X, ALU.max)
                op("vector", "tensor_reduce", Cc(3, c), xt[sl], AX.X, ALU.min)
                op("vector", "tensor_mul", scr, xt[sl], xt[sl], bar=True)
                p1v[sl] = op("vector", "tensor_reduce", Cc(1, c), scr, AX.X,
                             ALU.add, inc=S["vv"], bar=True)
            op("vector", "tensor_scalar", C(4), C(0), INV_D, None,
               op0=ALU.mult, bar=True)
            op("vector", "tensor_mul", C(5), C(4), C(4), bar=True)
            op("vector", "scalar_tensor_tensor", C(6), C(1), INV_D, C(5),
               op0=ALU.mult, op1=ALU.subtract, bar=True)
            v_n = op("vector", "tensor_scalar", C(5), C(6), 1e-5, None,
                     op0=ALU.add, inc=S["vs"], bar=True)
            wt("scalar", S["vs"], v_n)
            s_n = op("scalar", "activation", C(15), C(5), AF.Sqrt,
                     inc=S["sv"])
            wt("vector", S["sv"], s_n)
            op("vector", "reciprocal", C(7), C(15), bar=True)
            op("vector", "tensor_scalar", C(8), C(7), 0.5, 1.0,
               op0=ALU.mult, op1=ALU.add, bar=True)
            op("vector", "tensor_mul", C(5), C(4), C(7), bar=True)
            op("vector", "tensor_scalar", C(9), C(5), -0.5, None,
               op0=ALU.mult, bar=True)
            op("vector", "tensor_mul", C(15), C(6), C(7), bar=True)
            op("vector", "tensor_mul", C(15), C(15), C(7), bar=True)
            v_n = op("vector", "tensor_scalar", C(15), C(15), 1e-5, None,
                     op0=ALU.add, inc=S["vs"], bar=True)
            wt("scalar", S["vs"], v_n)
            s_n = op("scalar", "activation", C(15), C(15), AF.Sqrt,
                     inc=S["sv"])
            wt("vector", S["sv"], s_n)
            op("vector", "reciprocal", C(10), C(15), bar=True)
            op("vector", "tensor_sub", C(2), C(2), C(4), bar=True)
            op("vector", "tensor_sub", C(3), C(4), C(3), bar=True)
            op("vector", "tensor_max", C(2), C(2), C(3), bar=True)
            op("vector", "tensor_mul", C(2), C(2), C(7), bar=True)
            op("vector", "tensor_mul", C(11), C(2), C(10), bar=True)
            op("vector", "tensor_scalar", C(11), C(11), 1e-5, None,
               op0=ALU.max, bar=True)
            op("vector", "reciprocal", C(12), C(11), bar=True)
            op("vector", "tensor_scalar", C(12), C(12), 127.0, None,
               op0=ALU.mult, bar=True)
            op("vector", "tensor_mul", C(12), C(10), C(12), bar=True)
            op("vector", "tensor_mul", C(13), C(7), C(12), bar=True)
            op("vector", "tensor_mul", C(14), C(5), C(12), bar=True)
            op("vector", "tensor_scalar", C(14), C(14), -1.0, None,
               op0=ALU.mult, bar=True)
            v_st = op("vector", "tensor_scalar", rx_sb[:, c0:c0 + ncl], C(11),
                      mwc_bc[:, 0:1], None, op0=ALU.mult, inc=S["vs"],
                      bar=True)
            xa_g = [0] * 2
            xq_g = [0] * 2
            tq_g = [0, 0]
            p3_sv = [0, 0]
            tq2 = [tq, scr]
            for j, c in enumerate(cl):
                sl = c % 2
                if j < 2:
                    wt("sync", S["vv"], p1v[sl])
                else:
                    wt("sync", S["sv"], p3_sv[sl])
                n_x = load(S["x"], xt[sl], x_src[128 * c:128 * (c + 1), :])
                wt("scalar", S["x"], n_x)
                if j == 0:
                    wt("scalar", S["vs"], v_st)
                wt("scalar", S["stx"], xa_g[sl])
                s_xa = op("scalar", "activation", xa[sl], xt[sl], AF.Identity,
                          bias=Cc(9, c), scale=Cc(8, c), inc=S["sg"])
                wt("gpsimd", S["sg"], s_xa)
                xa_g[sl] = store(S["stx"], xa_dst[128 * c:128 * (c + 1), :],
                                 xa[sl])
                wt("scalar", S["vv"], tq_g[sl])
                s_t = op("scalar", "activation", tq2[sl], xt[sl], AF.Identity,
                         bias=Cc(14, c), scale=Cc(13, c), inc=S["sv"])
                p3_sv[sl] = s_t
                wt("vector", S["sv"], s_t)
                tq_g[sl] = op("vector", "tensor_copy", i32t, tq2[sl],
                              inc=S["vv"], bar=True)
                wt("vector", S["stq"], xq_g[sl])
                vg_n = op("vector", "tensor_copy", xqbf[sl], i32t, inc=S["vg"])
                wt("gpsimd", S["vg"], vg_n)
                xq_g[sl] = store(S["stq"], xq_d[128 * c:128 * (c + 1), :],
                                 xqbf[sl])
            return None

        P0_PE = S["pe"].n
        wint_st = 0
        p2_xt_g = [p0_xt_g[0], p0_xt_g[1]]
        p2_tq_g = 0
        for i in range(4):
            sl = i % 2
            if i < 2:
                wt("sync", S["vv"], p0_xt_g[sl])
            else:
                wt("sync", S["sv"], p2_xt_g[sl])
            n_w = load(S["x"], xt[sl], wfs_in[128 * i:128 * (i + 1), :])
            wt("scalar", S["x"], n_w)
            if i == 0:
                wt("scalar", S["vs"], WS_READY)
            wt("scalar", S["vv"], p2_tq_g)
            s_n = op("scalar", "activation", tq, xt[sl], AF.Copy,
                     scale=ws_bc[:, 0:1], inc=S["sv"])
            p2_xt_g[sl] = s_n
            wt("vector", S["sv"], s_n)
            p2_tq_g = op("vector", "tensor_copy", i32t, tq, inc=S["vv"],
                         bar=True)
            wt("vector", S["stw"], wint_st)
            vg_n = op("vector", "tensor_scalar", xqbf[sl], i32t, 1.0, -1.0,
                      op0=ALU.min, op1=ALU.max, inc=S["vg"])
            wt("gpsimd", S["vg"], vg_n)
            wint_st = store(S["stw"], wint_i[128 * i:128 * (i + 1), :],
                            xqbf[sl])
        P2A_SV = S["sv"].n
        P2A_VV = S["vv"].n
        P2A_VG = S["vg"].n
        wtT = [R(60, BF16, ES), R(68, BF16, ES)]
        wt_stl = []
        for k in range(32):
            sl = k % 2
            wt("sync", S["stw"], wint_st if k < 2 else wt_stl[k - 2])
            n_l = load(WR[sl], wtT[sl], wint_i[0:ES, 128 * k:128 * (k + 1)],
                       transpose=True, serial=False)
            wt("gpsimd", WR[sl], n_l)
            wt_stl.append(store(S["stw"], wintT_i[128 * k:128 * (k + 1), :],
                                wtT[sl]))
        wt("gpsimd", S["stw"], wt_stl[-1])
        cc_wint = cc("AllGather", ALU.bypass, wintT_i, wintT_o)
        wintT_flat = wintT_o.rearrange("r p d -> (r p) d")
        P2_STW = S["stw"].n

        g_p1 = [(S["vs"], WS_READY), (S["sv"], P2A_SV), (S["vv"], P2A_VV),
                (S["vg"], P2A_VG), (S["stw"], wint_st), (S["pe"], P0_PE)]
        ln_quant_phase(x_in, xA, rx1, g_p1)
        P1_STX = S["stx"].n
        P1_STQ = S["stq"].n
        P1_SV = S["sv"].n
        P1_VG = S["vg"].n

        wring_g = [0] * 8
        widx = [0]
        xr_g = [0] * 4
        xo_g = {"stx": [0, 0], "out": [0, 0]}

        def ffn_like_phase(dst_sem_name, lhsT_sb, rhs_src_fn, rhs_transpose,
                           res_src, dst, epi_fn, guards_sync, guards_tensor,
                           per_k_guards=None, lhs_k_guards=None,
                           ts_list=(0, 1), og_list=None):
            st_sem = S[dst_sem_name]
            bank_g = [S["vs"].n] * 2
            for gs, gv in guards_sync:
                wt("sync", gs, gv)
            for gs, gv in guards_tensor:
                wt("tensor", gs, gv)
            ts0 = list(ts_list)[0]
            ogl = list(og_list) if og_list is not None else list(range(8))
            og0 = ogl[0]
            steps = [(ts, og, k) for ts in ts_list for og in ogl
                     for k in range(32)]
            w0 = widx[0]
            PE0 = S["pe"].n
            n_ws = [0] * len(steps)

            def issue_step(j):
                ts_, og_, k_ = steps[j]
                if j >= 8:
                    wt("scalar", S["pe"], PE0 + j - 7)
                n_ws[j] = load(WR[(w0 + j) % 8], wring[(w0 + j) % 8],
                               rhs_src_fn(og_, k_), serial=False,
                               queue="scalar")

            wt("scalar", S["pe"], PE0)
            for gs, gv in guards_tensor:
                wt("scalar", gs, gv)
            for j in range(min(8, len(steps))):
                issue_step(j)
            for j, (ts, og, k) in enumerate(steps):
                sl = (w0 + j) % 8
                bg = og % 2
                if k == 0:
                    wt("tensor", S["vs"], bank_g[bg])
                if lhs_k_guards is not None and ts == ts0 and og == og0:
                    wt("tensor", *lhs_k_guards[k])
                wt("tensor", WR[sl], n_ws[j])
                for tt in range(4):
                    t0 = 1024 * k + 512 * ts + 128 * tt
                    pe_group = mm(PS8[4 * bg + tt],
                                  lhsT_sb[:, t0:t0 + 128],
                                  wring[sl][:], k == 0, k == 31,
                                  inc=S["pe"] if tt == 3 else None)
                widx[0] += 1
                if j + 8 < len(steps):
                    issue_step(j + 8)
                if k == 31:
                    last_vs = 0
                    for tt in range(4):
                        tg = 4 * ts + tt
                        rows = slice(128 * tg, 128 * (tg + 1))
                        cols = slice(512 * og, 512 * (og + 1))
                        xsl = (2 * og + tt) % 4
                        wt("sync", S["vg"], xr_g[xsl])
                        n_xr = load(S["x"], xr[xsl], res_src[rows, cols])
                        last_vs = epi_fn(bg, tt, tg, rows, cols, xsl, n_xr,
                                         pe_group, st_sem, dst)
                    bank_g[bg] = last_vs

        def make_ffn_epi(rx_sb, st_sem_name):
            g_guard = [0, 0]

            def epi(bg, tt, tg, rows, cols, xsl, n_xr, pe_group, st_sem, dst):
                ysl = tt % 2
                wt("vector", S["pe"], pe_group)
                wt("vector", S["sv"], g_guard[ysl])
                s_n = op("vector", "scalar_tensor_tensor", yt_t[ysl],
                         PS8[4 * bg + tt], rx_sb[:, tg:tg + 1], bf_bc[:, cols],
                         op0=ALU.mult, op1=ALU.add, inc=S["vs"])
                wt("scalar", S["vs"], s_n)
                s_g = op("scalar", "activation", g_t[ysl], yt_t[ysl], AF.Gelu,
                         inc=S["sv"])
                g_guard[ysl] = s_g
                osl = tt % 2
                wt("vector", S["sv"], s_g)
                wt("vector", S["x"], n_xr)
                wt("vector", st_sem, xo_g[st_sem_name][osl])
                vg_n = op("vector", "scalar_tensor_tensor", xo[osl], g_t[ysl],
                          0.5, xr[xsl], op0=ALU.mult, op1=ALU.add, inc=S["vg"])
                xr_g[xsl] = vg_n
                wt("gpsimd", S["vg"], vg_n)
                xo_g[st_sem_name][osl] = store(st_sem, dst[rows, cols],
                                               xo[osl])
                return s_n
            return epi

        def ffn_phase(rx_sb, res_src, dst, st_sem_name, guards):
            for gs, gv in guards:
                wt("sync", gs, gv)
            xqt_k = []
            for k in range(32):
                xqt_k.append((HR[k % 4],
                              load(HR[k % 4], xqT[:, 1024 * k:1024 * (k + 1)],
                                   xq_d[0:TPC, 128 * k:128 * (k + 1)],
                                   transpose=True, serial=True)))
            ffn_like_phase(
                st_sem_name, xqT,
                lambda og, k: wintT_flat[4096 * og + 128 * k:
                                         4096 * og + 128 * (k + 1), :],
                False, res_src, dst, make_ffn_epi(rx_sb, st_sem_name),
                [], [(S["cc"], cc_wint)], lhs_k_guards=xqt_k)

        if stage >= 3:
            ffn_phase(rx1, xA, xB, "stx",
                      [(S["stq"], P1_STQ), (S["stx"], P1_STX)])
        P3_PE = S["pe"].n
        P3_STX = S["stx"].n
        P3_SV = S["sv"].n
        P3_VG = S["vg"].n

        cc_h2 = [0, 0]
        P4_STH = 0
        if stage >= 5:
            h2_g = [0] * 2
            CB = lambda bk: stw_t[:, 8 * bk:8 * (bk + 1)]
            CBc = lambda bk, cc_: stw_t[:, 8 * bk + cc_:8 * bk + cc_ + 1]
            p4v = [0] * 2
            for c in range(NCH):
                sl = c % 2
                wt("sync", S["vv"], p4v[sl])
                if c == 0:
                    wt("sync", S["pe"], P3_PE)
                    wt("sync", S["stx"], P3_STX)
                    wt("sync", S["sv"], S["sv"].n)
                    wt("sync", S["vg"], S["vg"].n)
                n_x = load(S["x"], xt[sl], xB[128 * c:128 * (c + 1), :])
                wt("vector", S["x"], n_x)
                op("vector", "tensor_reduce", CBc(0, c), xt[sl], AX.X,
                   ALU.add)
                op("vector", "tensor_mul", scr, xt[sl], xt[sl], bar=True)
                p4v[sl] = op("vector", "tensor_reduce", CBc(1, c), scr, AX.X,
                             ALU.add, inc=S["vv"], bar=True)
            op("vector", "tensor_scalar", CB(4), CB(0), INV_D, None,
               op0=ALU.mult, bar=True)
            op("vector", "tensor_mul", CB(5), CB(4), CB(4), bar=True)
            op("vector", "scalar_tensor_tensor", CB(6), CB(1), INV_D, CB(5),
               op0=ALU.mult, op1=ALU.subtract, bar=True)
            v_n = op("vector", "tensor_scalar", CB(5), CB(6), 1e-5, None,
                     op0=ALU.add, inc=S["vs"], bar=True)
            wt("scalar", S["vs"], v_n)
            s_n = op("scalar", "activation", CB(3), CB(5), AF.Sqrt,
                     inc=S["sv"])
            wt("vector", S["sv"], s_n)
            op("vector", "reciprocal", CB(7), CB(3), bar=True)
            v_st4 = op("vector", "scalar_tensor_tensor", CB(9), CB(4), -1.0,
                       CB(7), op0=ALU.mult, op1=ALU.mult, inc=S["vs"],
                       bar=True)
            p4_sg = [0, 0]
            for c in range(NCH):
                sl = c % 2
                if c < 2:
                    wt("sync", S["vv"], p4v[sl])
                else:
                    wt("sync", S["sg"], p4_sg[sl])
                n_x = load(S["x"], xt[sl], xB[128 * c:128 * (c + 1), :])
                wt("scalar", S["x"], n_x)
                if c == 0:
                    wt("scalar", S["vs"], v_st4)
                wt("scalar", S["sth"], h2_g[sl])
                sg_n = op("scalar", "activation", h2st[sl], xt[sl],
                          AF.Identity, bias=CBc(9, c), scale=CBc(7, c),
                          inc=S["sg"])
                p4_sg[sl] = sg_n
                wt("gpsimd", S["sg"], sg_n)
                h2_g[sl] = store(
                    S["sth"], h2_i[c // 4][128 * (c % 4):128 * (c % 4) + 128, :],
                    h2st[sl])
                if c == 3:
                    wt("gpsimd", S["sth"], S["sth"].n)
                    cc_h2[0] = cc("AllGather", ALU.bypass, h2_i[0], h2_o[0])
                if c == 7:
                    wt("gpsimd", S["sth"], S["sth"].n)
                    cc_h2[1] = cc("AllGather", ALU.bypass, h2_i[1], h2_o[1])
            P4_STH = S["sth"].n

        hring_g = [0] * 8
        qn_g = [0] * 2
        at_g = [0] * 2
        cc_at = [0, 0]
        if stage >= 5:
            Ehalf_g = [0, 0]
            pair_g = [0, 0]
            pl_g = 0
            pavrb_g = 0
            rank1_g = 0
            for b in range(1 if dump == 24 else B):
                wt("sync", S["pe"], S["pe"].n)
                wt("sync", S["sf"], S["sf"].n)
                wt("sync", S["sv"], S["sv"].n)
                wt("sync", S["vg"], S["vg"].n)
                wt("sync", S["stq"], st_wqkv if b == 0 else S["stq"].n)
                if b == 0 and stage >= 6:
                    wt("sync", S["sth"], P4_STH)
                n_wqkv = 0
                for (w_sb, off) in ((wq_sb, 0), (wk_sb, ES), (wv_sb, 2 * ES)):
                    for k in range(32):
                        n_wqkv = load(S["wqkv"], w_sb[:, ES * k:ES * (k + 1)],
                                      wqkv_d[128 * k:128 * (k + 1),
                                             off:off + ES], serial=False)
                wt("tensor", S["wqkv"], n_wqkv)
                wt("sync", S["cc"], cc_h2[b])
                if b == 0:
                    wt("tensor", S["cc"], cc_h2[1])

                qkv_bank_g = S["vs"].n
                v_bank_g = S["vs"].n
                qkv_sq_g = S["sv"].n
                for tc in range(8):
                    r = tc
                    Xh = b
                    wt("tensor", S["vs"], qkv_bank_g)
                    pe_qk = 0
                    for k in range(32):
                        sl = k % 8
                        wt("sync", S["pe"], hring_g[sl])
                        n_h = load(HR[sl], hring[sl],
                                   h2_o[Xh][r][0:512, 128 * k:128 * (k + 1)],
                                   transpose=True, serial=False)
                        wt("tensor", HR[sl], n_h)
                        for tt in range(4):
                            mm(PS8[tt], hring[sl][:, 128 * tt:128 * (tt + 1)],
                               wq_sb[:, ES * k:ES * (k + 1)], k == 0, k == 31)
                        for tt in range(4):
                            pe_qk = mm(PS8[4 + tt],
                                       hring[sl][:, 128 * tt:128 * (tt + 1)],
                                       wk_sb[:, ES * k:ES * (k + 1)],
                                       k == 0, k == 31,
                                       inc=S["pe"] if tt == 3 else None)
                        hring_g[sl] = pe_qk
                    vs_qk = 0
                    vs_q4 = 0
                    sq_guard = qkv_sq_g
                    for qk in range(2):
                        bias_bc = bq_bc if qk == 0 else bk_bc
                        g_bc = gq_bc if qk == 0 else gk_bc
                        dstd = qn_d if qk == 0 else kn_d
                        for tt in range(4):
                            bank = PS8[4 * qk + tt]
                            wt("vector", S["pe"], pe_qk)
                            wt("vector", S["sv"], sq_guard)
                            s_n = op("vector", "scalar_tensor_tensor", qtmp,
                                     bank, 1.0, bias_bc, op0=ALU.mult,
                                     op1=ALU.add, inc=S["vs"], bar=True)
                            vs_qk = s_n
                            wt("scalar", S["vs"], s_n)
                            for hh in range(4):
                                cseg = slice(128 * hh, 128 * (hh + 1))
                                sq_guard = op(
                                    "scalar", "activation", sqq[:, cseg],
                                    qtmp[:, cseg], AF.Square,
                                    accum_out=st3[:, hh:hh + 1],
                                    inc=S["sv"] if hh == 3 else None)
                            wt("vector", S["sv"], sq_guard)
                            v_n = op("vector", "tensor_scalar", st3[:, 0:4],
                                     st3[:, 0:4], float(1.0 / HD), 1e-6,
                                     op0=ALU.mult, op1=ALU.add, inc=S["vs"],
                                     bar=True)
                            wt("scalar", S["vs"], v_n)
                            s_n = op("scalar", "activation", st3[:, 4:8],
                                     st3[:, 0:4], AF.Sqrt, inc=S["sv"])
                            wt("vector", S["sv"], s_n)
                            op("vector", "reciprocal", st3[:, 8:12],
                               st3[:, 4:8], bar=True)
                            qsl = (2 * qk + tt) % 2
                            wt("vector", S["stq"], qn_g[qsl])
                            vg_n = 0
                            for hh in range(4):
                                cseg = slice(128 * hh, 128 * (hh + 1))
                                vg_n = op("vector", "scalar_tensor_tensor",
                                          qnst[qsl][:, cseg], qtmp[:, cseg],
                                          st3[:, 8 + hh:9 + hh], g_bc,
                                          op0=ALU.mult, op1=ALU.mult,
                                          inc=S["vg"] if hh == 3 else None)
                            wt("gpsimd", S["vg"], vg_n)
                            trow = 512 * tc + 128 * tt
                            qn_g[qsl] = store(S["stq"],
                                              dstd[trow:trow + 128, :],
                                              qnst[qsl])
                        if qk == 0:
                            vs_q4 = vs_qk
                    qkv_sq_g = sq_guard
                    wt("tensor", S["vs"], vs_q4)
                    pe_v = 0
                    for k in range(32):
                        sl = k % 8
                        wt("sync", S["pe"], hring_g[sl])
                        n_h = load(HR[sl], hring[sl],
                                   h2_o[Xh][r][0:512, 128 * k:128 * (k + 1)],
                                   transpose=True, serial=False)
                        wt("tensor", HR[sl], n_h)
                        for tt in range(4):
                            pe_v = mm(PS8[tt],
                                      hring[sl][:, 128 * tt:128 * (tt + 1)],
                                      wv_sb[:, ES * k:ES * (k + 1)],
                                      k == 0, k == 31,
                                      inc=S["pe"] if tt == 3 else None)
                        hring_g[sl] = pe_v
                    for tt in range(4):
                        wt("vector", S["pe"], pe_v)
                        cseg = slice(512 * (4 * tc + tt),
                                     512 * (4 * tc + tt + 1))
                        qkv_bank_g = op("vector", "scalar_tensor_tensor",
                                        V_sb[:, cseg], PS8[tt], 1.0, bv_bc,
                                        op0=ALU.mult, op1=ALU.add, inc=S["vs"])
                QKV_STQ = S["stq"].n
                QKV_PE = S["pe"].n
                QKV_VS = S["vs"].n

                wt("sync", S["stq"], QKV_STQ)
                wt("sync", S["pe"], QKV_PE)
                n_qt = 0
                for (dsrc, dsb) in ((qn_d, qT), (kn_d, kT)):
                    for h in range(HPC):
                        n_qt = load(S["t"], dsb[:, N * h:N * (h + 1)],
                                    dsrc[0:N, 128 * h:128 * (h + 1)],
                                    transpose=True, serial=False)
                wt("tensor", S["t"], n_qt)
                wt("tensor", S["vs"], QKV_VS)

                for h in range(HPC):
                    for qc in range(8):
                        qcols = slice(N * h + 512 * qc,
                                      N * h + 512 * (qc + 1))
                        av_n = 0
                        for half in range(2):
                            exp_n = 0
                            for g_ in range(8):
                                pr = g_ % 2
                                ps_pair = ps2a if pr == 0 else ps2b
                                wt("tensor", S["sf"], pair_g[pr])
                                pe_n = 0
                                for j in range(2):
                                    kt = 16 * half + 2 * g_ + j
                                    kcol = slice(N * h + 128 * kt,
                                                 N * h + 128 * (kt + 1))
                                    pe_n = mm(
                                        ps_pair.ap()[:, 512 * j:512 * (j + 1)],
                                        kT[:, kcol], qT[:, qcols], True, True,
                                        inc=S["pe"] if j == 1 else None)
                                wt("scalar", S["pe"], pe_n)
                                wt("scalar", S["pe"], Ehalf_g[half])
                                ecol = slice(1024 * g_, 1024 * (g_ + 1))
                                exp_n = op("scalar", "activation",
                                           Eb[half][:, ecol], ps_pair.ap()[:],
                                           AF.Exp, inc=S["sf"])
                                pair_g[pr] = exp_n
                            wt("tensor", S["sf"], exp_n)
                            if half == 0:
                                wt("tensor", S["vg"], pavrb_g)
                                wt("tensor", S["vs"], pl_g)
                            for kt16 in range(16):
                                kt = 16 * half + kt16
                                erhs = Eb[half][:, 512 * kt16:512 * (kt16 + 1)]
                                vcol = slice(512 * kt + 128 * h,
                                             512 * kt + 128 * (h + 1))
                                mm(pav.ap()[:], V_sb[:, vcol], erhs,
                                   kt == 0, kt == 31)
                                av_n = mm(pl.ap()[0:1, :], ones_bf[:, 0:1],
                                          erhs, kt == 0, kt == 31,
                                          inc=S["pe"] if kt16 == 15 else None)
                            Ehalf_g[half] = av_n
                        wt("vector", S["pe"], av_n)
                        wt("vector", S["pe"], rank1_g)
                        pl_g = op("vector", "reciprocal", lrow[0:1, :],
                                  pl.ap()[0:1, :], inc=S["vs"])
                        wt("tensor", S["vs"], pl_g)
                        rank1_g = mm(prb.ap()[:], onesr_f[0:1, :],
                                     lrow[0:1, :], True, True, inc=S["pe"])
                        asl = qc % 2
                        wt("vector", S["pe"], rank1_g)
                        op("vector", "tensor_copy", rbc_sb, prb.ap()[:],
                           bar=True)
                        wt("vector", S["sth"], at_g[asl])
                        vg_n = op("vector", "tensor_mul", atst[asl],
                                  pav.ap()[:], rbc_sb, inc=S["vg"])
                        pavrb_g = vg_n
                        wt("gpsimd", S["vg"], vg_n)
                        gcol = 4096 * b + 512 * qc
                        at_g[asl] = store(
                            S["sth"],
                            at_d[128 * h:128 * (h + 1), gcol:gcol + 512],
                            atst[asl])

        cc_rs = [0, 0]
        if stage >= 6:
            wt("sync", S["sth"], S["sth"].n)
            wt("sync", S["pe"], S["pe"].n)
            wt("sync", S["sf"], S["sf"].n)
            wt("sync", S["vg"], S["vg"].n)
            wt("sync", S["stw"], st_wom)
            n_at = 0
            for e in range(4):
                n_at = load(S["t"], at4[e], at_d[128 * e:128 * (e + 1), :],
                            serial=False)
            n_wo = 0
            for e in range(4):
                n_wo = load(S["wqkv"], wo_sb[:, D * e:D * (e + 1)],
                            wom_d[128 * e:128 * (e + 1), :], serial=False)
            wt("tensor", S["t"], n_at)
            wt("tensor", S["wqkv"], n_wo)
            wt("tensor", S["vs"], S["vs"].n)
            rnd_g = [S["vs"].n] * 2
            stg_g = [0, 0]
            rs_st = 0
            for T in range(64):
                ssl = T % 2
                vs_cp = 0
                for rnd in range(2):
                    gr = (2 * T + rnd) % 2
                    wt("tensor", S["vs"], rnd_g[gr])
                    pe_n = 0
                    for e in range(4):
                        for oc in range(4):
                            o = 4 * rnd + oc
                            pe_n = mm(PS8[4 * gr + oc],
                                      at4[e][:, 128 * T:128 * (T + 1)],
                                      wo_sb[:, D * e + 512 * o:D * e + 512 * (o + 1)],
                                      e == 0, e == 3,
                                      inc=S["pe"] if oc == 3 else None)
                    for oc in range(4):
                        o = 4 * rnd + oc
                        wt("vector", S["pe"], pe_n)
                        if rnd == 0 and oc == 0:
                            wt("vector", S["stx"], stg_g[ssl])
                        vs_cp = op("vector", "tensor_copy",
                                   stg6[ssl][:, 512 * o:512 * (o + 1)],
                                   PS8[4 * gr + oc], inc=S["vs"])
                    rnd_g[gr] = vs_cp
                wt("gpsimd", S["vs"], vs_cp)
                stg_g[ssl] = store(S["stx"], rs_i[128 * T:128 * (T + 1), :],
                                   stg6[ssl])
                if T == 31:
                    wt("gpsimd", S["stx"], S["stx"].n)
                    cc_rs[0] = cc("ReduceScatter", ALU.add,
                                  rs_i[0:4096, :], rs_o[0:512, :])
                if T == 63:
                    wt("gpsimd", S["stx"], S["stx"].n)
                    cc_rs[1] = cc("ReduceScatter", ALU.add,
                                  rs_i[4096:8192, :], rs_o[512:1024, :])
            P6A_PE = S["pe"].n

            wint8_rhs = lambda og, k: wintT_flat[4096 * og + 128 * k:
                                                 4096 * og + 128 * (k + 1), :]
            xbstage = [R(28 + 8 * j, F32, 2048) for j in range(8)]
            wt("sync", S["pe"], P6A_PE)
            n_xb0 = 0
            for j in range(8):
                c2, hf = divmod(j, 2)
                n_xb0 = load(S["x"], xbstage[j],
                             xB[128 * c2:128 * (c2 + 1),
                                2048 * hf:2048 * (hf + 1)], serial=False)
            wt("sync", S["cc"], cc_rs[0])
            x3_st_g = 0
            for j in range(8):
                c2, hf = divmod(j, 2)
                rows = slice(128 * c2, 128 * (c2 + 1))
                cols = slice(2048 * hf, 2048 * (hf + 1))
                wt("sync", S["vg"], S["vg"].n)
                n_r = load(S["x"], rsb_t, rs_o[rows, cols])
                wt("vector", S["x"], n_r)
                wt("vector", S["stx"], x3_st_g)
                op("vector", "scalar_tensor_tensor", x3st, rsb_t, 1.0,
                   xbstage[j], op0=ALU.mult, op1=ALU.add, bar=True)
                vg_n = op("vector", "tensor_add", x3st, x3st,
                          bo_bc[:, cols], inc=S["vg"])
                wt("gpsimd", S["vg"], vg_n)
                x3_st_g = store(S["stx"], xA[rows, cols], x3st)
            P6B0_STX = S["stx"].n
            if stage >= 8:
                g_p7a = [(S["stx"], P6B0_STX), (S["pe"], P6A_PE),
                         (S["sv"], S["sv"].n), (S["vg"], S["vg"].n),
                         (S["stq"], S["stq"].n), (S["stw"], S["stw"].n),
                         (S["sth"], S["sth"].n), (S["sf"], S["sf"].n)]
                ln_quant_phase(xA, xB, rx2, g_p7a, chunks=range(0, 4))
                epi8 = make_ffn_epi(rx2, "out")
                g_p8a = [(S["stq"], S["stq"].n), (S["stx"], S["stx"].n)]
                for gs, gv in g_p8a:
                    wt("sync", gs, gv)
                xqt_a = []
                for k in range(32):
                    xqt_a.append((HR[k % 4],
                                  load(HR[k % 4],
                                       xqT[:, 1024 * k:1024 * k + 512],
                                       xq_d[0:512, 128 * k:128 * (k + 1)],
                                       transpose=True, serial=True)))
                ffn_like_phase("out", xqT, wint8_rhs, False, xB, out_t, epi8,
                               [], [], lhs_k_guards=xqt_a, ts_list=(0,),
                               og_list=range(0, 4))
            wt("sync", S["cc"], cc_rs[1])
            wt("sync", S["stq"], S["stq"].n)
            x3_st_g1 = 0
            for c2 in range(4):
                rows = slice(512 + 128 * c2, 512 + 128 * (c2 + 1))
                for hf in range(2):
                    cols = slice(2048 * hf, 2048 * (hf + 1))
                    wt("sync", S["vg"], S["vg"].n)
                    n_r = load(S["x"], rsb_t, rs_o[rows, cols])
                    n_x2 = load(S["x"], xbt_t, xB[rows, cols])
                    wt("vector", S["x"], n_x2)
                    wt("vector", S["stx"], x3_st_g1)
                    op("vector", "scalar_tensor_tensor", x3st, rsb_t, 1.0,
                       xbt_t, op0=ALU.mult, op1=ALU.add, bar=True)
                    vg_n = op("vector", "tensor_add", x3st, x3st,
                              bo_bc[:, cols], inc=S["vg"])
                    wt("gpsimd", S["vg"], vg_n)
                    x3_st_g1 = store(S["stx"], xA[rows, cols], x3st)
        P6_STX = S["stx"].n
        P6_PE = S["pe"].n

        if stage >= 8:
            g_p7b = [(S["stx"], P6_STX), (S["pe"], P6_PE),
                     (S["sv"], S["sv"].n), (S["vg"], S["vg"].n),
                     (S["stq"], S["stq"].n), (S["stw"], S["stw"].n),
                     (S["sth"], S["sth"].n), (S["sf"], S["sf"].n)]
            ln_quant_phase(xA, xB, rx2, g_p7b, chunks=range(4, 8))
            wt("sync", S["stq"], S["stq"].n)
            xqt_c = []
            for k in range(32):
                xqt_c.append((HR[k % 4],
                              load(HR[k % 4],
                                   xqT[:, 1024 * k:1024 * k + 512],
                                   xq_d[0:512, 128 * k:128 * (k + 1)],
                                   transpose=True, serial=True)))
            ffn_like_phase("out", xqT, wint8_rhs, False, xB, out_t, epi8,
                           [], [], lhs_k_guards=xqt_c, ts_list=(0,),
                           og_list=range(4, 8))
            g_p8b = [(S["stq"], S["stq"].n), (S["stx"], S["stx"].n)]
            for gs, gv in g_p8b:
                wt("sync", gs, gv)
            xqt_b = []
            for k in range(32):
                xqt_b.append((HR[k % 4],
                              load(HR[k % 4],
                                   xqT[:, 1024 * k + 512:1024 * (k + 1)],
                                   xq_d[512:1024, 128 * k:128 * (k + 1)],
                                   transpose=True, serial=True)))
            ffn_like_phase("out", xqT, wint8_rhs, False, xB, out_t, epi8,
                           [], [], lhs_k_guards=xqt_b, ts_list=(1,))

        if dump:
            for nm in ["stx", "stq", "sth", "stw", "cc", "vg", "sg"]:
                wt("gpsimd", S[nm], S[nm].n)
            if dump == 21:
                store(S["out"], out_t[:, 0:D // 2].bitcast(BF16), xq_d)
            elif dump == 22:
                store(S["out"], out_t[0:512, 0:D // 2].bitcast(BF16), wint_i)
            elif dump == 23:
                store(S["out"], out_t[0:512, 0:D // 2].bitcast(BF16)
                      .rearrange("p (a b) -> p a b", a=2)
                      .rearrange("p a b -> (p a) b") if False else
                      out_t.bitcast(BF16)[0:512, 0:B * N], at_d)
            elif dump == 24:
                store(S["out"], out_t.bitcast(BF16)[0:1024, 0:ES],
                      qn_d[1280:2304])
        elif stage < 8:
            src = {1: xA, 3: xB, 6: xA}[stage]
            for nm in ["stx", "stq", "sth", "stw", "cc", "vg", "sg"]:
                wt("gpsimd", S[nm], S[nm].n)
            store(S["out"], out_t, src)

        wt("sync", S["out"], S["out"].n)
        wt("gpsimd", S["out"], S["out"].n)

        block = es.enter_context(nc.Block())
        @block.sync
        def _(e):
            for fn in prog["sync"]:
                fn(e)

        @block.gpsimd
        def _(e):
            for fn in prog["gpsimd"]:
                fn(e)

        @block.vector
        def _(e):
            for fn in prog["vector"]:
                fn(e)

        @block.scalar
        def _(e):
            for fn in prog["scalar"]:
                fn(e)

        @block.tensor
        def _(e):
            for fn in prog["tensor"]:
                fn(e)

    return nc


def kernel(L0_attn_hidden_states, L0_attn_Wq, L0_attn_bq, L0_attn_Wk,
           L0_attn_bk, L0_attn_Wv, L0_attn_bv, L0_attn_Wo, L0_attn_bo,
           L0_attn_gq, L0_attn_gk, L0_ffn_W, L0_ffn_b):
    x = np.ascontiguousarray(
        np.asarray(L0_attn_hidden_states, dtype=np.float32).reshape(B * N, D))
    Wq = np.asarray(L0_attn_Wq, np.float32)
    Wk = np.asarray(L0_attn_Wk, np.float32)
    Wv = np.asarray(L0_attn_Wv, np.float32)
    Wo = np.asarray(L0_attn_Wo, np.float32)
    Wf = np.asarray(L0_ffn_W, np.float32)
    bq = np.asarray(L0_attn_bq, np.float32).reshape(1, D)
    bk = np.asarray(L0_attn_bk, np.float32).reshape(1, D)
    bv = np.asarray(L0_attn_bv, np.float32).reshape(1, D)
    bo = np.asarray(L0_attn_bo, np.float32).reshape(1, D)
    bf_ = np.asarray(L0_ffn_b, np.float32).reshape(1, D)
    gq = np.asarray(L0_attn_gq, np.float32).reshape(1, HD)
    gk = np.asarray(L0_attn_gk, np.float32).reshape(1, HD)

    in_maps = []
    for c in range(NC):
        esl = slice(ES * c, ES * (c + 1))
        myrows = np.r_[512 * c:512 * (c + 1), N + 512 * c:N + 512 * (c + 1)]
        in_maps.append({
            "x_my": np.ascontiguousarray(x[myrows]),
            "wq_my": np.ascontiguousarray(Wq[:, esl]),
            "wk_my": np.ascontiguousarray(Wk[:, esl]),
            "wv_my": np.ascontiguousarray(Wv[:, esl]),
            "bq_my": np.ascontiguousarray(bq[:, esl]),
            "bk_my": np.ascontiguousarray(bk[:, esl]),
            "bv_my": np.ascontiguousarray(bv[:, esl]),
            "gq": gq, "gk": gk,
            "wfs_my": np.ascontiguousarray(Wf[esl, :]),
            "wos_my": np.ascontiguousarray(Wo[esl, :]),
            "bf": bf_, "bo": bo,
        })
    nc = build_nc(_STAGE)
    res = run_bass_kernel_spmd(nc, in_maps, core_ids=list(range(NC)))
    out = np.empty((B * N, D), np.float32)
    for c in range(NC):
        oc = res.results[c]["out"]
        out[512 * c:512 * (c + 1)] = oc[0:512]
        out[N + 512 * c:N + 512 * (c + 1)] = oc[512:1024]
    return out.reshape(B, N, D)


# version 13
# speedup vs baseline: 1.0214x; 1.0029x over previous
import numpy as np
import concourse.bass as bass
import concourse.mybir as mybir
from concourse.bass_utils import run_bass_kernel_spmd

F32 = mybir.dt.float32
BF16 = mybir.dt.bfloat16
I32 = mybir.dt.int32
AF = mybir.ActivationFunctionType
ALU = mybir.AluOpType
AX = mybir.AxisListType

D = 4096
B = 2
N = 4096
TPC = 1024
NC = 8
HD = 128
HPC = 4
ES = HPC * HD
NCH = TPC // 128
RG = [[0, 1, 2, 3, 4, 5, 6, 7]]
ALPHA = float(1.0 / np.sqrt(HD))
INV_D = float(1.0 / D)
MEAN_W = float(2.0 ** -24)

_STAGE = 8


class Sem:
    def __init__(self, h):
        self.h = h
        self.n = 0


def build_nc(stage):
    dump = stage if stage in (21, 22, 23, 24) else None
    if dump in (21, 22):
        stage = 2
    elif dump in (23, 24):
        stage = 5
    nc = bass.Bass("TRN2", target_bir_lowering=False, debug=False)

    x_in = nc.dram_tensor("x_my", [TPC, D], F32, kind="ExternalInput").ap()
    wq_in = nc.dram_tensor("wq_my", [D, ES], F32, kind="ExternalInput").ap()
    wk_in = nc.dram_tensor("wk_my", [D, ES], F32, kind="ExternalInput").ap()
    wv_in = nc.dram_tensor("wv_my", [D, ES], F32, kind="ExternalInput").ap()
    bq_in = nc.dram_tensor("bq_my", [1, ES], F32, kind="ExternalInput").ap()
    bk_in = nc.dram_tensor("bk_my", [1, ES], F32, kind="ExternalInput").ap()
    bv_in = nc.dram_tensor("bv_my", [1, ES], F32, kind="ExternalInput").ap()
    gq_in = nc.dram_tensor("gq", [1, HD], F32, kind="ExternalInput").ap()
    gk_in = nc.dram_tensor("gk", [1, HD], F32, kind="ExternalInput").ap()
    wfs_in = nc.dram_tensor("wfs_my", [ES, D], F32, kind="ExternalInput").ap()
    wos_in = nc.dram_tensor("wos_my", [ES, D], F32, kind="ExternalInput").ap()
    bf_in = nc.dram_tensor("bf", [1, D], F32, kind="ExternalInput").ap()
    bo_in = nc.dram_tensor("bo", [1, D], F32, kind="ExternalInput").ap()
    out_t = nc.dram_tensor("out", [TPC, D], F32, kind="ExternalOutput").ap()

    xA = nc.dram_tensor("xA", [TPC, D], F32, kind="Internal").ap()
    xB = nc.dram_tensor("xB", [TPC, D], F32, kind="Internal").ap()
    xq_d = nc.dram_tensor("xq_d", [TPC, D], BF16, kind="Internal").ap()
    qn_d = nc.dram_tensor("qn_d", [N, ES], BF16, kind="Internal").ap()
    kn_d = nc.dram_tensor("kn_d", [N, ES], BF16, kind="Internal").ap()
    wqkv_d = nc.dram_tensor("wqkv_d", [D, 3 * ES], BF16, kind="Internal").ap()

    ws_i = nc.dram_tensor("ws_i", [1, 8], F32, kind="Internal").ap()
    ws_o = nc.dram_tensor("ws_o", [1, 8], F32, kind="Internal").ap()
    wint_i = nc.dram_tensor("wint_i", [ES, D], BF16, kind="Internal").ap()
    wintT_i = nc.dram_tensor("wintT_i", [D, ES], BF16, kind="Internal").ap()
    wintT_o = nc.dram_tensor("wintT_o", [NC, D, ES], BF16,
                             kind="Internal").ap()
    wom_d = nc.dram_tensor("wom_d", [ES, D], BF16, kind="Internal").ap()
    h2_i = [nc.dram_tensor(f"h2_i{j}", [512, D], BF16, kind="Internal").ap()
            for j in range(2)]
    h2_o = [nc.dram_tensor(f"h2_o{j}", [NC, 512, D], BF16, kind="Internal").ap()
            for j in range(2)]
    at_d = nc.dram_tensor("at_d", [ES, B * N], BF16, kind="Internal").ap()
    rs_i = nc.dram_tensor("rs_i", [B * N, D], BF16, kind="Internal").ap()
    rs_o = nc.dram_tensor("rs_o", [TPC, D], BF16, kind="Internal").ap()

    from contextlib import ExitStack
    with ExitStack() as es:
        ARENA_KB = 176
        arena_t = es.enter_context(
            nc.sbuf_tensor("arena", [128, ARENA_KB * 1024], mybir.dt.uint8))
        _ar = arena_t.ap()

        _pers = [0]

        def P(dt, cols):
            sz = cols * {F32: 4, BF16: 2, I32: 4}[dt]
            off = (_pers[0] + 63) // 64 * 64
            _pers[0] = off + sz
            assert _pers[0] <= 28 * 1024
            return _ar[:, off:off + sz].bitcast(dt)

        def R(off_kb, dt, cols):
            off = int(off_kb * 1024)
            sz = cols * {F32: 4, BF16: 2, I32: 4}[dt]
            assert 28 * 1024 <= off and off + sz <= ARENA_KB * 1024
            return _ar[:, off:off + sz].bitcast(dt)

        bo_bc = P(BF16, D)
        bf_bc = P(BF16, D)
        bq_bc = P(F32, ES)
        bk_bc = P(F32, ES)
        bv_bc = P(F32, ES)
        gq_bc = P(F32, HD)
        gk_bc = P(F32, HD)
        ones_bf = P(BF16, 32)
        onesr_f = P(F32, 128)
        onescol = P(F32, 16)
        rx1 = P(F32, 16)
        rx2 = P(F32, 16)
        wsrow = P(F32, 16)
        ws_bc = P(F32, 16)
        mwc_bc = P(F32, 16)
        lrow = P(F32, 512)
        st1 = P(F32, 16)
        st2 = P(F32, 16)
        st3 = P(F32, 16)
        stp = [P(F32, 32), P(F32, 32)]
        stw_t = P(F32, 256)
        st3b = [P(F32, 16), P(F32, 16)]


        xt = [R(28, F32, D), R(44, F32, D)]
        lnt = R(60, F32, D)
        scr = R(76, F32, D)
        xa = [R(92, F32, D), R(108, F32, D)]
        tq = R(124, F32, D)
        i32t = R(140, I32, D)
        xqbf = [R(156, BF16, D), R(164, BF16, D)]
        h2st = [R(156, BF16, D), R(164, BF16, D)]
        xqT = R(28, BF16, 32 * 1024)
        at_sb = R(28, BF16, 32 * 1024)
        wring = [R(92 + i, BF16, 512) for i in range(8)]
        yt_t = [R(100, F32, 512), R(102, F32, 512)]
        g_t = [R(104, F32, 512), R(106, F32, 512)]
        xo = [R(108, F32, 512), R(110, F32, 512)]
        xr = [R(112 + 2 * i, F32, 512) for i in range(4)]
        wq_sb = R(28, BF16, 32 * ES)
        wk_sb = R(60, BF16, 32 * ES)
        wv_sb = R(92, BF16, 32 * ES)
        V_sb = R(124, BF16, 32 * ES)
        hring = [R(156 + i, BF16, 512) for i in range(8)]
        qtmp = R(164, F32, 512)
        sqq = R(166, F32, 512)
        qnst = [R(168, BF16, 512), R(169, BF16, 512)]
        atst = [R(170, BF16, 512), R(171, BF16, 512)]
        rbc_sb = R(172, F32, 512)
        qT = R(28, BF16, HPC * N)
        kT = R(60, BF16, HPC * N)
        Eb = [R(92, BF16, 16 * 512), R(108, BF16, 16 * 512)]
        at4 = [R(28 + 16 * e, BF16, B * N) for e in range(4)]
        wo_sb = R(124, BF16, HPC * D)
        stg6 = [R(92, BF16, D), R(100, BF16, D)]
        rsb_t = R(156, BF16, 2048)
        xbt_t = R(160, F32, 2048)
        x3st = R(168, F32, 2048)

        ps2a = es.enter_context(nc.psum_tensor("ps2a", [128, 1024], F32))
        ps2b = es.enter_context(nc.psum_tensor("ps2b", [128, 1024], F32))
        pav = es.enter_context(nc.psum_tensor("pav", [128, 512], F32))
        pl = es.enter_context(nc.psum_tensor("pl", [128, 512], F32))
        prb = es.enter_context(nc.psum_tensor("prb", [128, 512], F32))
        pex = es.enter_context(nc.psum_tensor("pex", [128, 512], F32))
        PS8 = [ps2a.ap()[:, 0:512], ps2a.ap()[:, 512:1024],
               ps2b.ap()[:, 0:512], ps2b.ap()[:, 512:1024],
               pav.ap()[:], pl.ap()[:], prb.ap()[:], pex.ap()[:]]

        S = {}
        for nm in ["prep", "x", "w", "t", "h2", "stx", "stq", "sth", "stw",
                   "cc", "pe", "vs", "sv", "vg", "sg", "sf", "out",
                   "vv", "sc", "wqkv"]:
            S[nm] = Sem(es.enter_context(nc.semaphore("s_" + nm)))
        WR = [Sem(es.enter_context(nc.semaphore(f"s_wr{i}"))) for i in range(8)]
        HR = [Sem(es.enter_context(nc.semaphore(f"s_hr{i}"))) for i in range(8)]

        prog = {e: [] for e in ["sync", "gpsimd", "vector", "scalar", "tensor"]}

        def em(eng, fn):
            prog[eng].append(fn)

        def wt(eng, sem, val):
            if val > 0:
                em(eng, lambda e, s=sem, v=val: e.wait_ge(s.h, v))

        def load(sem, dst, src, transpose=False, serial=True, queue="sync"):
            if serial and sem.n > 0:
                wt(queue, sem, sem.n)
            def fn(e, d=dst, r=src, t=transpose, s=sem):
                e.dma_start(out=d, in_=r, transpose=t).then_inc(s.h, 16)
            em(queue, fn)
            sem.n += 16
            return sem.n

        def store(sem, dst, src, serial=True, queue="gpsimd"):
            if serial and sem.n > 0:
                wt(queue, sem, sem.n)
            def fn(e, d=dst, r=src, s=sem):
                e.dma_start(out=d, in_=r).then_inc(s.h, 16)
            em(queue, fn)
            sem.n += 16
            return sem.n

        SELF = {"vector": "vv", "scalar": "sc"}

        def op(eng, name, *args, inc=None, bar=False, **kw):
            sem = inc if inc is not None else (S[SELF[eng]] if bar else None)
            def fn(e, name=name, args=args, kw=kw, sem=sem):
                ins = getattr(e, name)(*args, **kw)
                if sem is not None:
                    ins.then_inc(sem.h, 1)
            em(eng, fn)
            if sem is not None:
                sem.n += 1
                if bar:
                    wt(eng, sem, sem.n)
                return sem.n
            return None

        def mm(out, lhsT, rhs, start, stop, inc=None):
            return op("tensor", "matmul", out, lhsT, rhs,
                      start=start, stop=stop, inc=inc)

        def cc(kind, alu, i_, o_):
            def fn(e, k=kind, a=alu, i=i_, o=o_):
                e.collective_compute(k, a, replica_groups=RG, ins=[i],
                                     outs=[o]).then_inc(S["cc"].h, 1)
            em("gpsimd", fn)
            S["cc"].n += 1
            return S["cc"].n

        load(S["prep"], bq_bc[0:1, :], bq_in, serial=False)
        load(S["prep"], bk_bc[0:1, :], bk_in, serial=False)
        load(S["prep"], bv_bc[0:1, :], bv_in, serial=False)
        load(S["prep"], gq_bc[0:1, :], gq_in, serial=False)
        load(S["prep"], gk_bc[0:1, :], gk_in, serial=False)
        load(S["prep"], lnt[0:1, 0:D], bo_in, serial=False)
        n_all = load(S["prep"], scr[0:1, 0:D], bf_in, serial=False)
        n_bq = n_bk = n_bv = n_gq = n_gk = n_bo = n_bf = n_all

        em("vector", lambda e: e.memset(ones_bf[:, 0:1], 1.0))
        em("vector", lambda e: e.memset(onesr_f[0:1, :], 1.0))
        em("vector", lambda e: e.memset(onescol[:, 0:1], 1.0))

        wt("vector", S["prep"], n_gq)
        op("vector", "tensor_scalar", gq_bc[0:1, :], gq_bc[0:1, :], ALPHA,
           None, op0=ALU.mult, inc=S["vs"], bar=True)

        def bcast(row_ap, dst_ap, cols, dep_n):
            for j in range((cols + 511) // 512):
                c0, c1 = 512 * j, min(512 * (j + 1), cols)
                wt("tensor", S["prep"], dep_n)
                wt("tensor", S["vs"], S["vs"].n)
                pe_n = mm(prb.ap()[:, 0:c1 - c0], onesr_f[0:1, :],
                          row_ap[0:1, c0:c1], True, True, inc=S["pe"])
                wt("vector", S["pe"], pe_n)
                op("vector", "tensor_copy", dst_ap[:, c0:c1],
                   prb.ap()[:, 0:c1 - c0], inc=S["vs"])

        bcast(bq_bc, bq_bc, ES, n_bq)
        bcast(bk_bc, bk_bc, ES, n_bk)
        bcast(bv_bc, bv_bc, ES, n_bv)
        bcast(gq_bc, gq_bc, HD, n_gq)
        bcast(gk_bc, gk_bc, HD, n_gk)
        bcast(lnt, bo_bc, D, n_bo)
        bcast(scr, bf_bc, D, n_bf)

        p0_xt_g = [0, 0]
        for i in range(4):
            stg = xt[i % 2]
            wt("sync", S["vv"], p0_xt_g[i % 2])
            n_l = load(S["prep"], stg, wfs_in[128 * i:128 * (i + 1), :])
            wt("vector", S["prep"], n_l)
            p0_xt_g[i % 2] = op("vector", "tensor_reduce", st1[:, i:i + 1],
                                stg, AX.X, ALU.add,
                                apply_absolute_value=True, inc=S["vv"],
                                bar=True)
        v_n = op("vector", "tensor_reduce", st1[:, 4:5], st1[:, 0:4], AX.X,
                 ALU.add, inc=S["vs"], bar=True)
        wt("tensor", S["vs"], v_n)
        pe_n = mm(pex.ap()[0:1, 0:1], st1[:, 4:5], onescol[:, 0:1], True, True,
                  inc=S["pe"])
        wt("vector", S["pe"], pe_n)
        op("vector", "tensor_copy", wsrow[0:1, 0:1], pex.ap()[0:1, 0:1])
        vg_n = op("vector", "memset", wsrow[0:1, 1:8], 0.0, inc=S["vg"])
        wt("gpsimd", S["vg"], vg_n)
        st_n = store(S["stw"], ws_i, wsrow[0:1, 0:8])
        wt("gpsimd", S["stw"], st_n)
        cc_ws = cc("AllReduce", ALU.add, ws_i, ws_o)

        store(S["stq"], wqkv_d[:, 0:ES], wq_in)
        store(S["stq"], wqkv_d[:, ES:2 * ES], wk_in)
        st_wqkv = store(S["stq"], wqkv_d[:, 2 * ES:3 * ES], wv_in)
        st_wom = store(S["stw"], wom_d, wos_in)

        wt("sync", S["cc"], cc_ws)
        n_ws = load(S["x"], wsrow[0:1, 8:16], ws_o)
        wt("vector", S["x"], n_ws)
        op("vector", "tensor_scalar", wsrow[0:1, 0:1], wsrow[0:1, 8:9],
           MEAN_W, 1e-5, op0=ALU.mult, op1=ALU.max, bar=True)
        op("vector", "tensor_scalar", wsrow[0:1, 1:2], wsrow[0:1, 0:1],
           float(1.0 / 127.0), None, op0=ALU.mult, bar=True)
        v_n = op("vector", "reciprocal", wsrow[0:1, 2:3], wsrow[0:1, 0:1],
                 inc=S["vs"], bar=True)
        wt("tensor", S["vs"], v_n)
        mm(pex.ap()[:, 0:1], onesr_f[0:1, :], wsrow[0:1, 2:3], True, True)
        pe_n = mm(pex.ap()[:, 1:2], onesr_f[0:1, :], wsrow[0:1, 1:2], True,
                  True, inc=S["pe"])
        wt("vector", S["pe"], pe_n)
        op("vector", "tensor_copy", ws_bc[:, 0:1], pex.ap()[:, 0:1])
        WS_READY = op("vector", "tensor_copy", mwc_bc[:, 0:1], pex.ap()[:, 1:2],
                      inc=S["vs"])

        def ln_quant_phase(x_src, xa_dst, rx_sb, guards, chunks=None):
            cl = list(chunks) if chunks is not None else list(range(NCH))
            c0, ncl = cl[0], len(cl)
            C = lambda b: stw_t[:, 8 * b + c0:8 * b + c0 + ncl]
            Cc = lambda b, c: stw_t[:, 8 * b + c:8 * b + c + 1]
            p1v = [0] * 2
            for c in cl:
                sl = c % 2
                wt("sync", S["vv"], p1v[sl])
                if c == cl[0]:
                    for gs, gv in guards:
                        wt("sync", gs, gv)
                n_x = load(S["x"], xt[sl], x_src[128 * c:128 * (c + 1), :])
                wt("vector", S["x"], n_x)
                op("vector", "tensor_reduce", Cc(0, c), xt[sl], AX.X, ALU.add)
                op("vector", "tensor_reduce", Cc(2, c), xt[sl], AX.X, ALU.max)
                op("vector", "tensor_reduce", Cc(3, c), xt[sl], AX.X, ALU.min)
                op("vector", "tensor_mul", scr, xt[sl], xt[sl], bar=True)
                p1v[sl] = op("vector", "tensor_reduce", Cc(1, c), scr, AX.X,
                             ALU.add, inc=S["vv"], bar=True)
            op("vector", "tensor_scalar", C(4), C(0), INV_D, None,
               op0=ALU.mult, bar=True)
            op("vector", "tensor_mul", C(5), C(4), C(4), bar=True)
            op("vector", "scalar_tensor_tensor", C(6), C(1), INV_D, C(5),
               op0=ALU.mult, op1=ALU.subtract, bar=True)
            v_n = op("vector", "tensor_scalar", C(5), C(6), 1e-5, None,
                     op0=ALU.add, inc=S["vs"], bar=True)
            wt("scalar", S["vs"], v_n)
            s_n = op("scalar", "activation", C(15), C(5), AF.Sqrt,
                     inc=S["sv"])
            wt("vector", S["sv"], s_n)
            op("vector", "reciprocal", C(7), C(15), bar=True)
            op("vector", "tensor_scalar", C(8), C(7), 0.5, 1.0,
               op0=ALU.mult, op1=ALU.add, bar=True)
            op("vector", "tensor_mul", C(5), C(4), C(7), bar=True)
            op("vector", "tensor_scalar", C(9), C(5), -0.5, None,
               op0=ALU.mult, bar=True)
            op("vector", "tensor_mul", C(15), C(6), C(7), bar=True)
            op("vector", "tensor_mul", C(15), C(15), C(7), bar=True)
            v_n = op("vector", "tensor_scalar", C(15), C(15), 1e-5, None,
                     op0=ALU.add, inc=S["vs"], bar=True)
            wt("scalar", S["vs"], v_n)
            s_n = op("scalar", "activation", C(15), C(15), AF.Sqrt,
                     inc=S["sv"])
            wt("vector", S["sv"], s_n)
            op("vector", "reciprocal", C(10), C(15), bar=True)
            op("vector", "tensor_sub", C(2), C(2), C(4), bar=True)
            op("vector", "tensor_sub", C(3), C(4), C(3), bar=True)
            op("vector", "tensor_max", C(2), C(2), C(3), bar=True)
            op("vector", "tensor_mul", C(2), C(2), C(7), bar=True)
            op("vector", "tensor_mul", C(11), C(2), C(10), bar=True)
            op("vector", "tensor_scalar", C(11), C(11), 1e-5, None,
               op0=ALU.max, bar=True)
            op("vector", "reciprocal", C(12), C(11), bar=True)
            op("vector", "tensor_scalar", C(12), C(12), 127.0, None,
               op0=ALU.mult, bar=True)
            op("vector", "tensor_mul", C(12), C(10), C(12), bar=True)
            op("vector", "tensor_mul", C(13), C(7), C(12), bar=True)
            op("vector", "tensor_mul", C(14), C(5), C(12), bar=True)
            op("vector", "tensor_scalar", C(14), C(14), -1.0, None,
               op0=ALU.mult, bar=True)
            v_st = op("vector", "tensor_scalar", rx_sb[:, c0:c0 + ncl], C(11),
                      mwc_bc[:, 0:1], None, op0=ALU.mult, inc=S["vs"],
                      bar=True)
            xa_g = [0] * 2
            xq_g = [0] * 2
            tq_g = [0, 0]
            p3_sv = [0, 0]
            tq2 = [tq, scr]
            for j, c in enumerate(cl):
                sl = c % 2
                if j < 2:
                    wt("sync", S["vv"], p1v[sl])
                else:
                    wt("sync", S["sv"], p3_sv[sl])
                n_x = load(S["x"], xt[sl], x_src[128 * c:128 * (c + 1), :])
                wt("scalar", S["x"], n_x)
                if j == 0:
                    wt("scalar", S["vs"], v_st)
                wt("scalar", S["stx"], xa_g[sl])
                s_xa = op("scalar", "activation", xa[sl], xt[sl], AF.Identity,
                          bias=Cc(9, c), scale=Cc(8, c), inc=S["sg"])
                wt("gpsimd", S["sg"], s_xa)
                xa_g[sl] = store(S["stx"], xa_dst[128 * c:128 * (c + 1), :],
                                 xa[sl])
                wt("scalar", S["vv"], tq_g[sl])
                s_t = op("scalar", "activation", tq2[sl], xt[sl], AF.Identity,
                         bias=Cc(14, c), scale=Cc(13, c), inc=S["sv"])
                p3_sv[sl] = s_t
                wt("vector", S["sv"], s_t)
                tq_g[sl] = op("vector", "tensor_copy", i32t, tq2[sl],
                              inc=S["vv"], bar=True)
                wt("vector", S["stq"], xq_g[sl])
                vg_n = op("vector", "tensor_copy", xqbf[sl], i32t, inc=S["vg"])
                wt("gpsimd", S["vg"], vg_n)
                xq_g[sl] = store(S["stq"], xq_d[128 * c:128 * (c + 1), :],
                                 xqbf[sl])
            return None

        P0_PE = S["pe"].n
        wint_st = 0
        p2_xt_g = [p0_xt_g[0], p0_xt_g[1]]
        p2_tq_g = 0
        for i in range(4):
            sl = i % 2
            if i < 2:
                wt("sync", S["vv"], p0_xt_g[sl])
            else:
                wt("sync", S["sv"], p2_xt_g[sl])
            n_w = load(S["x"], xt[sl], wfs_in[128 * i:128 * (i + 1), :])
            wt("scalar", S["x"], n_w)
            if i == 0:
                wt("scalar", S["vs"], WS_READY)
            wt("scalar", S["vv"], p2_tq_g)
            s_n = op("scalar", "activation", tq, xt[sl], AF.Copy,
                     scale=ws_bc[:, 0:1], inc=S["sv"])
            p2_xt_g[sl] = s_n
            wt("vector", S["sv"], s_n)
            p2_tq_g = op("vector", "tensor_copy", i32t, tq, inc=S["vv"],
                         bar=True)
            wt("vector", S["stw"], wint_st)
            vg_n = op("vector", "tensor_scalar", xqbf[sl], i32t, 1.0, -1.0,
                      op0=ALU.min, op1=ALU.max, inc=S["vg"])
            wt("gpsimd", S["vg"], vg_n)
            wint_st = store(S["stw"], wint_i[128 * i:128 * (i + 1), :],
                            xqbf[sl])
        P2A_SV = S["sv"].n
        P2A_VV = S["vv"].n
        P2A_VG = S["vg"].n
        wtT = [R(60, BF16, ES), R(68, BF16, ES)]
        wt_stl = []
        for k in range(32):
            sl = k % 2
            wt("sync", S["stw"], wint_st if k < 2 else wt_stl[k - 2])
            n_l = load(WR[sl], wtT[sl], wint_i[0:ES, 128 * k:128 * (k + 1)],
                       transpose=True, serial=False)
            wt("gpsimd", WR[sl], n_l)
            wt_stl.append(store(S["stw"], wintT_i[128 * k:128 * (k + 1), :],
                                wtT[sl]))
        wt("gpsimd", S["stw"], wt_stl[-1])
        cc_wint = cc("AllGather", ALU.bypass, wintT_i, wintT_o)
        wintT_flat = wintT_o.rearrange("r p d -> (r p) d")
        P2_STW = S["stw"].n

        g_p1 = [(S["vs"], WS_READY), (S["sv"], P2A_SV), (S["vv"], P2A_VV),
                (S["vg"], P2A_VG), (S["stw"], wint_st), (S["pe"], P0_PE)]
        ln_quant_phase(x_in, xA, rx1, g_p1)
        P1_STX = S["stx"].n
        P1_STQ = S["stq"].n
        P1_SV = S["sv"].n
        P1_VG = S["vg"].n

        wring_g = [0] * 8
        widx = [0]
        xr_g = [0] * 4
        xo_g = {"stx": [0, 0], "out": [0, 0]}

        def ffn_like_phase(dst_sem_name, lhsT_sb, rhs_src_fn, rhs_transpose,
                           res_src, dst, epi_fn, guards_sync, guards_tensor,
                           per_k_guards=None, lhs_k_guards=None,
                           ts_list=(0, 1), og_list=None, guards_issue=()):
            st_sem = S[dst_sem_name]
            bank_g = [S["vs"].n] * 2
            for gs, gv in guards_sync:
                wt("sync", gs, gv)
            for gs, gv in guards_tensor:
                wt("tensor", gs, gv)
            ts0 = list(ts_list)[0]
            ogl = list(og_list) if og_list is not None else list(range(8))
            og0 = ogl[0]
            steps = [(ts, og, k) for ts in ts_list for og in ogl
                     for k in range(32)]
            w0 = widx[0]
            PE0 = S["pe"].n
            n_ws = [0] * len(steps)

            def issue_step(j):
                ts_, og_, k_ = steps[j]
                if j >= 8:
                    wt("scalar", S["pe"], PE0 + j - 7)
                n_ws[j] = load(WR[(w0 + j) % 8], wring[(w0 + j) % 8],
                               rhs_src_fn(og_, k_), serial=False,
                               queue="scalar")

            wt("scalar", S["pe"], PE0)
            for gs, gv in guards_tensor:
                wt("scalar", gs, gv)
            for gs, gv in guards_issue:
                wt("scalar", gs, gv)
            for j in range(min(8, len(steps))):
                issue_step(j)
            for j, (ts, og, k) in enumerate(steps):
                sl = (w0 + j) % 8
                bg = og % 2
                if k == 0:
                    wt("tensor", S["vs"], bank_g[bg])
                if lhs_k_guards is not None and ts == ts0 and og == og0:
                    wt("tensor", *lhs_k_guards[k])
                wt("tensor", WR[sl], n_ws[j])
                for tt in range(4):
                    t0 = 1024 * k + 512 * ts + 128 * tt
                    pe_group = mm(PS8[4 * bg + tt],
                                  lhsT_sb[:, t0:t0 + 128],
                                  wring[sl][:], k == 0, k == 31,
                                  inc=S["pe"] if tt == 3 else None)
                widx[0] += 1
                if j + 8 < len(steps):
                    issue_step(j + 8)
                if k == 31:
                    last_vs = 0
                    for tt in range(4):
                        tg = 4 * ts + tt
                        rows = slice(128 * tg, 128 * (tg + 1))
                        cols = slice(512 * og, 512 * (og + 1))
                        xsl = (2 * og + tt) % 4
                        wt("sync", S["vg"], xr_g[xsl])
                        n_xr = load(S["x"], xr[xsl], res_src[rows, cols])
                        last_vs = epi_fn(bg, tt, tg, rows, cols, xsl, n_xr,
                                         pe_group, st_sem, dst)
                    bank_g[bg] = last_vs

        def make_ffn_epi(rx_sb, st_sem_name):
            g_guard = [0, 0]

            def epi(bg, tt, tg, rows, cols, xsl, n_xr, pe_group, st_sem, dst):
                ysl = tt % 2
                wt("vector", S["pe"], pe_group)
                wt("vector", S["sv"], g_guard[ysl])
                s_n = op("vector", "scalar_tensor_tensor", yt_t[ysl],
                         PS8[4 * bg + tt], rx_sb[:, tg:tg + 1], bf_bc[:, cols],
                         op0=ALU.mult, op1=ALU.add, inc=S["vs"])
                wt("scalar", S["vs"], s_n)
                s_g = op("scalar", "activation", g_t[ysl], yt_t[ysl], AF.Gelu,
                         inc=S["sv"])
                g_guard[ysl] = s_g
                osl = tt % 2
                wt("vector", S["sv"], s_g)
                wt("vector", S["x"], n_xr)
                wt("vector", st_sem, xo_g[st_sem_name][osl])
                vg_n = op("vector", "scalar_tensor_tensor", xo[osl], g_t[ysl],
                          0.5, xr[xsl], op0=ALU.mult, op1=ALU.add, inc=S["vg"])
                xr_g[xsl] = vg_n
                wt("gpsimd", S["vg"], vg_n)
                xo_g[st_sem_name][osl] = store(st_sem, dst[rows, cols],
                                               xo[osl])
                return s_n
            return epi

        def ffn_phase(rx_sb, res_src, dst, st_sem_name, guards):
            for gs, gv in guards:
                wt("sync", gs, gv)
            xqt_k = []
            for k in range(32):
                xqt_k.append((HR[k % 4],
                              load(HR[k % 4], xqT[:, 1024 * k:1024 * (k + 1)],
                                   xq_d[0:TPC, 128 * k:128 * (k + 1)],
                                   transpose=True, serial=True)))
            ffn_like_phase(
                st_sem_name, xqT,
                lambda og, k: wintT_flat[4096 * og + 128 * k:
                                         4096 * og + 128 * (k + 1), :],
                False, res_src, dst, make_ffn_epi(rx_sb, st_sem_name),
                [], [(S["cc"], cc_wint)], lhs_k_guards=xqt_k,
                guards_issue=guards)

        if stage >= 3:
            ffn_phase(rx1, xA, xB, "stx",
                      [(S["stq"], P1_STQ), (S["stx"], P1_STX)])
        P3_PE = S["pe"].n
        P3_STX = S["stx"].n
        P3_SV = S["sv"].n
        P3_VG = S["vg"].n

        cc_h2 = [0, 0]
        P4_STH = 0
        if stage >= 5:
            h2_g = [0] * 2
            CB = lambda bk: stw_t[:, 8 * bk:8 * (bk + 1)]
            CBc = lambda bk, cc_: stw_t[:, 8 * bk + cc_:8 * bk + cc_ + 1]
            p4v = [0] * 2
            for c in range(NCH):
                sl = c % 2
                wt("sync", S["vv"], p4v[sl])
                if c == 0:
                    wt("sync", S["pe"], P3_PE)
                    wt("sync", S["stx"], P3_STX)
                    wt("sync", S["sv"], S["sv"].n)
                    wt("sync", S["vg"], S["vg"].n)
                n_x = load(S["x"], xt[sl], xB[128 * c:128 * (c + 1), :])
                wt("vector", S["x"], n_x)
                op("vector", "tensor_reduce", CBc(0, c), xt[sl], AX.X,
                   ALU.add)
                op("vector", "tensor_mul", scr, xt[sl], xt[sl], bar=True)
                p4v[sl] = op("vector", "tensor_reduce", CBc(1, c), scr, AX.X,
                             ALU.add, inc=S["vv"], bar=True)
            op("vector", "tensor_scalar", CB(4), CB(0), INV_D, None,
               op0=ALU.mult, bar=True)
            op("vector", "tensor_mul", CB(5), CB(4), CB(4), bar=True)
            op("vector", "scalar_tensor_tensor", CB(6), CB(1), INV_D, CB(5),
               op0=ALU.mult, op1=ALU.subtract, bar=True)
            v_n = op("vector", "tensor_scalar", CB(5), CB(6), 1e-5, None,
                     op0=ALU.add, inc=S["vs"], bar=True)
            wt("scalar", S["vs"], v_n)
            s_n = op("scalar", "activation", CB(3), CB(5), AF.Sqrt,
                     inc=S["sv"])
            wt("vector", S["sv"], s_n)
            op("vector", "reciprocal", CB(7), CB(3), bar=True)
            v_st4 = op("vector", "scalar_tensor_tensor", CB(9), CB(4), -1.0,
                       CB(7), op0=ALU.mult, op1=ALU.mult, inc=S["vs"],
                       bar=True)
            p4_sg = [0, 0]
            for c in range(NCH):
                sl = c % 2
                if c < 2:
                    wt("sync", S["vv"], p4v[sl])
                else:
                    wt("sync", S["sg"], p4_sg[sl])
                n_x = load(S["x"], xt[sl], xB[128 * c:128 * (c + 1), :])
                wt("scalar", S["x"], n_x)
                if c == 0:
                    wt("scalar", S["vs"], v_st4)
                wt("scalar", S["sth"], h2_g[sl])
                sg_n = op("scalar", "activation", h2st[sl], xt[sl],
                          AF.Identity, bias=CBc(9, c), scale=CBc(7, c),
                          inc=S["sg"])
                p4_sg[sl] = sg_n
                wt("gpsimd", S["sg"], sg_n)
                h2_g[sl] = store(
                    S["sth"], h2_i[c // 4][128 * (c % 4):128 * (c % 4) + 128, :],
                    h2st[sl])
                if c == 3:
                    wt("gpsimd", S["sth"], S["sth"].n)
                    cc_h2[0] = cc("AllGather", ALU.bypass, h2_i[0], h2_o[0])
                if c == 7:
                    wt("gpsimd", S["sth"], S["sth"].n)
                    cc_h2[1] = cc("AllGather", ALU.bypass, h2_i[1], h2_o[1])
            P4_STH = S["sth"].n

        hring_g = [0] * 8
        qn_g = [0] * 2
        at_g = [0] * 2
        cc_at = [0, 0]
        if stage >= 5:
            Ehalf_g = [0, 0]
            pair_g = [0, 0]
            pl_g = 0
            pavrb_g = 0
            rank1_g = 0
            for b in range(1 if dump == 24 else B):
                wt("sync", S["pe"], S["pe"].n)
                wt("sync", S["sf"], S["sf"].n)
                wt("sync", S["sv"], S["sv"].n)
                wt("sync", S["vg"], S["vg"].n)
                wt("sync", S["stq"], st_wqkv if b == 0 else S["stq"].n)
                if b == 0 and stage >= 6:
                    wt("sync", S["sth"], P4_STH)
                n_wqkv = 0
                for (w_sb, off) in ((wq_sb, 0), (wk_sb, ES), (wv_sb, 2 * ES)):
                    for k in range(32):
                        n_wqkv = load(S["wqkv"], w_sb[:, ES * k:ES * (k + 1)],
                                      wqkv_d[128 * k:128 * (k + 1),
                                             off:off + ES], serial=False)
                wt("tensor", S["wqkv"], n_wqkv)
                wt("sync", S["cc"], cc_h2[b])
                if b == 0:
                    wt("tensor", S["cc"], cc_h2[1])

                qkv_bank_g = S["vs"].n
                v_bank_g = S["vs"].n
                qkv_sq_g = S["sv"].n
                for tc in range(8):
                    r = tc
                    Xh = b
                    wt("tensor", S["vs"], qkv_bank_g)
                    pe_qk = 0
                    for k in range(32):
                        sl = k % 8
                        wt("sync", S["pe"], hring_g[sl])
                        n_h = load(HR[sl], hring[sl],
                                   h2_o[Xh][r][0:512, 128 * k:128 * (k + 1)],
                                   transpose=True, serial=False)
                        wt("tensor", HR[sl], n_h)
                        for tt in range(4):
                            mm(PS8[tt], hring[sl][:, 128 * tt:128 * (tt + 1)],
                               wq_sb[:, ES * k:ES * (k + 1)], k == 0, k == 31)
                        for tt in range(4):
                            pe_qk = mm(PS8[4 + tt],
                                       hring[sl][:, 128 * tt:128 * (tt + 1)],
                                       wk_sb[:, ES * k:ES * (k + 1)],
                                       k == 0, k == 31,
                                       inc=S["pe"] if tt == 3 else None)
                        hring_g[sl] = pe_qk
                    vs_qk = 0
                    vs_q4 = 0
                    sq_guard = qkv_sq_g
                    for qk in range(2):
                        bias_bc = bq_bc if qk == 0 else bk_bc
                        g_bc = gq_bc if qk == 0 else gk_bc
                        dstd = qn_d if qk == 0 else kn_d
                        for tt in range(4):
                            bank = PS8[4 * qk + tt]
                            wt("vector", S["pe"], pe_qk)
                            wt("vector", S["sv"], sq_guard)
                            s_n = op("vector", "scalar_tensor_tensor", qtmp,
                                     bank, 1.0, bias_bc, op0=ALU.mult,
                                     op1=ALU.add, inc=S["vs"], bar=True)
                            vs_qk = s_n
                            wt("scalar", S["vs"], s_n)
                            for hh in range(4):
                                cseg = slice(128 * hh, 128 * (hh + 1))
                                sq_guard = op(
                                    "scalar", "activation", sqq[:, cseg],
                                    qtmp[:, cseg], AF.Square,
                                    accum_out=st3[:, hh:hh + 1],
                                    inc=S["sv"] if hh == 3 else None)
                            wt("vector", S["sv"], sq_guard)
                            v_n = op("vector", "tensor_scalar", st3[:, 0:4],
                                     st3[:, 0:4], float(1.0 / HD), 1e-6,
                                     op0=ALU.mult, op1=ALU.add, inc=S["vs"],
                                     bar=True)
                            wt("scalar", S["vs"], v_n)
                            s_n = op("scalar", "activation", st3[:, 4:8],
                                     st3[:, 0:4], AF.Sqrt, inc=S["sv"])
                            wt("vector", S["sv"], s_n)
                            op("vector", "reciprocal", st3[:, 8:12],
                               st3[:, 4:8], bar=True)
                            qsl = (2 * qk + tt) % 2
                            wt("vector", S["stq"], qn_g[qsl])
                            vg_n = 0
                            for hh in range(4):
                                cseg = slice(128 * hh, 128 * (hh + 1))
                                vg_n = op("vector", "scalar_tensor_tensor",
                                          qnst[qsl][:, cseg], qtmp[:, cseg],
                                          st3[:, 8 + hh:9 + hh], g_bc,
                                          op0=ALU.mult, op1=ALU.mult,
                                          inc=S["vg"] if hh == 3 else None)
                            wt("gpsimd", S["vg"], vg_n)
                            trow = 512 * tc + 128 * tt
                            qn_g[qsl] = store(S["stq"],
                                              dstd[trow:trow + 128, :],
                                              qnst[qsl])
                        if qk == 0:
                            vs_q4 = vs_qk
                    qkv_sq_g = sq_guard
                    wt("tensor", S["vs"], vs_q4)
                    pe_v = 0
                    for k in range(32):
                        sl = k % 8
                        wt("sync", S["pe"], hring_g[sl])
                        n_h = load(HR[sl], hring[sl],
                                   h2_o[Xh][r][0:512, 128 * k:128 * (k + 1)],
                                   transpose=True, serial=False)
                        wt("tensor", HR[sl], n_h)
                        for tt in range(4):
                            pe_v = mm(PS8[tt],
                                      hring[sl][:, 128 * tt:128 * (tt + 1)],
                                      wv_sb[:, ES * k:ES * (k + 1)],
                                      k == 0, k == 31,
                                      inc=S["pe"] if tt == 3 else None)
                        hring_g[sl] = pe_v
                    for tt in range(4):
                        wt("vector", S["pe"], pe_v)
                        cseg = slice(512 * (4 * tc + tt),
                                     512 * (4 * tc + tt + 1))
                        qkv_bank_g = op("vector", "scalar_tensor_tensor",
                                        V_sb[:, cseg], PS8[tt], 1.0, bv_bc,
                                        op0=ALU.mult, op1=ALU.add, inc=S["vs"])
                QKV_STQ = S["stq"].n
                QKV_PE = S["pe"].n
                QKV_VS = S["vs"].n

                wt("sync", S["stq"], QKV_STQ)
                wt("sync", S["pe"], QKV_PE)
                n_qt = 0
                for (dsrc, dsb) in ((qn_d, qT), (kn_d, kT)):
                    for h in range(HPC):
                        n_qt = load(S["t"], dsb[:, N * h:N * (h + 1)],
                                    dsrc[0:N, 128 * h:128 * (h + 1)],
                                    transpose=True, serial=False)
                wt("tensor", S["t"], n_qt)
                wt("tensor", S["vs"], QKV_VS)

                for h in range(HPC):
                    for qc in range(8):
                        qcols = slice(N * h + 512 * qc,
                                      N * h + 512 * (qc + 1))
                        av_n = 0
                        for half in range(2):
                            exp_n = 0
                            for g_ in range(8):
                                pr = g_ % 2
                                ps_pair = ps2a if pr == 0 else ps2b
                                wt("tensor", S["sf"], pair_g[pr])
                                pe_n = 0
                                for j in range(2):
                                    kt = 16 * half + 2 * g_ + j
                                    kcol = slice(N * h + 128 * kt,
                                                 N * h + 128 * (kt + 1))
                                    pe_n = mm(
                                        ps_pair.ap()[:, 512 * j:512 * (j + 1)],
                                        kT[:, kcol], qT[:, qcols], True, True,
                                        inc=S["pe"] if j == 1 else None)
                                wt("scalar", S["pe"], pe_n)
                                wt("scalar", S["pe"], Ehalf_g[half])
                                ecol = slice(1024 * g_, 1024 * (g_ + 1))
                                exp_n = op("scalar", "activation",
                                           Eb[half][:, ecol], ps_pair.ap()[:],
                                           AF.Exp, inc=S["sf"])
                                pair_g[pr] = exp_n
                            wt("tensor", S["sf"], exp_n)
                            if half == 0:
                                wt("tensor", S["vg"], pavrb_g)
                                wt("tensor", S["vs"], pl_g)
                            for kt16 in range(16):
                                kt = 16 * half + kt16
                                erhs = Eb[half][:, 512 * kt16:512 * (kt16 + 1)]
                                vcol = slice(512 * kt + 128 * h,
                                             512 * kt + 128 * (h + 1))
                                mm(pav.ap()[:], V_sb[:, vcol], erhs,
                                   kt == 0, kt == 31)
                                av_n = mm(pl.ap()[0:1, :], ones_bf[:, 0:1],
                                          erhs, kt == 0, kt == 31,
                                          inc=S["pe"] if kt16 == 15 else None)
                            Ehalf_g[half] = av_n
                        wt("vector", S["pe"], av_n)
                        wt("vector", S["pe"], rank1_g)
                        pl_g = op("vector", "reciprocal", lrow[0:1, :],
                                  pl.ap()[0:1, :], inc=S["vs"])
                        wt("tensor", S["vs"], pl_g)
                        rank1_g = mm(prb.ap()[:], onesr_f[0:1, :],
                                     lrow[0:1, :], True, True, inc=S["pe"])
                        asl = qc % 2
                        wt("vector", S["pe"], rank1_g)
                        op("vector", "tensor_copy", rbc_sb, prb.ap()[:],
                           bar=True)
                        wt("vector", S["sth"], at_g[asl])
                        vg_n = op("vector", "tensor_mul", atst[asl],
                                  pav.ap()[:], rbc_sb, inc=S["vg"])
                        pavrb_g = vg_n
                        wt("gpsimd", S["vg"], vg_n)
                        gcol = 4096 * b + 512 * qc
                        at_g[asl] = store(
                            S["sth"],
                            at_d[128 * h:128 * (h + 1), gcol:gcol + 512],
                            atst[asl])

        cc_rs = [0, 0]
        if stage >= 6:
            wt("sync", S["sth"], S["sth"].n)
            wt("sync", S["pe"], S["pe"].n)
            wt("sync", S["sf"], S["sf"].n)
            wt("sync", S["vg"], S["vg"].n)
            wt("sync", S["stw"], st_wom)
            n_at = 0
            for e in range(4):
                n_at = load(S["t"], at4[e], at_d[128 * e:128 * (e + 1), :],
                            serial=False)
            n_wo = 0
            for e in range(4):
                n_wo = load(S["wqkv"], wo_sb[:, D * e:D * (e + 1)],
                            wom_d[128 * e:128 * (e + 1), :], serial=False)
            wt("tensor", S["t"], n_at)
            wt("tensor", S["wqkv"], n_wo)
            wt("tensor", S["vs"], S["vs"].n)
            rnd_g = [S["vs"].n] * 2
            stg_g = [0, 0]
            rs_st = 0
            for T in range(64):
                ssl = T % 2
                vs_cp = 0
                for rnd in range(2):
                    gr = (2 * T + rnd) % 2
                    wt("tensor", S["vs"], rnd_g[gr])
                    pe_n = 0
                    for e in range(4):
                        for oc in range(4):
                            o = 4 * rnd + oc
                            pe_n = mm(PS8[4 * gr + oc],
                                      at4[e][:, 128 * T:128 * (T + 1)],
                                      wo_sb[:, D * e + 512 * o:D * e + 512 * (o + 1)],
                                      e == 0, e == 3,
                                      inc=S["pe"] if oc == 3 else None)
                    for oc in range(4):
                        o = 4 * rnd + oc
                        wt("vector", S["pe"], pe_n)
                        if rnd == 0 and oc == 0:
                            wt("vector", S["stx"], stg_g[ssl])
                        vs_cp = op("vector", "tensor_copy",
                                   stg6[ssl][:, 512 * o:512 * (o + 1)],
                                   PS8[4 * gr + oc], inc=S["vs"])
                    rnd_g[gr] = vs_cp
                wt("gpsimd", S["vs"], vs_cp)
                stg_g[ssl] = store(S["stx"], rs_i[128 * T:128 * (T + 1), :],
                                   stg6[ssl])
                if T == 31:
                    wt("gpsimd", S["stx"], S["stx"].n)
                    cc_rs[0] = cc("ReduceScatter", ALU.add,
                                  rs_i[0:4096, :], rs_o[0:512, :])
            P6A_STX_RS = S["stx"].n
            P6A_PE = S["pe"].n

            wint8_rhs = lambda og, k: wintT_flat[4096 * og + 128 * k:
                                                 4096 * og + 128 * (k + 1), :]
            wt("sync", S["cc"], cc_rs[0])
            x3_st_g = 0
            for j in range(8):
                c2, hf = divmod(j, 2)
                rows = slice(128 * c2, 128 * (c2 + 1))
                cols = slice(2048 * hf, 2048 * (hf + 1))
                wt("sync", S["vg"], S["vg"].n)
                n_r = load(S["x"], rsb_t, rs_o[rows, cols])
                n_x2 = load(S["x"], xbt_t, xB[rows, cols])
                wt("vector", S["x"], n_x2)
                wt("vector", S["stx"], x3_st_g)
                op("vector", "scalar_tensor_tensor", x3st, rsb_t, 1.0,
                   xbt_t, op0=ALU.mult, op1=ALU.add, bar=True)
                vg_n = op("vector", "tensor_add", x3st, x3st,
                          bo_bc[:, cols], inc=S["vg"])
                wt("gpsimd", S["vg"], vg_n)
                x3_st_g = store(S["stx"], xA[rows, cols], x3st)
            wt("gpsimd", S["stx"], P6A_STX_RS)
            wt("gpsimd", S["x"], S["x"].n)
            cc_rs[1] = cc("ReduceScatter", ALU.add,
                          rs_i[4096:8192, :], rs_o[512:1024, :])
            P6B0_STX = S["stx"].n
            if stage >= 8:
                g_p7a = [(S["stx"], P6B0_STX), (S["pe"], P6A_PE),
                         (S["sv"], S["sv"].n), (S["vg"], S["vg"].n),
                         (S["stq"], S["stq"].n), (S["stw"], S["stw"].n),
                         (S["sth"], S["sth"].n), (S["sf"], S["sf"].n)]
                ln_quant_phase(xA, xB, rx2, g_p7a, chunks=range(0, 4))
                epi8 = make_ffn_epi(rx2, "out")
                g_p8a = [(S["stq"], S["stq"].n), (S["stx"], S["stx"].n)]
                for gs, gv in g_p8a:
                    wt("sync", gs, gv)
                xqt_a = []
                for k in range(32):
                    xqt_a.append((HR[k % 4],
                                  load(HR[k % 4],
                                       xqT[:, 1024 * k:1024 * k + 512],
                                       xq_d[0:512, 128 * k:128 * (k + 1)],
                                       transpose=True, serial=True)))
                ffn_like_phase("out", xqT, wint8_rhs, False, xB, out_t,
                               epi8, [], [], lhs_k_guards=xqt_a,
                               ts_list=(0,), guards_issue=g_p8a)
            wt("sync", S["cc"], cc_rs[1])
            wt("sync", S["stq"], S["stq"].n)
            x3_st_g1 = 0
            for c2 in range(4):
                rows = slice(512 + 128 * c2, 512 + 128 * (c2 + 1))
                for hf in range(2):
                    cols = slice(2048 * hf, 2048 * (hf + 1))
                    wt("sync", S["vg"], S["vg"].n)
                    n_r = load(S["x"], rsb_t, rs_o[rows, cols])
                    n_x2 = load(S["x"], xbt_t, xB[rows, cols])
                    wt("vector", S["x"], n_x2)
                    wt("vector", S["stx"], x3_st_g1)
                    op("vector", "scalar_tensor_tensor", x3st, rsb_t, 1.0,
                       xbt_t, op0=ALU.mult, op1=ALU.add, bar=True)
                    vg_n = op("vector", "tensor_add", x3st, x3st,
                              bo_bc[:, cols], inc=S["vg"])
                    wt("gpsimd", S["vg"], vg_n)
                    x3_st_g1 = store(S["stx"], xA[rows, cols], x3st)
        P6_STX = S["stx"].n
        P6_PE = S["pe"].n

        if stage >= 8:
            g_p7b = [(S["stx"], P6_STX), (S["pe"], P6_PE),
                     (S["sv"], S["sv"].n), (S["vg"], S["vg"].n),
                     (S["stq"], S["stq"].n), (S["stw"], S["stw"].n),
                     (S["sth"], S["sth"].n), (S["sf"], S["sf"].n)]
            ln_quant_phase(xA, xB, rx2, g_p7b, chunks=range(4, 8))
            g_p8b = [(S["stq"], S["stq"].n), (S["stx"], S["stx"].n)]
            for gs, gv in g_p8b:
                wt("sync", gs, gv)
            xqt_b = []
            for k in range(32):
                xqt_b.append((HR[k % 4],
                              load(HR[k % 4],
                                   xqT[:, 1024 * k + 512:1024 * (k + 1)],
                                   xq_d[512:1024, 128 * k:128 * (k + 1)],
                                   transpose=True, serial=True)))
            ffn_like_phase("out", xqT, wint8_rhs, False, xB, out_t, epi8,
                           [], [], lhs_k_guards=xqt_b, ts_list=(1,),
                           guards_issue=g_p8b)

        if dump:
            for nm in ["stx", "stq", "sth", "stw", "cc", "vg", "sg"]:
                wt("gpsimd", S[nm], S[nm].n)
            if dump == 21:
                store(S["out"], out_t[:, 0:D // 2].bitcast(BF16), xq_d)
            elif dump == 22:
                store(S["out"], out_t[0:512, 0:D // 2].bitcast(BF16), wint_i)
            elif dump == 23:
                store(S["out"], out_t[0:512, 0:D // 2].bitcast(BF16)
                      .rearrange("p (a b) -> p a b", a=2)
                      .rearrange("p a b -> (p a) b") if False else
                      out_t.bitcast(BF16)[0:512, 0:B * N], at_d)
            elif dump == 24:
                store(S["out"], out_t.bitcast(BF16)[0:1024, 0:ES],
                      qn_d[1280:2304])
        elif stage < 8:
            src = {1: xA, 3: xB, 6: xA}[stage]
            for nm in ["stx", "stq", "sth", "stw", "cc", "vg", "sg"]:
                wt("gpsimd", S[nm], S[nm].n)
            store(S["out"], out_t, src)

        wt("sync", S["out"], S["out"].n)
        wt("gpsimd", S["out"], S["out"].n)

        block = es.enter_context(nc.Block())
        @block.sync
        def _(e):
            for fn in prog["sync"]:
                fn(e)

        @block.gpsimd
        def _(e):
            for fn in prog["gpsimd"]:
                fn(e)

        @block.vector
        def _(e):
            for fn in prog["vector"]:
                fn(e)

        @block.scalar
        def _(e):
            for fn in prog["scalar"]:
                fn(e)

        @block.tensor
        def _(e):
            for fn in prog["tensor"]:
                fn(e)

    return nc


def kernel(L0_attn_hidden_states, L0_attn_Wq, L0_attn_bq, L0_attn_Wk,
           L0_attn_bk, L0_attn_Wv, L0_attn_bv, L0_attn_Wo, L0_attn_bo,
           L0_attn_gq, L0_attn_gk, L0_ffn_W, L0_ffn_b):
    x = np.ascontiguousarray(
        np.asarray(L0_attn_hidden_states, dtype=np.float32).reshape(B * N, D))
    Wq = np.asarray(L0_attn_Wq, np.float32)
    Wk = np.asarray(L0_attn_Wk, np.float32)
    Wv = np.asarray(L0_attn_Wv, np.float32)
    Wo = np.asarray(L0_attn_Wo, np.float32)
    Wf = np.asarray(L0_ffn_W, np.float32)
    bq = np.asarray(L0_attn_bq, np.float32).reshape(1, D)
    bk = np.asarray(L0_attn_bk, np.float32).reshape(1, D)
    bv = np.asarray(L0_attn_bv, np.float32).reshape(1, D)
    bo = np.asarray(L0_attn_bo, np.float32).reshape(1, D)
    bf_ = np.asarray(L0_ffn_b, np.float32).reshape(1, D)
    gq = np.asarray(L0_attn_gq, np.float32).reshape(1, HD)
    gk = np.asarray(L0_attn_gk, np.float32).reshape(1, HD)

    in_maps = []
    for c in range(NC):
        esl = slice(ES * c, ES * (c + 1))
        myrows = np.r_[512 * c:512 * (c + 1), N + 512 * c:N + 512 * (c + 1)]
        in_maps.append({
            "x_my": np.ascontiguousarray(x[myrows]),
            "wq_my": np.ascontiguousarray(Wq[:, esl]),
            "wk_my": np.ascontiguousarray(Wk[:, esl]),
            "wv_my": np.ascontiguousarray(Wv[:, esl]),
            "bq_my": np.ascontiguousarray(bq[:, esl]),
            "bk_my": np.ascontiguousarray(bk[:, esl]),
            "bv_my": np.ascontiguousarray(bv[:, esl]),
            "gq": gq, "gk": gk,
            "wfs_my": np.ascontiguousarray(Wf[esl, :]),
            "wos_my": np.ascontiguousarray(Wo[esl, :]),
            "bf": bf_, "bo": bo,
        })
    nc = build_nc(_STAGE)
    res = run_bass_kernel_spmd(nc, in_maps, core_ids=list(range(NC)))
    out = np.empty((B * N, D), np.float32)
    for c in range(NC):
        oc = res.results[c]["out"]
        out[512 * c:512 * (c + 1)] = oc[0:512]
        out[N + 512 * c:N + 512 * (c + 1)] = oc[512:1024]
    return out.reshape(B, N, D)


# version 14
# speedup vs baseline: 1.0230x; 1.0016x over previous
import numpy as np
import concourse.bass as bass
import concourse.mybir as mybir
from concourse.bass_utils import run_bass_kernel_spmd

F32 = mybir.dt.float32
BF16 = mybir.dt.bfloat16
I32 = mybir.dt.int32
AF = mybir.ActivationFunctionType
ALU = mybir.AluOpType
AX = mybir.AxisListType

D = 4096
B = 2
N = 4096
TPC = 1024
NC = 8
HD = 128
HPC = 4
ES = HPC * HD
NCH = TPC // 128
RG = [[0, 1, 2, 3, 4, 5, 6, 7]]
ALPHA = float(1.0 / np.sqrt(HD))
INV_D = float(1.0 / D)
MEAN_W = float(2.0 ** -24)

_STAGE = 8


class Sem:
    def __init__(self, h):
        self.h = h
        self.n = 0


def build_nc(stage):
    dump = stage if stage in (21, 22, 23, 24) else None
    if dump in (21, 22):
        stage = 2
    elif dump in (23, 24):
        stage = 5
    nc = bass.Bass("TRN2", target_bir_lowering=False, debug=False)

    x_in = nc.dram_tensor("x_my", [TPC, D], F32, kind="ExternalInput").ap()
    wq_in = nc.dram_tensor("wq_my", [D, ES], F32, kind="ExternalInput").ap()
    wk_in = nc.dram_tensor("wk_my", [D, ES], F32, kind="ExternalInput").ap()
    wv_in = nc.dram_tensor("wv_my", [D, ES], F32, kind="ExternalInput").ap()
    bq_in = nc.dram_tensor("bq_my", [1, ES], F32, kind="ExternalInput").ap()
    bk_in = nc.dram_tensor("bk_my", [1, ES], F32, kind="ExternalInput").ap()
    bv_in = nc.dram_tensor("bv_my", [1, ES], F32, kind="ExternalInput").ap()
    gq_in = nc.dram_tensor("gq", [1, HD], F32, kind="ExternalInput").ap()
    gk_in = nc.dram_tensor("gk", [1, HD], F32, kind="ExternalInput").ap()
    wfs_in = nc.dram_tensor("wfs_my", [ES, D], F32, kind="ExternalInput").ap()
    wos_in = nc.dram_tensor("wos_my", [ES, D], F32, kind="ExternalInput").ap()
    bf_in = nc.dram_tensor("bf", [1, D], F32, kind="ExternalInput").ap()
    bo_in = nc.dram_tensor("bo", [1, D], F32, kind="ExternalInput").ap()
    out_t = nc.dram_tensor("out", [TPC, D], F32, kind="ExternalOutput").ap()

    xA = nc.dram_tensor("xA", [TPC, D], F32, kind="Internal").ap()
    xB = nc.dram_tensor("xB", [TPC, D], F32, kind="Internal").ap()
    xq_d = nc.dram_tensor("xq_d", [TPC, D], BF16, kind="Internal").ap()
    qn_d = nc.dram_tensor("qn_d", [N, ES], BF16, kind="Internal").ap()
    kn_d = nc.dram_tensor("kn_d", [N, ES], BF16, kind="Internal").ap()
    wqkv_d = nc.dram_tensor("wqkv_d", [D, 3 * ES], BF16, kind="Internal").ap()

    ws_i = nc.dram_tensor("ws_i", [1, 8], F32, kind="Internal").ap()
    ws_o = nc.dram_tensor("ws_o", [1, 8], F32, kind="Internal").ap()
    wint_i = nc.dram_tensor("wint_i", [ES, D], BF16, kind="Internal").ap()
    wintT_i = nc.dram_tensor("wintT_i", [D, ES], BF16, kind="Internal").ap()
    wintT_o = nc.dram_tensor("wintT_o", [NC, D, ES], BF16,
                             kind="Internal").ap()
    wom_d = nc.dram_tensor("wom_d", [ES, D], BF16, kind="Internal").ap()
    h2_i = [nc.dram_tensor(f"h2_i{j}", [512, D], BF16, kind="Internal").ap()
            for j in range(2)]
    h2_o = [nc.dram_tensor(f"h2_o{j}", [NC, 512, D], BF16, kind="Internal").ap()
            for j in range(2)]
    at_d = nc.dram_tensor("at_d", [ES, B * N], BF16, kind="Internal").ap()
    rs_i = nc.dram_tensor("rs_i", [B * N, D], BF16, kind="Internal").ap()
    rs_o = nc.dram_tensor("rs_o", [TPC, D], BF16, kind="Internal").ap()

    from contextlib import ExitStack
    with ExitStack() as es:
        ARENA_KB = 176
        arena_t = es.enter_context(
            nc.sbuf_tensor("arena", [128, ARENA_KB * 1024], mybir.dt.uint8))
        _ar = arena_t.ap()

        _pers = [0]

        def P(dt, cols):
            sz = cols * {F32: 4, BF16: 2, I32: 4}[dt]
            off = (_pers[0] + 63) // 64 * 64
            _pers[0] = off + sz
            assert _pers[0] <= 28 * 1024
            return _ar[:, off:off + sz].bitcast(dt)

        def R(off_kb, dt, cols):
            off = int(off_kb * 1024)
            sz = cols * {F32: 4, BF16: 2, I32: 4}[dt]
            assert 28 * 1024 <= off and off + sz <= ARENA_KB * 1024
            return _ar[:, off:off + sz].bitcast(dt)

        bo_bc = P(BF16, D)
        bf_bc = P(BF16, D)
        bq_bc = P(F32, ES)
        bk_bc = P(F32, ES)
        bv_bc = P(F32, ES)
        gq_bc = P(F32, HD)
        gk_bc = P(F32, HD)
        ones_bf = P(BF16, 32)
        onesr_f = P(F32, 128)
        onescol = P(F32, 16)
        rx1 = P(F32, 16)
        rx2 = P(F32, 16)
        wsrow = P(F32, 16)
        ws_bc = P(F32, 16)
        mwc_bc = P(F32, 16)
        lrow = P(F32, 512)
        st1 = P(F32, 16)
        st2 = P(F32, 16)
        st3 = P(F32, 16)
        stp = [P(F32, 32), P(F32, 32)]
        stw_t = P(F32, 256)
        st3b = [P(F32, 16), P(F32, 16)]


        xt = [R(28, F32, D), R(44, F32, D)]
        lnt = R(60, F32, D)
        scr = R(76, F32, D)
        xa = [R(92, F32, D), R(108, F32, D)]
        tq = R(124, F32, D)
        i32t = R(140, I32, D)
        xqbf = [R(156, BF16, D), R(164, BF16, D)]
        h2st = [R(156, BF16, D), R(164, BF16, D)]
        xqT = R(28, BF16, 32 * 1024)
        at_sb = R(28, BF16, 32 * 1024)
        wring = [R(92 + i, BF16, 512) for i in range(8)]
        yt_t = [R(100, F32, 512), R(102, F32, 512)]
        g_t = [R(104, F32, 512), R(106, F32, 512)]
        xo = [R(108, F32, 512), R(110, F32, 512)]
        xr = [R(112 + 2 * i, F32, 512) for i in range(4)]
        wq_sb = R(28, BF16, 32 * ES)
        wk_sb = R(60, BF16, 32 * ES)
        wv_sb = R(92, BF16, 32 * ES)
        V_sb = R(124, BF16, 32 * ES)
        hring = [R(156 + i, BF16, 512) for i in range(8)]
        qtmp = R(164, F32, 512)
        sqq = R(166, F32, 512)
        qnst = [R(168, BF16, 512), R(169, BF16, 512)]
        atst = [R(170, BF16, 512), R(171, BF16, 512)]
        rbc_sb = R(172, F32, 512)
        qT = R(28, BF16, HPC * N)
        kT = R(60, BF16, HPC * N)
        Eb = [R(92, BF16, 16 * 512), R(108, BF16, 16 * 512)]
        at4 = [R(28 + 16 * e, BF16, B * N) for e in range(4)]
        wo_sb = R(124, BF16, HPC * D)
        stg6 = [R(92, BF16, D), R(100, BF16, D)]
        rsb_t = R(156, BF16, 2048)
        xbt_t = R(160, F32, 2048)
        x3st = R(168, F32, 2048)

        ps2a = es.enter_context(nc.psum_tensor("ps2a", [128, 1024], F32))
        ps2b = es.enter_context(nc.psum_tensor("ps2b", [128, 1024], F32))
        pav = es.enter_context(nc.psum_tensor("pav", [128, 512], F32))
        pl = es.enter_context(nc.psum_tensor("pl", [128, 512], F32))
        prb = es.enter_context(nc.psum_tensor("prb", [128, 512], F32))
        pex = es.enter_context(nc.psum_tensor("pex", [128, 512], F32))
        PS8 = [ps2a.ap()[:, 0:512], ps2a.ap()[:, 512:1024],
               ps2b.ap()[:, 0:512], ps2b.ap()[:, 512:1024],
               pav.ap()[:], pl.ap()[:], prb.ap()[:], pex.ap()[:]]

        S = {}
        for nm in ["prep", "x", "w", "t", "h2", "stx", "stq", "sth", "stw",
                   "cc", "pe", "vs", "sv", "vg", "sg", "sf", "out",
                   "vv", "sc", "wqkv"]:
            S[nm] = Sem(es.enter_context(nc.semaphore("s_" + nm)))
        WR = [Sem(es.enter_context(nc.semaphore(f"s_wr{i}"))) for i in range(8)]
        HR = [Sem(es.enter_context(nc.semaphore(f"s_hr{i}"))) for i in range(8)]

        prog = {e: [] for e in ["sync", "gpsimd", "vector", "scalar", "tensor"]}

        def em(eng, fn):
            prog[eng].append(fn)

        def wt(eng, sem, val):
            if val > 0:
                em(eng, lambda e, s=sem, v=val: e.wait_ge(s.h, v))

        def load(sem, dst, src, transpose=False, serial=True, queue="sync"):
            if serial and sem.n > 0:
                wt(queue, sem, sem.n)
            def fn(e, d=dst, r=src, t=transpose, s=sem):
                e.dma_start(out=d, in_=r, transpose=t).then_inc(s.h, 16)
            em(queue, fn)
            sem.n += 16
            return sem.n

        def store(sem, dst, src, serial=True, queue="gpsimd"):
            if serial and sem.n > 0:
                wt(queue, sem, sem.n)
            def fn(e, d=dst, r=src, s=sem):
                e.dma_start(out=d, in_=r).then_inc(s.h, 16)
            em(queue, fn)
            sem.n += 16
            return sem.n

        SELF = {"vector": "vv", "scalar": "sc"}

        def op(eng, name, *args, inc=None, bar=False, **kw):
            sem = inc if inc is not None else (S[SELF[eng]] if bar else None)
            def fn(e, name=name, args=args, kw=kw, sem=sem):
                ins = getattr(e, name)(*args, **kw)
                if sem is not None:
                    ins.then_inc(sem.h, 1)
            em(eng, fn)
            if sem is not None:
                sem.n += 1
                if bar:
                    wt(eng, sem, sem.n)
                return sem.n
            return None

        def mm(out, lhsT, rhs, start, stop, inc=None):
            return op("tensor", "matmul", out, lhsT, rhs,
                      start=start, stop=stop, inc=inc)

        def cc(kind, alu, i_, o_):
            def fn(e, k=kind, a=alu, i=i_, o=o_):
                e.collective_compute(k, a, replica_groups=RG, ins=[i],
                                     outs=[o]).then_inc(S["cc"].h, 1)
            em("gpsimd", fn)
            S["cc"].n += 1
            return S["cc"].n

        load(S["prep"], bq_bc[0:1, :], bq_in, serial=False)
        load(S["prep"], bk_bc[0:1, :], bk_in, serial=False)
        load(S["prep"], bv_bc[0:1, :], bv_in, serial=False)
        load(S["prep"], gq_bc[0:1, :], gq_in, serial=False)
        load(S["prep"], gk_bc[0:1, :], gk_in, serial=False)
        load(S["prep"], lnt[0:1, 0:D], bo_in, serial=False)
        n_all = load(S["prep"], scr[0:1, 0:D], bf_in, serial=False)
        n_bq = n_bk = n_bv = n_gq = n_gk = n_bo = n_bf = n_all

        em("vector", lambda e: e.memset(ones_bf[:, 0:1], 1.0))
        em("vector", lambda e: e.memset(onesr_f[0:1, :], 1.0))
        em("vector", lambda e: e.memset(onescol[:, 0:1], 1.0))

        wt("vector", S["prep"], n_gq)
        op("vector", "tensor_scalar", gq_bc[0:1, :], gq_bc[0:1, :], ALPHA,
           None, op0=ALU.mult, inc=S["vs"], bar=True)

        def bcast(row_ap, dst_ap, cols, dep_n):
            for j in range((cols + 511) // 512):
                c0, c1 = 512 * j, min(512 * (j + 1), cols)
                wt("tensor", S["prep"], dep_n)
                wt("tensor", S["vs"], S["vs"].n)
                pe_n = mm(prb.ap()[:, 0:c1 - c0], onesr_f[0:1, :],
                          row_ap[0:1, c0:c1], True, True, inc=S["pe"])
                wt("vector", S["pe"], pe_n)
                op("vector", "tensor_copy", dst_ap[:, c0:c1],
                   prb.ap()[:, 0:c1 - c0], inc=S["vs"])

        bcast(bq_bc, bq_bc, ES, n_bq)
        bcast(bk_bc, bk_bc, ES, n_bk)
        bcast(bv_bc, bv_bc, ES, n_bv)
        bcast(gq_bc, gq_bc, HD, n_gq)
        bcast(gk_bc, gk_bc, HD, n_gk)
        bcast(lnt, bo_bc, D, n_bo)
        bcast(scr, bf_bc, D, n_bf)

        p0_xt_g = [0, 0]
        for i in range(4):
            stg = xt[i % 2]
            wt("sync", S["vv"], p0_xt_g[i % 2])
            n_l = load(S["prep"], stg, wfs_in[128 * i:128 * (i + 1), :])
            wt("vector", S["prep"], n_l)
            p0_xt_g[i % 2] = op("vector", "tensor_reduce", st1[:, i:i + 1],
                                stg, AX.X, ALU.add,
                                apply_absolute_value=True, inc=S["vv"],
                                bar=True)
        v_n = op("vector", "tensor_reduce", st1[:, 4:5], st1[:, 0:4], AX.X,
                 ALU.add, inc=S["vs"], bar=True)
        wt("tensor", S["vs"], v_n)
        pe_n = mm(pex.ap()[0:1, 0:1], st1[:, 4:5], onescol[:, 0:1], True, True,
                  inc=S["pe"])
        wt("vector", S["pe"], pe_n)
        op("vector", "tensor_copy", wsrow[0:1, 0:1], pex.ap()[0:1, 0:1])
        vg_n = op("vector", "memset", wsrow[0:1, 1:8], 0.0, inc=S["vg"])
        wt("gpsimd", S["vg"], vg_n)
        st_n = store(S["stw"], ws_i, wsrow[0:1, 0:8])
        wt("gpsimd", S["stw"], st_n)
        cc_ws = cc("AllReduce", ALU.add, ws_i, ws_o)

        store(S["stq"], wqkv_d[:, 0:ES], wq_in)
        store(S["stq"], wqkv_d[:, ES:2 * ES], wk_in)
        st_wqkv = store(S["stq"], wqkv_d[:, 2 * ES:3 * ES], wv_in)
        st_wom = store(S["stw"], wom_d, wos_in)

        wt("sync", S["cc"], cc_ws)
        n_ws = load(S["x"], wsrow[0:1, 8:16], ws_o)
        wt("vector", S["x"], n_ws)
        op("vector", "tensor_scalar", wsrow[0:1, 0:1], wsrow[0:1, 8:9],
           MEAN_W, 1e-5, op0=ALU.mult, op1=ALU.max, bar=True)
        op("vector", "tensor_scalar", wsrow[0:1, 1:2], wsrow[0:1, 0:1],
           float(1.0 / 127.0), None, op0=ALU.mult, bar=True)
        v_n = op("vector", "reciprocal", wsrow[0:1, 2:3], wsrow[0:1, 0:1],
                 inc=S["vs"], bar=True)
        wt("tensor", S["vs"], v_n)
        mm(pex.ap()[:, 0:1], onesr_f[0:1, :], wsrow[0:1, 2:3], True, True)
        pe_n = mm(pex.ap()[:, 1:2], onesr_f[0:1, :], wsrow[0:1, 1:2], True,
                  True, inc=S["pe"])
        wt("vector", S["pe"], pe_n)
        op("vector", "tensor_copy", ws_bc[:, 0:1], pex.ap()[:, 0:1])
        WS_READY = op("vector", "tensor_copy", mwc_bc[:, 0:1], pex.ap()[:, 1:2],
                      inc=S["vs"])

        def ln_quant_phase(x_src, xa_dst, rx_sb, guards, chunks=None):
            cl = list(chunks) if chunks is not None else list(range(NCH))
            c0, ncl = cl[0], len(cl)
            resident = ncl <= 4
            if resident:
                xt_r = [R(28, F32, D), R(44, F32, D), R(60, F32, D),
                        R(76, F32, D)]
                scr_r = R(92, F32, D)
                xa_r = R(108, F32, D)
            C = lambda b: stw_t[:, 8 * b + c0:8 * b + c0 + ncl]
            Cc = lambda b, c: stw_t[:, 8 * b + c:8 * b + c + 1]
            p1v = [0] * 2
            for j, c in enumerate(cl):
                sl = c % 2
                xtile = xt_r[j] if resident else xt[sl]
                sq_t = scr_r if resident else scr
                if not resident:
                    wt("sync", S["vv"], p1v[sl])
                if c == cl[0]:
                    for gs, gv in guards:
                        wt("sync", gs, gv)
                n_x = load(S["x"], xtile, x_src[128 * c:128 * (c + 1), :])
                wt("vector", S["x"], n_x)
                op("vector", "tensor_reduce", Cc(0, c), xtile, AX.X, ALU.add)
                op("vector", "tensor_reduce", Cc(2, c), xtile, AX.X, ALU.max)
                op("vector", "tensor_reduce", Cc(3, c), xtile, AX.X, ALU.min)
                op("vector", "tensor_mul", sq_t, xtile, xtile, bar=True)
                p1v[sl] = op("vector", "tensor_reduce", Cc(1, c), sq_t, AX.X,
                             ALU.add, inc=S["vv"], bar=True)
            op("vector", "tensor_scalar", C(4), C(0), INV_D, None,
               op0=ALU.mult, bar=True)
            op("vector", "tensor_mul", C(5), C(4), C(4), bar=True)
            op("vector", "scalar_tensor_tensor", C(6), C(1), INV_D, C(5),
               op0=ALU.mult, op1=ALU.subtract, bar=True)
            v_n = op("vector", "tensor_scalar", C(5), C(6), 1e-5, None,
                     op0=ALU.add, inc=S["vs"], bar=True)
            wt("scalar", S["vs"], v_n)
            s_n = op("scalar", "activation", C(15), C(5), AF.Sqrt,
                     inc=S["sv"])
            wt("vector", S["sv"], s_n)
            op("vector", "reciprocal", C(7), C(15), bar=True)
            op("vector", "tensor_scalar", C(8), C(7), 0.5, 1.0,
               op0=ALU.mult, op1=ALU.add, bar=True)
            op("vector", "tensor_mul", C(5), C(4), C(7), bar=True)
            op("vector", "tensor_scalar", C(9), C(5), -0.5, None,
               op0=ALU.mult, bar=True)
            op("vector", "tensor_mul", C(15), C(6), C(7), bar=True)
            op("vector", "tensor_mul", C(15), C(15), C(7), bar=True)
            v_n = op("vector", "tensor_scalar", C(15), C(15), 1e-5, None,
                     op0=ALU.add, inc=S["vs"], bar=True)
            wt("scalar", S["vs"], v_n)
            s_n = op("scalar", "activation", C(15), C(15), AF.Sqrt,
                     inc=S["sv"])
            wt("vector", S["sv"], s_n)
            op("vector", "reciprocal", C(10), C(15), bar=True)
            op("vector", "tensor_sub", C(2), C(2), C(4), bar=True)
            op("vector", "tensor_sub", C(3), C(4), C(3), bar=True)
            op("vector", "tensor_max", C(2), C(2), C(3), bar=True)
            op("vector", "tensor_mul", C(2), C(2), C(7), bar=True)
            op("vector", "tensor_mul", C(11), C(2), C(10), bar=True)
            op("vector", "tensor_scalar", C(11), C(11), 1e-5, None,
               op0=ALU.max, bar=True)
            op("vector", "reciprocal", C(12), C(11), bar=True)
            op("vector", "tensor_scalar", C(12), C(12), 127.0, None,
               op0=ALU.mult, bar=True)
            op("vector", "tensor_mul", C(12), C(10), C(12), bar=True)
            op("vector", "tensor_mul", C(13), C(7), C(12), bar=True)
            op("vector", "tensor_mul", C(14), C(5), C(12), bar=True)
            op("vector", "tensor_scalar", C(14), C(14), -1.0, None,
               op0=ALU.mult, bar=True)
            v_st = op("vector", "tensor_scalar", rx_sb[:, c0:c0 + ncl], C(11),
                      mwc_bc[:, 0:1], None, op0=ALU.mult, inc=S["vs"],
                      bar=True)
            xa_g = [0] * 2
            xq_g = [0] * 2
            tq_g = [0, 0]
            p3_sv = [0, 0]
            tq2 = [tq, scr_r if resident else scr]
            for j, c in enumerate(cl):
                sl = c % 2
                if resident:
                    xtile = xt_r[j]
                    xa_t = xa_r
                else:
                    xtile = xt[sl]
                    xa_t = xa[sl]
                    if j < 2:
                        wt("sync", S["vv"], p1v[sl])
                    else:
                        wt("sync", S["sv"], p3_sv[sl])
                    n_x = load(S["x"], xtile, x_src[128 * c:128 * (c + 1), :])
                    wt("scalar", S["x"], n_x)
                if j == 0:
                    wt("scalar", S["vs"], v_st)
                wt("scalar", S["stx"], xa_g[0] if resident else xa_g[sl])
                s_xa = op("scalar", "activation", xa_t, xtile, AF.Identity,
                          bias=Cc(9, c), scale=Cc(8, c), inc=S["sg"])
                wt("gpsimd", S["sg"], s_xa)
                xa_g[0 if resident else sl] = store(
                    S["stx"], xa_dst[128 * c:128 * (c + 1), :], xa_t)
                wt("scalar", S["vv"], tq_g[sl])
                s_t = op("scalar", "activation", tq2[sl], xtile, AF.Identity,
                         bias=Cc(14, c), scale=Cc(13, c), inc=S["sv"])
                p3_sv[sl] = s_t
                wt("vector", S["sv"], s_t)
                tq_g[sl] = op("vector", "tensor_copy", i32t, tq2[sl],
                              inc=S["vv"], bar=True)
                wt("vector", S["stq"], xq_g[sl])
                vg_n = op("vector", "tensor_copy", xqbf[sl], i32t, inc=S["vg"])
                wt("gpsimd", S["vg"], vg_n)
                xq_g[sl] = store(S["stq"], xq_d[128 * c:128 * (c + 1), :],
                                 xqbf[sl])
            return None

        P0_PE = S["pe"].n
        wint_st = 0
        p2_xt_g = [p0_xt_g[0], p0_xt_g[1]]
        p2_tq_g = 0
        for i in range(4):
            sl = i % 2
            if i < 2:
                wt("sync", S["vv"], p0_xt_g[sl])
            else:
                wt("sync", S["sv"], p2_xt_g[sl])
            n_w = load(S["x"], xt[sl], wfs_in[128 * i:128 * (i + 1), :])
            wt("scalar", S["x"], n_w)
            if i == 0:
                wt("scalar", S["vs"], WS_READY)
            wt("scalar", S["vv"], p2_tq_g)
            s_n = op("scalar", "activation", tq, xt[sl], AF.Copy,
                     scale=ws_bc[:, 0:1], inc=S["sv"])
            p2_xt_g[sl] = s_n
            wt("vector", S["sv"], s_n)
            p2_tq_g = op("vector", "tensor_copy", i32t, tq, inc=S["vv"],
                         bar=True)
            wt("vector", S["stw"], wint_st)
            vg_n = op("vector", "tensor_scalar", xqbf[sl], i32t, 1.0, -1.0,
                      op0=ALU.min, op1=ALU.max, inc=S["vg"])
            wt("gpsimd", S["vg"], vg_n)
            wint_st = store(S["stw"], wint_i[128 * i:128 * (i + 1), :],
                            xqbf[sl])
        P2A_SV = S["sv"].n
        P2A_VV = S["vv"].n
        P2A_VG = S["vg"].n
        wtT = [R(60, BF16, ES), R(68, BF16, ES)]
        wt_stl = []
        for k in range(32):
            sl = k % 2
            wt("sync", S["stw"], wint_st if k < 2 else wt_stl[k - 2])
            n_l = load(WR[sl], wtT[sl], wint_i[0:ES, 128 * k:128 * (k + 1)],
                       transpose=True, serial=False)
            wt("gpsimd", WR[sl], n_l)
            wt_stl.append(store(S["stw"], wintT_i[128 * k:128 * (k + 1), :],
                                wtT[sl]))
        wt("gpsimd", S["stw"], wt_stl[-1])
        cc_wint = cc("AllGather", ALU.bypass, wintT_i, wintT_o)
        wintT_flat = wintT_o.rearrange("r p d -> (r p) d")
        P2_STW = S["stw"].n

        g_p1 = [(S["vs"], WS_READY), (S["sv"], P2A_SV), (S["vv"], P2A_VV),
                (S["vg"], P2A_VG), (S["stw"], wint_st), (S["pe"], P0_PE)]
        ln_quant_phase(x_in, xA, rx1, g_p1)
        P1_STX = S["stx"].n
        P1_STQ = S["stq"].n
        P1_SV = S["sv"].n
        P1_VG = S["vg"].n

        wring_g = [0] * 8
        widx = [0]
        xr_g = [0] * 4
        xo_g = {"stx": [0, 0], "out": [0, 0]}

        def ffn_like_phase(dst_sem_name, lhsT_sb, rhs_src_fn, rhs_transpose,
                           res_src, dst, epi_fn, guards_sync, guards_tensor,
                           per_k_guards=None, lhs_k_guards=None,
                           ts_list=(0, 1), og_list=None, guards_issue=()):
            st_sem = S[dst_sem_name]
            bank_g = [S["vs"].n] * 2
            for gs, gv in guards_sync:
                wt("sync", gs, gv)
            for gs, gv in guards_tensor:
                wt("tensor", gs, gv)
            ts0 = list(ts_list)[0]
            ogl = list(og_list) if og_list is not None else list(range(8))
            og0 = ogl[0]
            steps = [(ts, og, k) for ts in ts_list for og in ogl
                     for k in range(32)]
            w0 = widx[0]
            PE0 = S["pe"].n
            n_ws = [0] * len(steps)

            def issue_step(j):
                ts_, og_, k_ = steps[j]
                if j >= 8:
                    wt("scalar", S["pe"], PE0 + j - 7)
                n_ws[j] = load(WR[(w0 + j) % 8], wring[(w0 + j) % 8],
                               rhs_src_fn(og_, k_), serial=False,
                               queue="scalar")

            wt("scalar", S["pe"], PE0)
            for gs, gv in guards_tensor:
                wt("scalar", gs, gv)
            for gs, gv in guards_issue:
                wt("scalar", gs, gv)
            for j in range(min(8, len(steps))):
                issue_step(j)
            for j, (ts, og, k) in enumerate(steps):
                sl = (w0 + j) % 8
                bg = og % 2
                if k == 0:
                    wt("tensor", S["vs"], bank_g[bg])
                if lhs_k_guards is not None and ts == ts0 and og == og0:
                    wt("tensor", *lhs_k_guards[k])
                wt("tensor", WR[sl], n_ws[j])
                for tt in range(4):
                    t0 = 1024 * k + 512 * ts + 128 * tt
                    pe_group = mm(PS8[4 * bg + tt],
                                  lhsT_sb[:, t0:t0 + 128],
                                  wring[sl][:], k == 0, k == 31,
                                  inc=S["pe"] if tt == 3 else None)
                widx[0] += 1
                if j + 8 < len(steps):
                    issue_step(j + 8)
                if k == 31:
                    last_vs = 0
                    for tt in range(4):
                        tg = 4 * ts + tt
                        rows = slice(128 * tg, 128 * (tg + 1))
                        cols = slice(512 * og, 512 * (og + 1))
                        xsl = (2 * og + tt) % 4
                        wt("sync", S["vg"], xr_g[xsl])
                        n_xr = load(S["x"], xr[xsl], res_src[rows, cols])
                        last_vs = epi_fn(bg, tt, tg, rows, cols, xsl, n_xr,
                                         pe_group, st_sem, dst)
                    bank_g[bg] = last_vs

        def make_ffn_epi(rx_sb, st_sem_name):
            g_guard = [0, 0]

            def epi(bg, tt, tg, rows, cols, xsl, n_xr, pe_group, st_sem, dst):
                ysl = tt % 2
                wt("vector", S["pe"], pe_group)
                wt("vector", S["sv"], g_guard[ysl])
                s_n = op("vector", "scalar_tensor_tensor", yt_t[ysl],
                         PS8[4 * bg + tt], rx_sb[:, tg:tg + 1], bf_bc[:, cols],
                         op0=ALU.mult, op1=ALU.add, inc=S["vs"])
                wt("scalar", S["vs"], s_n)
                s_g = op("scalar", "activation", g_t[ysl], yt_t[ysl], AF.Gelu,
                         inc=S["sv"])
                g_guard[ysl] = s_g
                osl = tt % 2
                wt("vector", S["sv"], s_g)
                wt("vector", S["x"], n_xr)
                wt("vector", st_sem, xo_g[st_sem_name][osl])
                vg_n = op("vector", "scalar_tensor_tensor", xo[osl], g_t[ysl],
                          0.5, xr[xsl], op0=ALU.mult, op1=ALU.add, inc=S["vg"])
                xr_g[xsl] = vg_n
                wt("gpsimd", S["vg"], vg_n)
                xo_g[st_sem_name][osl] = store(st_sem, dst[rows, cols],
                                               xo[osl])
                return s_n
            return epi

        def ffn_phase(rx_sb, res_src, dst, st_sem_name, guards):
            for gs, gv in guards:
                wt("sync", gs, gv)
            xqt_k = []
            for k in range(32):
                xqt_k.append((HR[k % 4],
                              load(HR[k % 4], xqT[:, 1024 * k:1024 * (k + 1)],
                                   xq_d[0:TPC, 128 * k:128 * (k + 1)],
                                   transpose=True, serial=True)))
            ffn_like_phase(
                st_sem_name, xqT,
                lambda og, k: wintT_flat[4096 * og + 128 * k:
                                         4096 * og + 128 * (k + 1), :],
                False, res_src, dst, make_ffn_epi(rx_sb, st_sem_name),
                [], [(S["cc"], cc_wint)], lhs_k_guards=xqt_k,
                guards_issue=guards)

        if stage >= 3:
            ffn_phase(rx1, xA, xB, "stx",
                      [(S["stq"], P1_STQ), (S["stx"], P1_STX)])
        P3_PE = S["pe"].n
        P3_STX = S["stx"].n
        P3_SV = S["sv"].n
        P3_VG = S["vg"].n

        cc_h2 = [0, 0]
        P4_STH = 0
        if stage >= 5:
            h2_g = [0] * 2
            CB = lambda bk: stw_t[:, 8 * bk:8 * (bk + 1)]
            CBc = lambda bk, cc_: stw_t[:, 8 * bk + cc_:8 * bk + cc_ + 1]
            p4v = [0] * 2
            for c in range(NCH):
                sl = c % 2
                wt("sync", S["vv"], p4v[sl])
                if c == 0:
                    wt("sync", S["pe"], P3_PE)
                    wt("sync", S["stx"], P3_STX)
                    wt("sync", S["sv"], S["sv"].n)
                    wt("sync", S["vg"], S["vg"].n)
                n_x = load(S["x"], xt[sl], xB[128 * c:128 * (c + 1), :])
                wt("vector", S["x"], n_x)
                op("vector", "tensor_reduce", CBc(0, c), xt[sl], AX.X,
                   ALU.add)
                op("vector", "tensor_mul", scr, xt[sl], xt[sl], bar=True)
                p4v[sl] = op("vector", "tensor_reduce", CBc(1, c), scr, AX.X,
                             ALU.add, inc=S["vv"], bar=True)
            op("vector", "tensor_scalar", CB(4), CB(0), INV_D, None,
               op0=ALU.mult, bar=True)
            op("vector", "tensor_mul", CB(5), CB(4), CB(4), bar=True)
            op("vector", "scalar_tensor_tensor", CB(6), CB(1), INV_D, CB(5),
               op0=ALU.mult, op1=ALU.subtract, bar=True)
            v_n = op("vector", "tensor_scalar", CB(5), CB(6), 1e-5, None,
                     op0=ALU.add, inc=S["vs"], bar=True)
            wt("scalar", S["vs"], v_n)
            s_n = op("scalar", "activation", CB(3), CB(5), AF.Sqrt,
                     inc=S["sv"])
            wt("vector", S["sv"], s_n)
            op("vector", "reciprocal", CB(7), CB(3), bar=True)
            v_st4 = op("vector", "scalar_tensor_tensor", CB(9), CB(4), -1.0,
                       CB(7), op0=ALU.mult, op1=ALU.mult, inc=S["vs"],
                       bar=True)
            p4_sg = [0, 0]
            for c in range(NCH):
                sl = c % 2
                if c < 2:
                    wt("sync", S["vv"], p4v[sl])
                else:
                    wt("sync", S["sg"], p4_sg[sl])
                n_x = load(S["x"], xt[sl], xB[128 * c:128 * (c + 1), :])
                wt("scalar", S["x"], n_x)
                if c == 0:
                    wt("scalar", S["vs"], v_st4)
                wt("scalar", S["sth"], h2_g[sl])
                sg_n = op("scalar", "activation", h2st[sl], xt[sl],
                          AF.Identity, bias=CBc(9, c), scale=CBc(7, c),
                          inc=S["sg"])
                p4_sg[sl] = sg_n
                wt("gpsimd", S["sg"], sg_n)
                h2_g[sl] = store(
                    S["sth"], h2_i[c // 4][128 * (c % 4):128 * (c % 4) + 128, :],
                    h2st[sl])
                if c == 3:
                    wt("gpsimd", S["sth"], S["sth"].n)
                    cc_h2[0] = cc("AllGather", ALU.bypass, h2_i[0], h2_o[0])
                if c == 7:
                    wt("gpsimd", S["sth"], S["sth"].n)
                    cc_h2[1] = cc("AllGather", ALU.bypass, h2_i[1], h2_o[1])
            P4_STH = S["sth"].n

        hring_g = [0] * 8
        qn_g = [0] * 2
        at_g = [0] * 2
        cc_at = [0, 0]
        if stage >= 5:
            Ehalf_g = [0, 0]
            pair_g = [0, 0]
            pl_g = 0
            pavrb_g = 0
            rank1_g = 0
            for b in range(1 if dump == 24 else B):
                wt("sync", S["pe"], S["pe"].n)
                wt("sync", S["sf"], S["sf"].n)
                wt("sync", S["sv"], S["sv"].n)
                wt("sync", S["vg"], S["vg"].n)
                wt("sync", S["stq"], st_wqkv if b == 0 else S["stq"].n)
                if b == 0 and stage >= 6:
                    wt("sync", S["sth"], P4_STH)
                n_wqkv = 0
                for (w_sb, off) in ((wq_sb, 0), (wk_sb, ES), (wv_sb, 2 * ES)):
                    for k in range(32):
                        n_wqkv = load(S["wqkv"], w_sb[:, ES * k:ES * (k + 1)],
                                      wqkv_d[128 * k:128 * (k + 1),
                                             off:off + ES], serial=False)
                wt("tensor", S["wqkv"], n_wqkv)
                wt("sync", S["cc"], cc_h2[b])
                if b == 0:
                    wt("tensor", S["cc"], cc_h2[1])

                qkv_bank_g = S["vs"].n
                v_bank_g = S["vs"].n
                qkv_sq_g = S["sv"].n
                for tc in range(8):
                    r = tc
                    Xh = b
                    wt("tensor", S["vs"], qkv_bank_g)
                    pe_qk = 0
                    for k in range(32):
                        sl = k % 8
                        wt("sync", S["pe"], hring_g[sl])
                        n_h = load(HR[sl], hring[sl],
                                   h2_o[Xh][r][0:512, 128 * k:128 * (k + 1)],
                                   transpose=True, serial=False)
                        wt("tensor", HR[sl], n_h)
                        for tt in range(4):
                            mm(PS8[tt], hring[sl][:, 128 * tt:128 * (tt + 1)],
                               wq_sb[:, ES * k:ES * (k + 1)], k == 0, k == 31)
                        for tt in range(4):
                            pe_qk = mm(PS8[4 + tt],
                                       hring[sl][:, 128 * tt:128 * (tt + 1)],
                                       wk_sb[:, ES * k:ES * (k + 1)],
                                       k == 0, k == 31,
                                       inc=S["pe"] if tt == 3 else None)
                        hring_g[sl] = pe_qk
                    vs_qk = 0
                    vs_q4 = 0
                    sq_guard = qkv_sq_g
                    for qk in range(2):
                        bias_bc = bq_bc if qk == 0 else bk_bc
                        g_bc = gq_bc if qk == 0 else gk_bc
                        dstd = qn_d if qk == 0 else kn_d
                        for tt in range(4):
                            bank = PS8[4 * qk + tt]
                            wt("vector", S["pe"], pe_qk)
                            wt("vector", S["sv"], sq_guard)
                            s_n = op("vector", "scalar_tensor_tensor", qtmp,
                                     bank, 1.0, bias_bc, op0=ALU.mult,
                                     op1=ALU.add, inc=S["vs"], bar=True)
                            vs_qk = s_n
                            wt("scalar", S["vs"], s_n)
                            for hh in range(4):
                                cseg = slice(128 * hh, 128 * (hh + 1))
                                sq_guard = op(
                                    "scalar", "activation", sqq[:, cseg],
                                    qtmp[:, cseg], AF.Square,
                                    accum_out=st3[:, hh:hh + 1],
                                    inc=S["sv"] if hh == 3 else None)
                            wt("vector", S["sv"], sq_guard)
                            v_n = op("vector", "tensor_scalar", st3[:, 0:4],
                                     st3[:, 0:4], float(1.0 / HD), 1e-6,
                                     op0=ALU.mult, op1=ALU.add, inc=S["vs"],
                                     bar=True)
                            wt("scalar", S["vs"], v_n)
                            s_n = op("scalar", "activation", st3[:, 4:8],
                                     st3[:, 0:4], AF.Sqrt, inc=S["sv"])
                            wt("vector", S["sv"], s_n)
                            op("vector", "reciprocal", st3[:, 8:12],
                               st3[:, 4:8], bar=True)
                            qsl = (2 * qk + tt) % 2
                            wt("vector", S["stq"], qn_g[qsl])
                            vg_n = 0
                            for hh in range(4):
                                cseg = slice(128 * hh, 128 * (hh + 1))
                                vg_n = op("vector", "scalar_tensor_tensor",
                                          qnst[qsl][:, cseg], qtmp[:, cseg],
                                          st3[:, 8 + hh:9 + hh], g_bc,
                                          op0=ALU.mult, op1=ALU.mult,
                                          inc=S["vg"] if hh == 3 else None)
                            wt("gpsimd", S["vg"], vg_n)
                            trow = 512 * tc + 128 * tt
                            qn_g[qsl] = store(S["stq"],
                                              dstd[trow:trow + 128, :],
                                              qnst[qsl])
                        if qk == 0:
                            vs_q4 = vs_qk
                    qkv_sq_g = sq_guard
                    wt("tensor", S["vs"], vs_q4)
                    pe_v = 0
                    for k in range(32):
                        sl = k % 8
                        wt("sync", S["pe"], hring_g[sl])
                        n_h = load(HR[sl], hring[sl],
                                   h2_o[Xh][r][0:512, 128 * k:128 * (k + 1)],
                                   transpose=True, serial=False)
                        wt("tensor", HR[sl], n_h)
                        for tt in range(4):
                            pe_v = mm(PS8[tt],
                                      hring[sl][:, 128 * tt:128 * (tt + 1)],
                                      wv_sb[:, ES * k:ES * (k + 1)],
                                      k == 0, k == 31,
                                      inc=S["pe"] if tt == 3 else None)
                        hring_g[sl] = pe_v
                    for tt in range(4):
                        wt("vector", S["pe"], pe_v)
                        cseg = slice(512 * (4 * tc + tt),
                                     512 * (4 * tc + tt + 1))
                        qkv_bank_g = op("vector", "scalar_tensor_tensor",
                                        V_sb[:, cseg], PS8[tt], 1.0, bv_bc,
                                        op0=ALU.mult, op1=ALU.add, inc=S["vs"])
                QKV_STQ = S["stq"].n
                QKV_PE = S["pe"].n
                QKV_VS = S["vs"].n

                wt("sync", S["stq"], QKV_STQ)
                wt("sync", S["pe"], QKV_PE)
                n_qt = 0
                for (dsrc, dsb) in ((qn_d, qT), (kn_d, kT)):
                    for h in range(HPC):
                        n_qt = load(S["t"], dsb[:, N * h:N * (h + 1)],
                                    dsrc[0:N, 128 * h:128 * (h + 1)],
                                    transpose=True, serial=False)
                wt("tensor", S["t"], n_qt)
                wt("tensor", S["vs"], QKV_VS)

                for h in range(HPC):
                    for qc in range(8):
                        qcols = slice(N * h + 512 * qc,
                                      N * h + 512 * (qc + 1))
                        av_n = 0
                        for half in range(2):
                            exp_n = 0
                            for g_ in range(8):
                                pr = g_ % 2
                                ps_pair = ps2a if pr == 0 else ps2b
                                wt("tensor", S["sf"], pair_g[pr])
                                pe_n = 0
                                for j in range(2):
                                    kt = 16 * half + 2 * g_ + j
                                    kcol = slice(N * h + 128 * kt,
                                                 N * h + 128 * (kt + 1))
                                    pe_n = mm(
                                        ps_pair.ap()[:, 512 * j:512 * (j + 1)],
                                        kT[:, kcol], qT[:, qcols], True, True,
                                        inc=S["pe"] if j == 1 else None)
                                wt("scalar", S["pe"], pe_n)
                                wt("scalar", S["pe"], Ehalf_g[half])
                                ecol = slice(1024 * g_, 1024 * (g_ + 1))
                                exp_n = op("scalar", "activation",
                                           Eb[half][:, ecol], ps_pair.ap()[:],
                                           AF.Exp, inc=S["sf"])
                                pair_g[pr] = exp_n
                            wt("tensor", S["sf"], exp_n)
                            if half == 0:
                                wt("tensor", S["vg"], pavrb_g)
                                wt("tensor", S["vs"], pl_g)
                            for kt16 in range(16):
                                kt = 16 * half + kt16
                                erhs = Eb[half][:, 512 * kt16:512 * (kt16 + 1)]
                                vcol = slice(512 * kt + 128 * h,
                                             512 * kt + 128 * (h + 1))
                                mm(pav.ap()[:], V_sb[:, vcol], erhs,
                                   kt == 0, kt == 31)
                                av_n = mm(pl.ap()[0:1, :], ones_bf[:, 0:1],
                                          erhs, kt == 0, kt == 31,
                                          inc=S["pe"] if kt16 == 15 else None)
                            Ehalf_g[half] = av_n
                        wt("vector", S["pe"], av_n)
                        wt("vector", S["pe"], rank1_g)
                        pl_g = op("vector", "reciprocal", lrow[0:1, :],
                                  pl.ap()[0:1, :], inc=S["vs"])
                        wt("tensor", S["vs"], pl_g)
                        rank1_g = mm(prb.ap()[:], onesr_f[0:1, :],
                                     lrow[0:1, :], True, True, inc=S["pe"])
                        asl = qc % 2
                        wt("vector", S["pe"], rank1_g)
                        op("vector", "tensor_copy", rbc_sb, prb.ap()[:],
                           bar=True)
                        wt("vector", S["sth"], at_g[asl])
                        vg_n = op("vector", "tensor_mul", atst[asl],
                                  pav.ap()[:], rbc_sb, inc=S["vg"])
                        pavrb_g = vg_n
                        wt("gpsimd", S["vg"], vg_n)
                        gcol = 4096 * b + 512 * qc
                        at_g[asl] = store(
                            S["sth"],
                            at_d[128 * h:128 * (h + 1), gcol:gcol + 512],
                            atst[asl])

        cc_rs = [0, 0]
        if stage >= 6:
            wt("sync", S["sth"], S["sth"].n)
            wt("sync", S["pe"], S["pe"].n)
            wt("sync", S["sf"], S["sf"].n)
            wt("sync", S["vg"], S["vg"].n)
            wt("sync", S["stw"], st_wom)
            n_at = 0
            for e in range(4):
                n_at = load(S["t"], at4[e], at_d[128 * e:128 * (e + 1), :],
                            serial=False)
            n_wo = 0
            for e in range(4):
                n_wo = load(S["wqkv"], wo_sb[:, D * e:D * (e + 1)],
                            wom_d[128 * e:128 * (e + 1), :], serial=False)
            wt("tensor", S["t"], n_at)
            wt("tensor", S["wqkv"], n_wo)
            wt("tensor", S["vs"], S["vs"].n)
            rnd_g = [S["vs"].n] * 2
            stg_g = [0, 0]
            rs_st = 0
            for T in range(64):
                ssl = T % 2
                vs_cp = 0
                for rnd in range(2):
                    gr = (2 * T + rnd) % 2
                    wt("tensor", S["vs"], rnd_g[gr])
                    pe_n = 0
                    for e in range(4):
                        for oc in range(4):
                            o = 4 * rnd + oc
                            pe_n = mm(PS8[4 * gr + oc],
                                      at4[e][:, 128 * T:128 * (T + 1)],
                                      wo_sb[:, D * e + 512 * o:D * e + 512 * (o + 1)],
                                      e == 0, e == 3,
                                      inc=S["pe"] if oc == 3 else None)
                    for oc in range(4):
                        o = 4 * rnd + oc
                        wt("vector", S["pe"], pe_n)
                        if rnd == 0 and oc == 0:
                            wt("vector", S["stx"], stg_g[ssl])
                        vs_cp = op("vector", "tensor_copy",
                                   stg6[ssl][:, 512 * o:512 * (o + 1)],
                                   PS8[4 * gr + oc], inc=S["vs"])
                    rnd_g[gr] = vs_cp
                wt("gpsimd", S["vs"], vs_cp)
                stg_g[ssl] = store(S["stx"], rs_i[128 * T:128 * (T + 1), :],
                                   stg6[ssl])
                if T == 31:
                    wt("gpsimd", S["stx"], S["stx"].n)
                    cc_rs[0] = cc("ReduceScatter", ALU.add,
                                  rs_i[0:4096, :], rs_o[0:512, :])
            P6A_STX_RS = S["stx"].n
            P6A_PE = S["pe"].n

            wint8_rhs = lambda og, k: wintT_flat[4096 * og + 128 * k:
                                                 4096 * og + 128 * (k + 1), :]
            wt("sync", S["cc"], cc_rs[0])
            x3_st_g = 0
            for j in range(8):
                c2, hf = divmod(j, 2)
                rows = slice(128 * c2, 128 * (c2 + 1))
                cols = slice(2048 * hf, 2048 * (hf + 1))
                wt("sync", S["vg"], S["vg"].n)
                n_r = load(S["x"], rsb_t, rs_o[rows, cols])
                n_x2 = load(S["x"], xbt_t, xB[rows, cols])
                wt("vector", S["x"], n_x2)
                wt("vector", S["stx"], x3_st_g)
                op("vector", "scalar_tensor_tensor", x3st, rsb_t, 1.0,
                   xbt_t, op0=ALU.mult, op1=ALU.add, bar=True)
                vg_n = op("vector", "tensor_add", x3st, x3st,
                          bo_bc[:, cols], inc=S["vg"])
                wt("gpsimd", S["vg"], vg_n)
                x3_st_g = store(S["stx"], xA[rows, cols], x3st)
            wt("gpsimd", S["stx"], P6A_STX_RS)
            wt("gpsimd", S["x"], S["x"].n)
            cc_rs[1] = cc("ReduceScatter", ALU.add,
                          rs_i[4096:8192, :], rs_o[512:1024, :])
            P6B0_STX = S["stx"].n
            if stage >= 8:
                g_p7a = [(S["stx"], P6B0_STX), (S["pe"], P6A_PE),
                         (S["sv"], S["sv"].n), (S["vg"], S["vg"].n),
                         (S["stq"], S["stq"].n), (S["stw"], S["stw"].n),
                         (S["sth"], S["sth"].n), (S["sf"], S["sf"].n)]
                ln_quant_phase(xA, xB, rx2, g_p7a, chunks=range(0, 4))
                epi8 = make_ffn_epi(rx2, "out")
                g_p8a = [(S["stq"], S["stq"].n), (S["stx"], S["stx"].n)]
                for gs, gv in g_p8a:
                    wt("sync", gs, gv)
                xqt_a = []
                for k in range(32):
                    xqt_a.append((HR[k % 4],
                                  load(HR[k % 4],
                                       xqT[:, 1024 * k:1024 * k + 512],
                                       xq_d[0:512, 128 * k:128 * (k + 1)],
                                       transpose=True, serial=True)))
                ffn_like_phase("out", xqT, wint8_rhs, False, xB, out_t,
                               epi8, [], [], lhs_k_guards=xqt_a,
                               ts_list=(0,), guards_issue=g_p8a)
            wt("sync", S["cc"], cc_rs[1])
            wt("sync", S["stq"], S["stq"].n)
            x3_st_g1 = 0
            for c2 in range(4):
                rows = slice(512 + 128 * c2, 512 + 128 * (c2 + 1))
                for hf in range(2):
                    cols = slice(2048 * hf, 2048 * (hf + 1))
                    wt("sync", S["vg"], S["vg"].n)
                    n_r = load(S["x"], rsb_t, rs_o[rows, cols])
                    n_x2 = load(S["x"], xbt_t, xB[rows, cols])
                    wt("vector", S["x"], n_x2)
                    wt("vector", S["stx"], x3_st_g1)
                    op("vector", "scalar_tensor_tensor", x3st, rsb_t, 1.0,
                       xbt_t, op0=ALU.mult, op1=ALU.add, bar=True)
                    vg_n = op("vector", "tensor_add", x3st, x3st,
                              bo_bc[:, cols], inc=S["vg"])
                    wt("gpsimd", S["vg"], vg_n)
                    x3_st_g1 = store(S["stx"], xA[rows, cols], x3st)
        P6_STX = S["stx"].n
        P6_PE = S["pe"].n

        if stage >= 8:
            g_p7b = [(S["stx"], P6_STX), (S["pe"], P6_PE),
                     (S["sv"], S["sv"].n), (S["vg"], S["vg"].n),
                     (S["stq"], S["stq"].n), (S["stw"], S["stw"].n),
                     (S["sth"], S["sth"].n), (S["sf"], S["sf"].n)]
            ln_quant_phase(xA, xB, rx2, g_p7b, chunks=range(4, 8))
            g_p8b = [(S["stq"], S["stq"].n), (S["stx"], S["stx"].n)]
            for gs, gv in g_p8b:
                wt("sync", gs, gv)
            xqt_b = []
            for k in range(32):
                xqt_b.append((HR[k % 4],
                              load(HR[k % 4],
                                   xqT[:, 1024 * k + 512:1024 * (k + 1)],
                                   xq_d[512:1024, 128 * k:128 * (k + 1)],
                                   transpose=True, serial=True)))
            ffn_like_phase("out", xqT, wint8_rhs, False, xB, out_t, epi8,
                           [], [], lhs_k_guards=xqt_b, ts_list=(1,),
                           guards_issue=g_p8b)

        if dump:
            for nm in ["stx", "stq", "sth", "stw", "cc", "vg", "sg"]:
                wt("gpsimd", S[nm], S[nm].n)
            if dump == 21:
                store(S["out"], out_t[:, 0:D // 2].bitcast(BF16), xq_d)
            elif dump == 22:
                store(S["out"], out_t[0:512, 0:D // 2].bitcast(BF16), wint_i)
            elif dump == 23:
                store(S["out"], out_t[0:512, 0:D // 2].bitcast(BF16)
                      .rearrange("p (a b) -> p a b", a=2)
                      .rearrange("p a b -> (p a) b") if False else
                      out_t.bitcast(BF16)[0:512, 0:B * N], at_d)
            elif dump == 24:
                store(S["out"], out_t.bitcast(BF16)[0:1024, 0:ES],
                      qn_d[1280:2304])
        elif stage < 8:
            src = {1: xA, 3: xB, 6: xA}[stage]
            for nm in ["stx", "stq", "sth", "stw", "cc", "vg", "sg"]:
                wt("gpsimd", S[nm], S[nm].n)
            store(S["out"], out_t, src)

        wt("sync", S["out"], S["out"].n)
        wt("gpsimd", S["out"], S["out"].n)

        block = es.enter_context(nc.Block())
        @block.sync
        def _(e):
            for fn in prog["sync"]:
                fn(e)

        @block.gpsimd
        def _(e):
            for fn in prog["gpsimd"]:
                fn(e)

        @block.vector
        def _(e):
            for fn in prog["vector"]:
                fn(e)

        @block.scalar
        def _(e):
            for fn in prog["scalar"]:
                fn(e)

        @block.tensor
        def _(e):
            for fn in prog["tensor"]:
                fn(e)

    return nc


def kernel(L0_attn_hidden_states, L0_attn_Wq, L0_attn_bq, L0_attn_Wk,
           L0_attn_bk, L0_attn_Wv, L0_attn_bv, L0_attn_Wo, L0_attn_bo,
           L0_attn_gq, L0_attn_gk, L0_ffn_W, L0_ffn_b):
    x = np.ascontiguousarray(
        np.asarray(L0_attn_hidden_states, dtype=np.float32).reshape(B * N, D))
    Wq = np.asarray(L0_attn_Wq, np.float32)
    Wk = np.asarray(L0_attn_Wk, np.float32)
    Wv = np.asarray(L0_attn_Wv, np.float32)
    Wo = np.asarray(L0_attn_Wo, np.float32)
    Wf = np.asarray(L0_ffn_W, np.float32)
    bq = np.asarray(L0_attn_bq, np.float32).reshape(1, D)
    bk = np.asarray(L0_attn_bk, np.float32).reshape(1, D)
    bv = np.asarray(L0_attn_bv, np.float32).reshape(1, D)
    bo = np.asarray(L0_attn_bo, np.float32).reshape(1, D)
    bf_ = np.asarray(L0_ffn_b, np.float32).reshape(1, D)
    gq = np.asarray(L0_attn_gq, np.float32).reshape(1, HD)
    gk = np.asarray(L0_attn_gk, np.float32).reshape(1, HD)

    in_maps = []
    for c in range(NC):
        esl = slice(ES * c, ES * (c + 1))
        myrows = np.r_[512 * c:512 * (c + 1), N + 512 * c:N + 512 * (c + 1)]
        in_maps.append({
            "x_my": np.ascontiguousarray(x[myrows]),
            "wq_my": np.ascontiguousarray(Wq[:, esl]),
            "wk_my": np.ascontiguousarray(Wk[:, esl]),
            "wv_my": np.ascontiguousarray(Wv[:, esl]),
            "bq_my": np.ascontiguousarray(bq[:, esl]),
            "bk_my": np.ascontiguousarray(bk[:, esl]),
            "bv_my": np.ascontiguousarray(bv[:, esl]),
            "gq": gq, "gk": gk,
            "wfs_my": np.ascontiguousarray(Wf[esl, :]),
            "wos_my": np.ascontiguousarray(Wo[esl, :]),
            "bf": bf_, "bo": bo,
        })
    nc = build_nc(_STAGE)
    res = run_bass_kernel_spmd(nc, in_maps, core_ids=list(range(NC)))
    out = np.empty((B * N, D), np.float32)
    for c in range(NC):
        oc = res.results[c]["out"]
        out[512 * c:512 * (c + 1)] = oc[0:512]
        out[N + 512 * c:N + 512 * (c + 1)] = oc[512:1024]
    return out.reshape(B, N, D)


# version 15
# speedup vs baseline: 1.0369x; 1.0136x over previous
import numpy as np
import concourse.bass as bass
import concourse.mybir as mybir
from concourse.bass_utils import run_bass_kernel_spmd

F32 = mybir.dt.float32
BF16 = mybir.dt.bfloat16
I32 = mybir.dt.int32
AF = mybir.ActivationFunctionType
ALU = mybir.AluOpType
AX = mybir.AxisListType

D = 4096
B = 2
N = 4096
TPC = 1024
NC = 8
HD = 128
HPC = 4
ES = HPC * HD
NCH = TPC // 128
RG = [[0, 1, 2, 3, 4, 5, 6, 7]]
ALPHA = float(1.0 / np.sqrt(HD))
INV_D = float(1.0 / D)
MEAN_W = float(2.0 ** -24)

_STAGE = 8


class Sem:
    def __init__(self, h):
        self.h = h
        self.n = 0


def build_nc(stage):
    dump = stage if stage in (21, 22, 23, 24) else None
    if dump in (21, 22):
        stage = 2
    elif dump in (23, 24):
        stage = 5
    nc = bass.Bass("TRN2", target_bir_lowering=False, debug=False)

    x_in = nc.dram_tensor("x_my", [TPC, D], F32, kind="ExternalInput").ap()
    wq_in = nc.dram_tensor("wq_my", [D, ES], F32, kind="ExternalInput").ap()
    wk_in = nc.dram_tensor("wk_my", [D, ES], F32, kind="ExternalInput").ap()
    wv_in = nc.dram_tensor("wv_my", [D, ES], F32, kind="ExternalInput").ap()
    bq_in = nc.dram_tensor("bq_my", [1, ES], F32, kind="ExternalInput").ap()
    bk_in = nc.dram_tensor("bk_my", [1, ES], F32, kind="ExternalInput").ap()
    bv_in = nc.dram_tensor("bv_my", [1, ES], F32, kind="ExternalInput").ap()
    gq_in = nc.dram_tensor("gq", [1, HD], F32, kind="ExternalInput").ap()
    gk_in = nc.dram_tensor("gk", [1, HD], F32, kind="ExternalInput").ap()
    wfs_in = nc.dram_tensor("wfs_my", [ES, D], F32, kind="ExternalInput").ap()
    wos_in = nc.dram_tensor("wos_my", [ES, D], F32, kind="ExternalInput").ap()
    bf_in = nc.dram_tensor("bf", [1, D], F32, kind="ExternalInput").ap()
    bo_in = nc.dram_tensor("bo", [1, D], F32, kind="ExternalInput").ap()
    out_t = nc.dram_tensor("out", [TPC, D], F32, kind="ExternalOutput").ap()

    xA = nc.dram_tensor("xA", [TPC, D], F32, kind="Internal").ap()
    xB = nc.dram_tensor("xB", [TPC, D], F32, kind="Internal").ap()
    xq_d = nc.dram_tensor("xq_d", [TPC, D], BF16, kind="Internal").ap()
    qn_d = nc.dram_tensor("qn_d", [N, ES], BF16, kind="Internal").ap()
    kn_d = nc.dram_tensor("kn_d", [N, ES], BF16, kind="Internal").ap()
    wqkv_d = nc.dram_tensor("wqkv_d", [D, 3 * ES], BF16, kind="Internal").ap()

    ws_i = nc.dram_tensor("ws_i", [1, 8], F32, kind="Internal").ap()
    ws_o = nc.dram_tensor("ws_o", [1, 8], F32, kind="Internal").ap()
    wint_i = nc.dram_tensor("wint_i", [ES, D], BF16, kind="Internal").ap()
    wintT_i = nc.dram_tensor("wintT_i", [D, ES], BF16, kind="Internal").ap()
    wintT_o = nc.dram_tensor("wintT_o", [NC, D, ES], BF16,
                             kind="Internal").ap()
    wom_d = nc.dram_tensor("wom_d", [ES, D], BF16, kind="Internal").ap()
    h2_i = [nc.dram_tensor(f"h2_i{j}", [512, D], BF16, kind="Internal").ap()
            for j in range(2)]
    h2_o = [nc.dram_tensor(f"h2_o{j}", [NC, 512, D], BF16, kind="Internal").ap()
            for j in range(2)]
    at_d = nc.dram_tensor("at_d", [ES, B * N], BF16, kind="Internal").ap()
    rs_i = nc.dram_tensor("rs_i", [B * N, D], BF16, kind="Internal").ap()
    rs_o = nc.dram_tensor("rs_o", [TPC, D], BF16, kind="Internal").ap()

    from contextlib import ExitStack
    with ExitStack() as es:
        ARENA_KB = 176
        arena_t = es.enter_context(
            nc.sbuf_tensor("arena", [128, ARENA_KB * 1024], mybir.dt.uint8))
        _ar = arena_t.ap()

        _pers = [0]

        def P(dt, cols):
            sz = cols * {F32: 4, BF16: 2, I32: 4}[dt]
            off = (_pers[0] + 63) // 64 * 64
            _pers[0] = off + sz
            assert _pers[0] <= 28 * 1024
            return _ar[:, off:off + sz].bitcast(dt)

        def R(off_kb, dt, cols):
            off = int(off_kb * 1024)
            sz = cols * {F32: 4, BF16: 2, I32: 4}[dt]
            assert 28 * 1024 <= off and off + sz <= ARENA_KB * 1024
            return _ar[:, off:off + sz].bitcast(dt)

        bo_bc = P(BF16, D)
        bf_bc = P(BF16, D)
        bq_bc = P(F32, ES)
        bk_bc = P(F32, ES)
        bv_bc = P(F32, ES)
        gq_bc = P(F32, HD)
        gk_bc = P(F32, HD)
        ones_bf = P(BF16, 32)
        onesr_f = P(F32, 128)
        onescol = P(F32, 16)
        rx1 = P(F32, 16)
        rx2 = P(F32, 16)
        wsrow = P(F32, 16)
        ws_bc = P(F32, 16)
        mwc_bc = P(F32, 16)
        lrow = P(F32, 512)
        st1 = P(F32, 16)
        st2 = P(F32, 16)
        st3 = P(F32, 16)
        stp = [P(F32, 32), P(F32, 32)]
        stw_t = P(F32, 256)
        st3b = [P(F32, 16), P(F32, 16)]


        xt = [R(28, F32, D), R(44, F32, D)]
        lnt = R(60, F32, D)
        scr = R(76, F32, D)
        xa = [R(92, F32, D), R(108, F32, D)]
        tq = R(124, F32, D)
        i32t = R(140, I32, D)
        xqbf = [R(156, BF16, D), R(164, BF16, D)]
        h2st = [R(156, BF16, D), R(164, BF16, D)]
        xqT = R(28, BF16, 32 * 1024)
        at_sb = R(28, BF16, 32 * 1024)
        wring = [R(92 + i, BF16, 512) for i in range(8)]
        yt_t = [R(100, F32, 512), R(102, F32, 512)]
        g_t = [R(104, F32, 512), R(106, F32, 512)]
        xo = [R(108, F32, 512), R(110, F32, 512)]
        xr = [R(112 + 2 * i, F32, 512) for i in range(4)]
        wq_sb = R(28, BF16, 32 * ES)
        wk_sb = R(60, BF16, 32 * ES)
        wv_sb = R(92, BF16, 32 * ES)
        V_sb = R(124, BF16, 32 * ES)
        hring = [R(156 + i, BF16, 512) for i in range(8)]
        qtmp = R(164, F32, 512)
        sqq = R(166, F32, 512)
        qnst = [R(168, BF16, 512), R(169, BF16, 512)]
        atst = [R(170, BF16, 512), R(171, BF16, 512)]
        rbc_sb = R(172, F32, 512)
        qT = R(28, BF16, HPC * N)
        kT = R(60, BF16, HPC * N)
        Eb = [R(92, BF16, 16 * 512), R(108, BF16, 16 * 512)]
        at4 = [R(28 + 16 * e, BF16, B * N) for e in range(4)]
        wo_sb = R(124, BF16, HPC * D)
        stg6 = [R(92, BF16, D), R(100, BF16, D)]
        rsb_t = R(156, BF16, 2048)
        xbt_t = R(160, F32, 2048)
        x3st = R(168, F32, 2048)

        ps2a = es.enter_context(nc.psum_tensor("ps2a", [128, 1024], F32))
        ps2b = es.enter_context(nc.psum_tensor("ps2b", [128, 1024], F32))
        pav = es.enter_context(nc.psum_tensor("pav", [128, 512], F32))
        pl = es.enter_context(nc.psum_tensor("pl", [128, 512], F32))
        prb = es.enter_context(nc.psum_tensor("prb", [128, 512], F32))
        pex = es.enter_context(nc.psum_tensor("pex", [128, 512], F32))
        PS8 = [ps2a.ap()[:, 0:512], ps2a.ap()[:, 512:1024],
               ps2b.ap()[:, 0:512], ps2b.ap()[:, 512:1024],
               pav.ap()[:], pl.ap()[:], prb.ap()[:], pex.ap()[:]]

        S = {}
        for nm in ["prep", "x", "w", "t", "h2", "stx", "stq", "sth", "stw",
                   "cc", "pe", "vs", "sv", "vg", "sg", "sf", "out",
                   "vv", "sc", "wqkv"]:
            S[nm] = Sem(es.enter_context(nc.semaphore("s_" + nm)))
        WR = [Sem(es.enter_context(nc.semaphore(f"s_wr{i}"))) for i in range(8)]
        HR = [Sem(es.enter_context(nc.semaphore(f"s_hr{i}"))) for i in range(8)]

        prog = {e: [] for e in ["sync", "gpsimd", "vector", "scalar", "tensor"]}

        def em(eng, fn):
            prog[eng].append(fn)

        def wt(eng, sem, val):
            if val > 0:
                em(eng, lambda e, s=sem, v=val: e.wait_ge(s.h, v))

        def load(sem, dst, src, transpose=False, serial=True, queue="sync"):
            if serial and sem.n > 0:
                wt(queue, sem, sem.n)
            def fn(e, d=dst, r=src, t=transpose, s=sem):
                e.dma_start(out=d, in_=r, transpose=t).then_inc(s.h, 16)
            em(queue, fn)
            sem.n += 16
            return sem.n

        def store(sem, dst, src, serial=True, queue="gpsimd"):
            if serial and sem.n > 0:
                wt(queue, sem, sem.n)
            def fn(e, d=dst, r=src, s=sem):
                e.dma_start(out=d, in_=r).then_inc(s.h, 16)
            em(queue, fn)
            sem.n += 16
            return sem.n

        SELF = {"vector": "vv", "scalar": "sc"}

        def op(eng, name, *args, inc=None, bar=False, **kw):
            sem = inc if inc is not None else (S[SELF[eng]] if bar else None)
            def fn(e, name=name, args=args, kw=kw, sem=sem):
                ins = getattr(e, name)(*args, **kw)
                if sem is not None:
                    ins.then_inc(sem.h, 1)
            em(eng, fn)
            if sem is not None:
                sem.n += 1
                if bar:
                    wt(eng, sem, sem.n)
                return sem.n
            return None

        def mm(out, lhsT, rhs, start, stop, inc=None):
            return op("tensor", "matmul", out, lhsT, rhs,
                      start=start, stop=stop, inc=inc)

        def cc(kind, alu, i_, o_):
            def fn(e, k=kind, a=alu, i=i_, o=o_):
                e.collective_compute(k, a, replica_groups=RG, ins=[i],
                                     outs=[o]).then_inc(S["cc"].h, 1)
            em("gpsimd", fn)
            S["cc"].n += 1
            return S["cc"].n

        load(S["prep"], bq_bc[0:1, :], bq_in, serial=False)
        load(S["prep"], bk_bc[0:1, :], bk_in, serial=False)
        load(S["prep"], bv_bc[0:1, :], bv_in, serial=False)
        load(S["prep"], gq_bc[0:1, :], gq_in, serial=False)
        load(S["prep"], gk_bc[0:1, :], gk_in, serial=False)
        load(S["prep"], lnt[0:1, 0:D], bo_in, serial=False)
        n_all = load(S["prep"], scr[0:1, 0:D], bf_in, serial=False)
        n_bq = n_bk = n_bv = n_gq = n_gk = n_bo = n_bf = n_all

        em("vector", lambda e: e.memset(ones_bf[:, 0:1], 1.0))
        em("vector", lambda e: e.memset(onesr_f[0:1, :], 1.0))
        em("vector", lambda e: e.memset(onescol[:, 0:1], 1.0))

        wt("vector", S["prep"], n_gq)
        op("vector", "tensor_scalar", gq_bc[0:1, :], gq_bc[0:1, :], ALPHA,
           None, op0=ALU.mult, inc=S["vs"], bar=True)

        def bcast(row_ap, dst_ap, cols, dep_n):
            for j in range((cols + 511) // 512):
                c0, c1 = 512 * j, min(512 * (j + 1), cols)
                wt("tensor", S["prep"], dep_n)
                wt("tensor", S["vs"], S["vs"].n)
                pe_n = mm(prb.ap()[:, 0:c1 - c0], onesr_f[0:1, :],
                          row_ap[0:1, c0:c1], True, True, inc=S["pe"])
                wt("vector", S["pe"], pe_n)
                op("vector", "tensor_copy", dst_ap[:, c0:c1],
                   prb.ap()[:, 0:c1 - c0], inc=S["vs"])

        bcast(bq_bc, bq_bc, ES, n_bq)
        bcast(bk_bc, bk_bc, ES, n_bk)
        bcast(bv_bc, bv_bc, ES, n_bv)
        bcast(gq_bc, gq_bc, HD, n_gq)
        bcast(gk_bc, gk_bc, HD, n_gk)
        bcast(lnt, bo_bc, D, n_bo)
        bcast(scr, bf_bc, D, n_bf)

        p0_xt_g = [0, 0]
        for i in range(4):
            stg = xt[i % 2]
            wt("sync", S["vv"], p0_xt_g[i % 2])
            n_l = load(S["prep"], stg, wfs_in[128 * i:128 * (i + 1), :])
            wt("vector", S["prep"], n_l)
            p0_xt_g[i % 2] = op("vector", "tensor_reduce", st1[:, i:i + 1],
                                stg, AX.X, ALU.add,
                                apply_absolute_value=True, inc=S["vv"],
                                bar=True)
        v_n = op("vector", "tensor_reduce", st1[:, 4:5], st1[:, 0:4], AX.X,
                 ALU.add, inc=S["vs"], bar=True)
        wt("tensor", S["vs"], v_n)
        pe_n = mm(pex.ap()[0:1, 0:1], st1[:, 4:5], onescol[:, 0:1], True, True,
                  inc=S["pe"])
        wt("vector", S["pe"], pe_n)
        op("vector", "tensor_copy", wsrow[0:1, 0:1], pex.ap()[0:1, 0:1])
        vg_n = op("vector", "memset", wsrow[0:1, 1:8], 0.0, inc=S["vg"])
        wt("gpsimd", S["vg"], vg_n)
        st_n = store(S["stw"], ws_i, wsrow[0:1, 0:8])
        wt("gpsimd", S["stw"], st_n)
        cc_ws = cc("AllReduce", ALU.add, ws_i, ws_o)

        store(S["stq"], wqkv_d[:, 0:ES], wq_in)
        store(S["stq"], wqkv_d[:, ES:2 * ES], wk_in)
        st_wqkv = store(S["stq"], wqkv_d[:, 2 * ES:3 * ES], wv_in)
        st_wom = store(S["stw"], wom_d, wos_in)

        wt("sync", S["cc"], cc_ws)
        n_ws = load(S["x"], wsrow[0:1, 8:16], ws_o)
        wt("vector", S["x"], n_ws)
        op("vector", "tensor_scalar", wsrow[0:1, 0:1], wsrow[0:1, 8:9],
           MEAN_W, 1e-5, op0=ALU.mult, op1=ALU.max, bar=True)
        op("vector", "tensor_scalar", wsrow[0:1, 1:2], wsrow[0:1, 0:1],
           float(1.0 / 127.0), None, op0=ALU.mult, bar=True)
        v_n = op("vector", "reciprocal", wsrow[0:1, 2:3], wsrow[0:1, 0:1],
                 inc=S["vs"], bar=True)
        wt("tensor", S["vs"], v_n)
        mm(pex.ap()[:, 0:1], onesr_f[0:1, :], wsrow[0:1, 2:3], True, True)
        pe_n = mm(pex.ap()[:, 1:2], onesr_f[0:1, :], wsrow[0:1, 1:2], True,
                  True, inc=S["pe"])
        wt("vector", S["pe"], pe_n)
        op("vector", "tensor_copy", ws_bc[:, 0:1], pex.ap()[:, 0:1])
        WS_READY = op("vector", "tensor_copy", mwc_bc[:, 0:1], pex.ap()[:, 1:2],
                      inc=S["vs"])

        def ln_quant_phase(x_src, xa_dst, rx_sb, guards, chunks=None):
            cl = list(chunks) if chunks is not None else list(range(NCH))
            c0, ncl = cl[0], len(cl)
            resident = ncl <= 4
            if resident:
                xt_r = [R(28, F32, D), R(44, F32, D), R(60, F32, D),
                        R(76, F32, D)]
                scr_r = R(92, F32, D)
                xa_r = R(108, F32, D)
            C = lambda b: stw_t[:, 8 * b + c0:8 * b + c0 + ncl]
            Cc = lambda b, c: stw_t[:, 8 * b + c:8 * b + c + 1]
            p1v = [0] * 2
            for j, c in enumerate(cl):
                sl = c % 2
                xtile = xt_r[j] if resident else xt[sl]
                sq_t = scr_r if resident else scr
                if not resident:
                    wt("sync", S["vv"], p1v[sl])
                if c == cl[0]:
                    for gs, gv in guards:
                        wt("sync", gs, gv)
                n_x = load(S["x"], xtile, x_src[128 * c:128 * (c + 1), :])
                wt("vector", S["x"], n_x)
                op("vector", "tensor_reduce", Cc(0, c), xtile, AX.X, ALU.add)
                op("vector", "tensor_reduce", Cc(2, c), xtile, AX.X, ALU.max)
                op("vector", "tensor_reduce", Cc(3, c), xtile, AX.X, ALU.min)
                op("vector", "tensor_mul", sq_t, xtile, xtile, bar=True)
                p1v[sl] = op("vector", "tensor_reduce", Cc(1, c), sq_t, AX.X,
                             ALU.add, inc=S["vv"], bar=True)
            op("vector", "tensor_scalar", C(4), C(0), INV_D, None,
               op0=ALU.mult, bar=True)
            op("vector", "tensor_mul", C(5), C(4), C(4), bar=True)
            op("vector", "scalar_tensor_tensor", C(6), C(1), INV_D, C(5),
               op0=ALU.mult, op1=ALU.subtract, bar=True)
            v_n = op("vector", "tensor_scalar", C(5), C(6), 1e-5, None,
                     op0=ALU.add, inc=S["vs"], bar=True)
            wt("scalar", S["vs"], v_n)
            s_n = op("scalar", "activation", C(15), C(5), AF.Sqrt,
                     inc=S["sv"])
            wt("vector", S["sv"], s_n)
            op("vector", "reciprocal", C(7), C(15), bar=True)
            op("vector", "tensor_scalar", C(8), C(7), 0.5, 1.0,
               op0=ALU.mult, op1=ALU.add, bar=True)
            op("vector", "tensor_mul", C(5), C(4), C(7), bar=True)
            op("vector", "tensor_scalar", C(9), C(5), -0.5, None,
               op0=ALU.mult, bar=True)
            op("vector", "tensor_mul", C(15), C(6), C(7), bar=True)
            op("vector", "tensor_mul", C(15), C(15), C(7), bar=True)
            v_n = op("vector", "tensor_scalar", C(15), C(15), 1e-5, None,
                     op0=ALU.add, inc=S["vs"], bar=True)
            wt("scalar", S["vs"], v_n)
            s_n = op("scalar", "activation", C(15), C(15), AF.Sqrt,
                     inc=S["sv"])
            wt("vector", S["sv"], s_n)
            op("vector", "reciprocal", C(10), C(15), bar=True)
            op("vector", "tensor_sub", C(2), C(2), C(4), bar=True)
            op("vector", "tensor_sub", C(3), C(4), C(3), bar=True)
            op("vector", "tensor_max", C(2), C(2), C(3), bar=True)
            op("vector", "tensor_mul", C(2), C(2), C(7), bar=True)
            op("vector", "tensor_mul", C(11), C(2), C(10), bar=True)
            op("vector", "tensor_scalar", C(11), C(11), 1e-5, None,
               op0=ALU.max, bar=True)
            op("vector", "reciprocal", C(12), C(11), bar=True)
            op("vector", "tensor_scalar", C(12), C(12), 127.0, None,
               op0=ALU.mult, bar=True)
            op("vector", "tensor_mul", C(12), C(10), C(12), bar=True)
            op("vector", "tensor_mul", C(13), C(7), C(12), bar=True)
            op("vector", "tensor_mul", C(14), C(5), C(12), bar=True)
            op("vector", "tensor_scalar", C(14), C(14), -1.0, None,
               op0=ALU.mult, bar=True)
            v_st = op("vector", "tensor_scalar", rx_sb[:, c0:c0 + ncl], C(11),
                      mwc_bc[:, 0:1], None, op0=ALU.mult, inc=S["vs"],
                      bar=True)
            xa_g = [0] * 2
            xq_g = [0] * 2
            tq_g = [0, 0]
            p3_sv = [0, 0]
            tq2 = [tq, scr_r if resident else scr]
            for j, c in enumerate(cl):
                sl = c % 2
                if resident:
                    xtile = xt_r[j]
                    xa_t = xa_r
                else:
                    xtile = xt[sl]
                    xa_t = xa[sl]
                    if j < 2:
                        wt("sync", S["vv"], p1v[sl])
                    else:
                        wt("sync", S["sv"], p3_sv[sl])
                    n_x = load(S["x"], xtile, x_src[128 * c:128 * (c + 1), :])
                    wt("scalar", S["x"], n_x)
                if j == 0:
                    wt("scalar", S["vs"], v_st)
                wt("scalar", S["stx"], xa_g[0] if resident else xa_g[sl])
                s_xa = op("scalar", "activation", xa_t, xtile, AF.Identity,
                          bias=Cc(9, c), scale=Cc(8, c), inc=S["sg"])
                wt("gpsimd", S["sg"], s_xa)
                xa_g[0 if resident else sl] = store(
                    S["stx"], xa_dst[128 * c:128 * (c + 1), :], xa_t)
                wt("scalar", S["vv"], tq_g[sl])
                s_t = op("scalar", "activation", tq2[sl], xtile, AF.Identity,
                         bias=Cc(14, c), scale=Cc(13, c), inc=S["sv"])
                p3_sv[sl] = s_t
                wt("vector", S["sv"], s_t)
                tq_g[sl] = op("vector", "tensor_copy", i32t, tq2[sl],
                              inc=S["vv"], bar=True)
                wt("vector", S["stq"], xq_g[sl])
                vg_n = op("vector", "tensor_copy", xqbf[sl], i32t, inc=S["vg"])
                wt("gpsimd", S["vg"], vg_n)
                xq_g[sl] = store(S["stq"], xq_d[128 * c:128 * (c + 1), :],
                                 xqbf[sl])
            return None

        P0_PE = S["pe"].n
        wint_st = 0
        p2_xt_g = [p0_xt_g[0], p0_xt_g[1]]
        p2_tq_g = 0
        for i in range(4):
            sl = i % 2
            if i < 2:
                wt("sync", S["vv"], p0_xt_g[sl])
            else:
                wt("sync", S["sv"], p2_xt_g[sl])
            n_w = load(S["x"], xt[sl], wfs_in[128 * i:128 * (i + 1), :])
            wt("scalar", S["x"], n_w)
            if i == 0:
                wt("scalar", S["vs"], WS_READY)
            wt("scalar", S["vv"], p2_tq_g)
            s_n = op("scalar", "activation", tq, xt[sl], AF.Copy,
                     scale=ws_bc[:, 0:1], inc=S["sv"])
            p2_xt_g[sl] = s_n
            wt("vector", S["sv"], s_n)
            p2_tq_g = op("vector", "tensor_copy", i32t, tq, inc=S["vv"],
                         bar=True)
            wt("vector", S["stw"], wint_st)
            vg_n = op("vector", "tensor_scalar", xqbf[sl], i32t, 1.0, -1.0,
                      op0=ALU.min, op1=ALU.max, inc=S["vg"])
            wt("gpsimd", S["vg"], vg_n)
            wint_st = store(S["stw"], wint_i[128 * i:128 * (i + 1), :],
                            xqbf[sl])
        P2A_SV = S["sv"].n
        P2A_VV = S["vv"].n
        P2A_VG = S["vg"].n
        wtT = [R(60, BF16, ES), R(68, BF16, ES)]
        wt_stl = []
        for k in range(32):
            sl = k % 2
            wt("sync", S["stw"], wint_st if k < 2 else wt_stl[k - 2])
            n_l = load(WR[sl], wtT[sl], wint_i[0:ES, 128 * k:128 * (k + 1)],
                       transpose=True, serial=False)
            wt("gpsimd", WR[sl], n_l)
            wt_stl.append(store(S["stw"], wintT_i[128 * k:128 * (k + 1), :],
                                wtT[sl]))
        wt("gpsimd", S["stw"], wt_stl[-1])
        cc_wint = cc("AllGather", ALU.bypass, wintT_i, wintT_o)
        wintT_flat = wintT_o.rearrange("r p d -> (r p) d")
        P2_STW = S["stw"].n

        g_p1 = [(S["vs"], WS_READY), (S["sv"], P2A_SV), (S["vv"], P2A_VV),
                (S["vg"], P2A_VG), (S["stw"], wint_st), (S["pe"], P0_PE)]
        ln_quant_phase(x_in, xA, rx1, g_p1)
        P1_STX = S["stx"].n
        P1_STQ = S["stq"].n
        P1_SV = S["sv"].n
        P1_VG = S["vg"].n

        wring_g = [0] * 8
        widx = [0]
        xr_g = [0] * 4
        xo_g = {"stx": [0, 0], "out": [0, 0]}

        def ffn_like_phase(dst_sem_name, lhsT_sb, rhs_src_fn, rhs_transpose,
                           res_src, dst, epi_fn, guards_sync, guards_tensor,
                           per_k_guards=None, lhs_k_guards=None,
                           ts_list=(0, 1), og_list=None, guards_issue=()):
            st_sem = S[dst_sem_name]
            bank_g = [S["vs"].n] * 2
            for gs, gv in guards_sync:
                wt("sync", gs, gv)
            for gs, gv in guards_tensor:
                wt("tensor", gs, gv)
            ts0 = list(ts_list)[0]
            ogl = list(og_list) if og_list is not None else list(range(8))
            og0 = ogl[0]
            steps = [(ts, og, k) for ts in ts_list for og in ogl
                     for k in range(32)]
            w0 = widx[0]
            PE0 = S["pe"].n
            n_ws = [0] * len(steps)

            def issue_step(j):
                ts_, og_, k_ = steps[j]
                if j >= 8:
                    wt("scalar", S["pe"], PE0 + j - 7)
                n_ws[j] = load(WR[(w0 + j) % 8], wring[(w0 + j) % 8],
                               rhs_src_fn(og_, k_), serial=False,
                               queue="scalar")

            wt("scalar", S["pe"], PE0)
            for gs, gv in guards_tensor:
                wt("scalar", gs, gv)
            for gs, gv in guards_issue:
                wt("scalar", gs, gv)
            for j in range(min(8, len(steps))):
                issue_step(j)
            for j, (ts, og, k) in enumerate(steps):
                sl = (w0 + j) % 8
                bg = og % 2
                if k == 0:
                    wt("tensor", S["vs"], bank_g[bg])
                if lhs_k_guards is not None and ts == ts0 and og == og0:
                    wt("tensor", *lhs_k_guards[k])
                wt("tensor", WR[sl], n_ws[j])
                for tt in range(4):
                    t0 = 1024 * k + 512 * ts + 128 * tt
                    pe_group = mm(PS8[4 * bg + tt],
                                  lhsT_sb[:, t0:t0 + 128],
                                  wring[sl][:], k == 0, k == 31,
                                  inc=S["pe"] if tt == 3 else None)
                widx[0] += 1
                if j + 8 < len(steps):
                    issue_step(j + 8)
                if k == 31:
                    last_vs = 0
                    for tt in range(4):
                        tg = 4 * ts + tt
                        rows = slice(128 * tg, 128 * (tg + 1))
                        cols = slice(512 * og, 512 * (og + 1))
                        xsl = (2 * og + tt) % 4
                        wt("sync", S["vg"], xr_g[xsl])
                        n_xr = load(S["x"], xr[xsl], res_src[rows, cols])
                        last_vs = epi_fn(bg, tt, tg, rows, cols, xsl, n_xr,
                                         pe_group, st_sem, dst)
                    bank_g[bg] = last_vs

        def make_ffn_epi(rx_sb, st_sem_name):
            g_guard = [0, 0]

            def epi(bg, tt, tg, rows, cols, xsl, n_xr, pe_group, st_sem, dst):
                ysl = tt % 2
                wt("vector", S["pe"], pe_group)
                wt("vector", S["sv"], g_guard[ysl])
                s_n = op("vector", "scalar_tensor_tensor", yt_t[ysl],
                         PS8[4 * bg + tt], rx_sb[:, tg:tg + 1], bf_bc[:, cols],
                         op0=ALU.mult, op1=ALU.add, inc=S["vs"])
                wt("scalar", S["vs"], s_n)
                s_g = op("scalar", "activation", g_t[ysl], yt_t[ysl], AF.Gelu,
                         inc=S["sv"])
                g_guard[ysl] = s_g
                osl = tt % 2
                wt("vector", S["sv"], s_g)
                wt("vector", S["x"], n_xr)
                wt("vector", st_sem, xo_g[st_sem_name][osl])
                vg_n = op("vector", "scalar_tensor_tensor", xo[osl], g_t[ysl],
                          0.5, xr[xsl], op0=ALU.mult, op1=ALU.add, inc=S["vg"])
                xr_g[xsl] = vg_n
                wt("gpsimd", S["vg"], vg_n)
                xo_g[st_sem_name][osl] = store(st_sem, dst[rows, cols],
                                               xo[osl])
                return s_n
            return epi

        def ffn_phase(rx_sb, res_src, dst, st_sem_name, guards):
            for gs, gv in guards:
                wt("sync", gs, gv)
            xqt_k = []
            for k in range(32):
                xqt_k.append((HR[k % 4],
                              load(HR[k % 4], xqT[:, 1024 * k:1024 * (k + 1)],
                                   xq_d[0:TPC, 128 * k:128 * (k + 1)],
                                   transpose=True, serial=True)))
            ffn_like_phase(
                st_sem_name, xqT,
                lambda og, k: wintT_flat[4096 * og + 128 * k:
                                         4096 * og + 128 * (k + 1), :],
                False, res_src, dst, make_ffn_epi(rx_sb, st_sem_name),
                [], [(S["cc"], cc_wint)], lhs_k_guards=xqt_k,
                guards_issue=guards)

        if stage >= 3:
            ffn_phase(rx1, xA, xB, "stx",
                      [(S["stq"], P1_STQ), (S["stx"], P1_STX)])
        P3_PE = S["pe"].n
        P3_STX = S["stx"].n
        P3_SV = S["sv"].n
        P3_VG = S["vg"].n

        cc_h2 = [0, 0]
        P4_STH = 0
        if stage >= 5:
            h2_g = [0] * 2
            CBc = lambda bk, q: stw_t[:, 8 * bk + q:8 * bk + q + 1]
            xt_r4 = [R(28, F32, D), R(44, F32, D), R(60, F32, D),
                     R(76, F32, D)]
            scr4 = R(92, F32, D)
            p4_sg_last = 0
            for half in range(2):
                cls_ = range(4 * half, 4 * half + 4)
                cb0 = 4 * half
                CB = lambda bk: stw_t[:, 8 * bk + cb0:8 * bk + cb0 + 4]
                for j, c in enumerate(cls_):
                    if j == 0:
                        if half == 0:
                            wt("sync", S["pe"], P3_PE)
                            wt("sync", S["stx"], P3_STX)
                            wt("sync", S["sv"], S["sv"].n)
                            wt("sync", S["vg"], S["vg"].n)
                        else:
                            wt("sync", S["sg"], p4_sg_last)
                    n_x = load(S["x"], xt_r4[j], xB[128 * c:128 * (c + 1), :])
                    wt("vector", S["x"], n_x)
                    op("vector", "tensor_reduce", CBc(0, c), xt_r4[j], AX.X,
                       ALU.add)
                    op("vector", "tensor_mul", scr4, xt_r4[j], xt_r4[j],
                       bar=True)
                    op("vector", "tensor_reduce", CBc(1, c), scr4, AX.X,
                       ALU.add, bar=True)
                op("vector", "tensor_scalar", CB(4), CB(0), INV_D, None,
                   op0=ALU.mult, bar=True)
                op("vector", "tensor_mul", CB(5), CB(4), CB(4), bar=True)
                op("vector", "scalar_tensor_tensor", CB(6), CB(1), INV_D,
                   CB(5), op0=ALU.mult, op1=ALU.subtract, bar=True)
                v_n = op("vector", "tensor_scalar", CB(5), CB(6), 1e-5, None,
                         op0=ALU.add, inc=S["vs"], bar=True)
                wt("scalar", S["vs"], v_n)
                s_n = op("scalar", "activation", CB(3), CB(5), AF.Sqrt,
                         inc=S["sv"])
                wt("vector", S["sv"], s_n)
                op("vector", "reciprocal", CB(7), CB(3), bar=True)
                v_st4 = op("vector", "scalar_tensor_tensor", CB(9), CB(4),
                           -1.0, CB(7), op0=ALU.mult, op1=ALU.mult,
                           inc=S["vs"], bar=True)
                for j, c in enumerate(cls_):
                    sl = c % 2
                    if j == 0:
                        wt("scalar", S["vs"], v_st4)
                    wt("scalar", S["sth"], h2_g[sl])
                    sg_n = op("scalar", "activation", h2st[sl], xt_r4[j],
                              AF.Identity, bias=CBc(9, c), scale=CBc(7, c),
                              inc=S["sg"])
                    p4_sg_last = sg_n
                    wt("gpsimd", S["sg"], sg_n)
                    h2_g[sl] = store(
                        S["sth"],
                        h2_i[half][128 * (c % 4):128 * (c % 4) + 128, :],
                        h2st[sl])
                wt("gpsimd", S["sth"], S["sth"].n)
                cc_h2[half] = cc("AllGather", ALU.bypass, h2_i[half],
                                 h2_o[half])
            P4_STH = S["sth"].n

        hring_g = [0] * 8
        qn_g = [0] * 2
        at_g = [0] * 2
        cc_at = [0, 0]
        if stage >= 5:
            Ehalf_g = [0, 0]
            pair_g = [0, 0]
            pl_g = 0
            pavrb_g = 0
            rank1_g = 0
            for b in range(1 if dump == 24 else B):
                wt("sync", S["pe"], S["pe"].n)
                wt("sync", S["sf"], S["sf"].n)
                wt("sync", S["sv"], S["sv"].n)
                wt("sync", S["vg"], S["vg"].n)
                wt("sync", S["stq"], st_wqkv if b == 0 else S["stq"].n)
                if b == 0 and stage >= 6:
                    wt("sync", S["sth"], P4_STH)
                n_wqkv = 0
                for (w_sb, off) in ((wq_sb, 0), (wk_sb, ES), (wv_sb, 2 * ES)):
                    for k in range(32):
                        n_wqkv = load(S["wqkv"], w_sb[:, ES * k:ES * (k + 1)],
                                      wqkv_d[128 * k:128 * (k + 1),
                                             off:off + ES], serial=False)
                wt("tensor", S["wqkv"], n_wqkv)
                wt("sync", S["cc"], cc_h2[b])
                if b == 0:
                    wt("tensor", S["cc"], cc_h2[1])

                qkv_bank_g = S["vs"].n
                v_bank_g = S["vs"].n
                qkv_sq_g = S["sv"].n
                for tc in range(8):
                    r = tc
                    Xh = b
                    wt("tensor", S["vs"], qkv_bank_g)
                    pe_qk = 0
                    for k in range(32):
                        sl = k % 8
                        wt("sync", S["pe"], hring_g[sl])
                        n_h = load(HR[sl], hring[sl],
                                   h2_o[Xh][r][0:512, 128 * k:128 * (k + 1)],
                                   transpose=True, serial=False)
                        wt("tensor", HR[sl], n_h)
                        for tt in range(4):
                            mm(PS8[tt], hring[sl][:, 128 * tt:128 * (tt + 1)],
                               wq_sb[:, ES * k:ES * (k + 1)], k == 0, k == 31)
                        for tt in range(4):
                            pe_qk = mm(PS8[4 + tt],
                                       hring[sl][:, 128 * tt:128 * (tt + 1)],
                                       wk_sb[:, ES * k:ES * (k + 1)],
                                       k == 0, k == 31,
                                       inc=S["pe"] if tt == 3 else None)
                        hring_g[sl] = pe_qk
                    vs_qk = 0
                    vs_q4 = 0
                    sq_guard = qkv_sq_g
                    for qk in range(2):
                        bias_bc = bq_bc if qk == 0 else bk_bc
                        g_bc = gq_bc if qk == 0 else gk_bc
                        dstd = qn_d if qk == 0 else kn_d
                        for tt in range(4):
                            bank = PS8[4 * qk + tt]
                            wt("vector", S["pe"], pe_qk)
                            wt("vector", S["sv"], sq_guard)
                            s_n = op("vector", "scalar_tensor_tensor", qtmp,
                                     bank, 1.0, bias_bc, op0=ALU.mult,
                                     op1=ALU.add, inc=S["vs"], bar=True)
                            vs_qk = s_n
                            wt("scalar", S["vs"], s_n)
                            for hh in range(4):
                                cseg = slice(128 * hh, 128 * (hh + 1))
                                sq_guard = op(
                                    "scalar", "activation", sqq[:, cseg],
                                    qtmp[:, cseg], AF.Square,
                                    accum_out=st3[:, hh:hh + 1],
                                    inc=S["sv"] if hh == 3 else None)
                            wt("vector", S["sv"], sq_guard)
                            v_n = op("vector", "tensor_scalar", st3[:, 0:4],
                                     st3[:, 0:4], float(1.0 / HD), 1e-6,
                                     op0=ALU.mult, op1=ALU.add, inc=S["vs"],
                                     bar=True)
                            wt("scalar", S["vs"], v_n)
                            s_n = op("scalar", "activation", st3[:, 4:8],
                                     st3[:, 0:4], AF.Sqrt, inc=S["sv"])
                            wt("vector", S["sv"], s_n)
                            op("vector", "reciprocal", st3[:, 8:12],
                               st3[:, 4:8], bar=True)
                            qsl = (2 * qk + tt) % 2
                            wt("vector", S["stq"], qn_g[qsl])
                            vg_n = 0
                            for hh in range(4):
                                cseg = slice(128 * hh, 128 * (hh + 1))
                                vg_n = op("vector", "scalar_tensor_tensor",
                                          qnst[qsl][:, cseg], qtmp[:, cseg],
                                          st3[:, 8 + hh:9 + hh], g_bc,
                                          op0=ALU.mult, op1=ALU.mult,
                                          inc=S["vg"] if hh == 3 else None)
                            wt("gpsimd", S["vg"], vg_n)
                            trow = 512 * tc + 128 * tt
                            qn_g[qsl] = store(S["stq"],
                                              dstd[trow:trow + 128, :],
                                              qnst[qsl])
                        if qk == 0:
                            vs_q4 = vs_qk
                    qkv_sq_g = sq_guard
                    wt("tensor", S["vs"], vs_q4)
                    pe_v = 0
                    for k in range(32):
                        sl = k % 8
                        wt("sync", S["pe"], hring_g[sl])
                        n_h = load(HR[sl], hring[sl],
                                   h2_o[Xh][r][0:512, 128 * k:128 * (k + 1)],
                                   transpose=True, serial=False)
                        wt("tensor", HR[sl], n_h)
                        for tt in range(4):
                            pe_v = mm(PS8[tt],
                                      hring[sl][:, 128 * tt:128 * (tt + 1)],
                                      wv_sb[:, ES * k:ES * (k + 1)],
                                      k == 0, k == 31,
                                      inc=S["pe"] if tt == 3 else None)
                        hring_g[sl] = pe_v
                    for tt in range(4):
                        wt("vector", S["pe"], pe_v)
                        cseg = slice(512 * (4 * tc + tt),
                                     512 * (4 * tc + tt + 1))
                        qkv_bank_g = op("vector", "scalar_tensor_tensor",
                                        V_sb[:, cseg], PS8[tt], 1.0, bv_bc,
                                        op0=ALU.mult, op1=ALU.add, inc=S["vs"])
                QKV_STQ = S["stq"].n
                QKV_PE = S["pe"].n
                QKV_VS = S["vs"].n

                wt("sync", S["stq"], QKV_STQ)
                wt("sync", S["pe"], QKV_PE)
                n_qt = 0
                for (dsrc, dsb) in ((qn_d, qT), (kn_d, kT)):
                    for h in range(HPC):
                        n_qt = load(S["t"], dsb[:, N * h:N * (h + 1)],
                                    dsrc[0:N, 128 * h:128 * (h + 1)],
                                    transpose=True, serial=False)
                wt("tensor", S["t"], n_qt)
                wt("tensor", S["vs"], QKV_VS)

                for h in range(HPC):
                    for qc in range(8):
                        qcols = slice(N * h + 512 * qc,
                                      N * h + 512 * (qc + 1))
                        av_n = 0
                        for half in range(2):
                            exp_n = 0
                            for g_ in range(8):
                                pr = g_ % 2
                                ps_pair = ps2a if pr == 0 else ps2b
                                wt("tensor", S["sf"], pair_g[pr])
                                pe_n = 0
                                for j in range(2):
                                    kt = 16 * half + 2 * g_ + j
                                    kcol = slice(N * h + 128 * kt,
                                                 N * h + 128 * (kt + 1))
                                    pe_n = mm(
                                        ps_pair.ap()[:, 512 * j:512 * (j + 1)],
                                        kT[:, kcol], qT[:, qcols], True, True,
                                        inc=S["pe"] if j == 1 else None)
                                wt("scalar", S["pe"], pe_n)
                                wt("scalar", S["pe"], Ehalf_g[half])
                                ecol = slice(1024 * g_, 1024 * (g_ + 1))
                                exp_n = op("scalar", "activation",
                                           Eb[half][:, ecol], ps_pair.ap()[:],
                                           AF.Exp, inc=S["sf"])
                                pair_g[pr] = exp_n
                            wt("tensor", S["sf"], exp_n)
                            if half == 0:
                                wt("tensor", S["vg"], pavrb_g)
                                wt("tensor", S["vs"], pl_g)
                            for kt16 in range(16):
                                kt = 16 * half + kt16
                                erhs = Eb[half][:, 512 * kt16:512 * (kt16 + 1)]
                                vcol = slice(512 * kt + 128 * h,
                                             512 * kt + 128 * (h + 1))
                                mm(pav.ap()[:], V_sb[:, vcol], erhs,
                                   kt == 0, kt == 31)
                                av_n = mm(pl.ap()[0:1, :], ones_bf[:, 0:1],
                                          erhs, kt == 0, kt == 31,
                                          inc=S["pe"] if kt16 == 15 else None)
                            Ehalf_g[half] = av_n
                        wt("vector", S["pe"], av_n)
                        wt("vector", S["pe"], rank1_g)
                        pl_g = op("vector", "reciprocal", lrow[0:1, :],
                                  pl.ap()[0:1, :], inc=S["vs"])
                        wt("tensor", S["vs"], pl_g)
                        rank1_g = mm(prb.ap()[:], onesr_f[0:1, :],
                                     lrow[0:1, :], True, True, inc=S["pe"])
                        asl = qc % 2
                        wt("vector", S["pe"], rank1_g)
                        op("vector", "tensor_copy", rbc_sb, prb.ap()[:],
                           bar=True)
                        wt("vector", S["sth"], at_g[asl])
                        vg_n = op("vector", "tensor_mul", atst[asl],
                                  pav.ap()[:], rbc_sb, inc=S["vg"])
                        pavrb_g = vg_n
                        wt("gpsimd", S["vg"], vg_n)
                        gcol = 4096 * b + 512 * qc
                        at_g[asl] = store(
                            S["sth"],
                            at_d[128 * h:128 * (h + 1), gcol:gcol + 512],
                            atst[asl])

        cc_rs = [0, 0]
        if stage >= 6:
            wt("sync", S["sth"], S["sth"].n)
            wt("sync", S["pe"], S["pe"].n)
            wt("sync", S["sf"], S["sf"].n)
            wt("sync", S["vg"], S["vg"].n)
            wt("sync", S["stw"], st_wom)
            n_at = 0
            for e in range(4):
                n_at = load(S["t"], at4[e], at_d[128 * e:128 * (e + 1), :],
                            serial=False)
            n_wo = 0
            for e in range(4):
                n_wo = load(S["wqkv"], wo_sb[:, D * e:D * (e + 1)],
                            wom_d[128 * e:128 * (e + 1), :], serial=False)
            wt("tensor", S["t"], n_at)
            wt("tensor", S["wqkv"], n_wo)
            wt("tensor", S["vs"], S["vs"].n)
            rnd_g = [S["vs"].n] * 2
            stg_g = [0, 0]
            rs_st = 0
            for T in range(64):
                ssl = T % 2
                vs_cp = 0
                for rnd in range(2):
                    gr = (2 * T + rnd) % 2
                    wt("tensor", S["vs"], rnd_g[gr])
                    pe_n = 0
                    for e in range(4):
                        for oc in range(4):
                            o = 4 * rnd + oc
                            pe_n = mm(PS8[4 * gr + oc],
                                      at4[e][:, 128 * T:128 * (T + 1)],
                                      wo_sb[:, D * e + 512 * o:D * e + 512 * (o + 1)],
                                      e == 0, e == 3,
                                      inc=S["pe"] if oc == 3 else None)
                    for oc in range(4):
                        o = 4 * rnd + oc
                        wt("vector", S["pe"], pe_n)
                        if rnd == 0 and oc == 0:
                            wt("vector", S["stx"], stg_g[ssl])
                        vs_cp = op("vector", "tensor_copy",
                                   stg6[ssl][:, 512 * o:512 * (o + 1)],
                                   PS8[4 * gr + oc], inc=S["vs"])
                    rnd_g[gr] = vs_cp
                wt("gpsimd", S["vs"], vs_cp)
                stg_g[ssl] = store(S["stx"], rs_i[128 * T:128 * (T + 1), :],
                                   stg6[ssl])
                if T == 31:
                    wt("gpsimd", S["stx"], S["stx"].n)
                    cc_rs[0] = cc("ReduceScatter", ALU.add,
                                  rs_i[0:4096, :], rs_o[0:512, :])
            P6A_STX_RS = S["stx"].n
            P6A_PE = S["pe"].n

            wint8_rhs = lambda og, k: wintT_flat[4096 * og + 128 * k:
                                                 4096 * og + 128 * (k + 1), :]
            wt("sync", S["cc"], cc_rs[0])
            x3_st_g = 0
            for j in range(8):
                c2, hf = divmod(j, 2)
                rows = slice(128 * c2, 128 * (c2 + 1))
                cols = slice(2048 * hf, 2048 * (hf + 1))
                wt("sync", S["vg"], S["vg"].n)
                n_r = load(S["x"], rsb_t, rs_o[rows, cols])
                n_x2 = load(S["x"], xbt_t, xB[rows, cols])
                wt("vector", S["x"], n_x2)
                wt("vector", S["stx"], x3_st_g)
                op("vector", "scalar_tensor_tensor", x3st, rsb_t, 1.0,
                   xbt_t, op0=ALU.mult, op1=ALU.add, bar=True)
                vg_n = op("vector", "tensor_add", x3st, x3st,
                          bo_bc[:, cols], inc=S["vg"])
                wt("gpsimd", S["vg"], vg_n)
                x3_st_g = store(S["stx"], xA[rows, cols], x3st)
            wt("gpsimd", S["stx"], P6A_STX_RS)
            wt("gpsimd", S["x"], S["x"].n)
            cc_rs[1] = cc("ReduceScatter", ALU.add,
                          rs_i[4096:8192, :], rs_o[512:1024, :])
            P6B0_STX = S["stx"].n
            if stage >= 8:
                g_p7a = [(S["stx"], P6B0_STX), (S["pe"], P6A_PE),
                         (S["sv"], S["sv"].n), (S["vg"], S["vg"].n),
                         (S["stq"], S["stq"].n), (S["stw"], S["stw"].n),
                         (S["sth"], S["sth"].n), (S["sf"], S["sf"].n)]
                ln_quant_phase(xA, xB, rx2, g_p7a, chunks=range(0, 4))
                epi8 = make_ffn_epi(rx2, "out")
                g_p8a = [(S["stq"], S["stq"].n), (S["stx"], S["stx"].n)]
                for gs, gv in g_p8a:
                    wt("sync", gs, gv)
                xqt_a = []
                for k in range(32):
                    xqt_a.append((HR[k % 4],
                                  load(HR[k % 4],
                                       xqT[:, 1024 * k:1024 * k + 512],
                                       xq_d[0:512, 128 * k:128 * (k + 1)],
                                       transpose=True, serial=True)))
                ffn_like_phase("out", xqT, wint8_rhs, False, xB, out_t,
                               epi8, [], [], lhs_k_guards=xqt_a,
                               ts_list=(0,), guards_issue=g_p8a)
            wt("sync", S["cc"], cc_rs[1])
            wt("sync", S["stq"], S["stq"].n)
            x3_st_g1 = 0
            for c2 in range(4):
                rows = slice(512 + 128 * c2, 512 + 128 * (c2 + 1))
                for hf in range(2):
                    cols = slice(2048 * hf, 2048 * (hf + 1))
                    wt("sync", S["vg"], S["vg"].n)
                    n_r = load(S["x"], rsb_t, rs_o[rows, cols])
                    n_x2 = load(S["x"], xbt_t, xB[rows, cols])
                    wt("vector", S["x"], n_x2)
                    wt("vector", S["stx"], x3_st_g1)
                    op("vector", "scalar_tensor_tensor", x3st, rsb_t, 1.0,
                       xbt_t, op0=ALU.mult, op1=ALU.add, bar=True)
                    vg_n = op("vector", "tensor_add", x3st, x3st,
                              bo_bc[:, cols], inc=S["vg"])
                    wt("gpsimd", S["vg"], vg_n)
                    x3_st_g1 = store(S["stx"], xA[rows, cols], x3st)
        P6_STX = S["stx"].n
        P6_PE = S["pe"].n

        if stage >= 8:
            g_p7b = [(S["stx"], P6_STX), (S["pe"], P6_PE),
                     (S["sv"], S["sv"].n), (S["vg"], S["vg"].n),
                     (S["stq"], S["stq"].n), (S["stw"], S["stw"].n),
                     (S["sth"], S["sth"].n), (S["sf"], S["sf"].n)]
            ln_quant_phase(xA, xB, rx2, g_p7b, chunks=range(4, 8))
            g_p8b = [(S["stq"], S["stq"].n), (S["stx"], S["stx"].n)]
            for gs, gv in g_p8b:
                wt("sync", gs, gv)
            xqt_b = []
            for k in range(32):
                xqt_b.append((HR[k % 4],
                              load(HR[k % 4],
                                   xqT[:, 1024 * k + 512:1024 * (k + 1)],
                                   xq_d[512:1024, 128 * k:128 * (k + 1)],
                                   transpose=True, serial=True)))
            ffn_like_phase("out", xqT, wint8_rhs, False, xB, out_t, epi8,
                           [], [], lhs_k_guards=xqt_b, ts_list=(1,),
                           guards_issue=g_p8b)

        if dump:
            for nm in ["stx", "stq", "sth", "stw", "cc", "vg", "sg"]:
                wt("gpsimd", S[nm], S[nm].n)
            if dump == 21:
                store(S["out"], out_t[:, 0:D // 2].bitcast(BF16), xq_d)
            elif dump == 22:
                store(S["out"], out_t[0:512, 0:D // 2].bitcast(BF16), wint_i)
            elif dump == 23:
                store(S["out"], out_t[0:512, 0:D // 2].bitcast(BF16)
                      .rearrange("p (a b) -> p a b", a=2)
                      .rearrange("p a b -> (p a) b") if False else
                      out_t.bitcast(BF16)[0:512, 0:B * N], at_d)
            elif dump == 24:
                store(S["out"], out_t.bitcast(BF16)[0:1024, 0:ES],
                      qn_d[1280:2304])
        elif stage < 8:
            src = {1: xA, 3: xB, 6: xA}[stage]
            for nm in ["stx", "stq", "sth", "stw", "cc", "vg", "sg"]:
                wt("gpsimd", S[nm], S[nm].n)
            store(S["out"], out_t, src)

        wt("sync", S["out"], S["out"].n)
        wt("gpsimd", S["out"], S["out"].n)

        block = es.enter_context(nc.Block())
        @block.sync
        def _(e):
            for fn in prog["sync"]:
                fn(e)

        @block.gpsimd
        def _(e):
            for fn in prog["gpsimd"]:
                fn(e)

        @block.vector
        def _(e):
            for fn in prog["vector"]:
                fn(e)

        @block.scalar
        def _(e):
            for fn in prog["scalar"]:
                fn(e)

        @block.tensor
        def _(e):
            for fn in prog["tensor"]:
                fn(e)

    return nc


def kernel(L0_attn_hidden_states, L0_attn_Wq, L0_attn_bq, L0_attn_Wk,
           L0_attn_bk, L0_attn_Wv, L0_attn_bv, L0_attn_Wo, L0_attn_bo,
           L0_attn_gq, L0_attn_gk, L0_ffn_W, L0_ffn_b):
    x = np.ascontiguousarray(
        np.asarray(L0_attn_hidden_states, dtype=np.float32).reshape(B * N, D))
    Wq = np.asarray(L0_attn_Wq, np.float32)
    Wk = np.asarray(L0_attn_Wk, np.float32)
    Wv = np.asarray(L0_attn_Wv, np.float32)
    Wo = np.asarray(L0_attn_Wo, np.float32)
    Wf = np.asarray(L0_ffn_W, np.float32)
    bq = np.asarray(L0_attn_bq, np.float32).reshape(1, D)
    bk = np.asarray(L0_attn_bk, np.float32).reshape(1, D)
    bv = np.asarray(L0_attn_bv, np.float32).reshape(1, D)
    bo = np.asarray(L0_attn_bo, np.float32).reshape(1, D)
    bf_ = np.asarray(L0_ffn_b, np.float32).reshape(1, D)
    gq = np.asarray(L0_attn_gq, np.float32).reshape(1, HD)
    gk = np.asarray(L0_attn_gk, np.float32).reshape(1, HD)

    in_maps = []
    for c in range(NC):
        esl = slice(ES * c, ES * (c + 1))
        myrows = np.r_[512 * c:512 * (c + 1), N + 512 * c:N + 512 * (c + 1)]
        in_maps.append({
            "x_my": np.ascontiguousarray(x[myrows]),
            "wq_my": np.ascontiguousarray(Wq[:, esl]),
            "wk_my": np.ascontiguousarray(Wk[:, esl]),
            "wv_my": np.ascontiguousarray(Wv[:, esl]),
            "bq_my": np.ascontiguousarray(bq[:, esl]),
            "bk_my": np.ascontiguousarray(bk[:, esl]),
            "bv_my": np.ascontiguousarray(bv[:, esl]),
            "gq": gq, "gk": gk,
            "wfs_my": np.ascontiguousarray(Wf[esl, :]),
            "wos_my": np.ascontiguousarray(Wo[esl, :]),
            "bf": bf_, "bo": bo,
        })
    nc = build_nc(_STAGE)
    res = run_bass_kernel_spmd(nc, in_maps, core_ids=list(range(NC)))
    out = np.empty((B * N, D), np.float32)
    for c in range(NC):
        oc = res.results[c]["out"]
        out[512 * c:512 * (c + 1)] = oc[0:512]
        out[N + 512 * c:N + 512 * (c + 1)] = oc[512:1024]
    return out.reshape(B, N, D)


# version 16
# speedup vs baseline: 1.0789x; 1.0405x over previous
import numpy as np
import concourse.bass as bass
import concourse.mybir as mybir
from concourse.bass_utils import run_bass_kernel_spmd

F32 = mybir.dt.float32
BF16 = mybir.dt.bfloat16
I32 = mybir.dt.int32
AF = mybir.ActivationFunctionType
ALU = mybir.AluOpType
AX = mybir.AxisListType

D = 4096
B = 2
N = 4096
TPC = 1024
NC = 8
HD = 128
HPC = 4
ES = HPC * HD
NCH = TPC // 128
RG = [[0, 1, 2, 3, 4, 5, 6, 7]]
ALPHA = float(1.0 / np.sqrt(HD))
INV_D = float(1.0 / D)
MEAN_W = float(2.0 ** -24)

_STAGE = 8


class Sem:
    def __init__(self, h):
        self.h = h
        self.n = 0


def build_nc(stage):
    dump = stage if stage in (21, 22, 23, 24) else None
    if dump in (21, 22):
        stage = 2
    elif dump in (23, 24):
        stage = 5
    nc = bass.Bass("TRN2", target_bir_lowering=False, debug=False)

    x_in = nc.dram_tensor("x_my", [TPC, D], F32, kind="ExternalInput").ap()
    wq_in = nc.dram_tensor("wq_my", [D, ES], F32, kind="ExternalInput").ap()
    wk_in = nc.dram_tensor("wk_my", [D, ES], F32, kind="ExternalInput").ap()
    wv_in = nc.dram_tensor("wv_my", [D, ES], F32, kind="ExternalInput").ap()
    bq_in = nc.dram_tensor("bq_my", [1, ES], F32, kind="ExternalInput").ap()
    bk_in = nc.dram_tensor("bk_my", [1, ES], F32, kind="ExternalInput").ap()
    bv_in = nc.dram_tensor("bv_my", [1, ES], F32, kind="ExternalInput").ap()
    gq_in = nc.dram_tensor("gq", [1, HD], F32, kind="ExternalInput").ap()
    gk_in = nc.dram_tensor("gk", [1, HD], F32, kind="ExternalInput").ap()
    wfs_in = nc.dram_tensor("wfs_my", [ES, D], F32, kind="ExternalInput").ap()
    wos_in = nc.dram_tensor("wos_my", [ES, D], F32, kind="ExternalInput").ap()
    bf_in = nc.dram_tensor("bf", [1, D], F32, kind="ExternalInput").ap()
    bo_in = nc.dram_tensor("bo", [1, D], F32, kind="ExternalInput").ap()
    out_t = nc.dram_tensor("out", [TPC, D], F32, kind="ExternalOutput").ap()

    xA = nc.dram_tensor("xA", [TPC, D], F32, kind="Internal").ap()
    xB = nc.dram_tensor("xB", [TPC, D], F32, kind="Internal").ap()
    xq_d = nc.dram_tensor("xq_d", [TPC, D], BF16, kind="Internal").ap()
    qn_d = nc.dram_tensor("qn_d", [N, ES], BF16, kind="Internal").ap()
    kn_d = nc.dram_tensor("kn_d", [N, ES], BF16, kind="Internal").ap()
    wqkv_d = nc.dram_tensor("wqkv_d", [D, 3 * ES], BF16, kind="Internal").ap()

    ws_i = nc.dram_tensor("ws_i", [1, 8], F32, kind="Internal").ap()
    ws_o = nc.dram_tensor("ws_o", [1, 8], F32, kind="Internal").ap()
    wint_i = nc.dram_tensor("wint_i", [ES, D], BF16, kind="Internal").ap()
    wintT_i = nc.dram_tensor("wintT_i", [D, ES], BF16, kind="Internal").ap()
    wintT_o = nc.dram_tensor("wintT_o", [NC, D, ES], BF16,
                             kind="Internal").ap()
    wom_d = nc.dram_tensor("wom_d", [ES, D], BF16, kind="Internal").ap()
    h2_i = [nc.dram_tensor(f"h2_i{j}", [512, D], BF16, kind="Internal").ap()
            for j in range(2)]
    h2_o = [nc.dram_tensor(f"h2_o{j}", [NC, 512, D], BF16, kind="Internal").ap()
            for j in range(2)]
    at_d = nc.dram_tensor("at_d", [ES, B * N], BF16, kind="Internal").ap()
    rs_i = nc.dram_tensor("rs_i", [B * N, D], BF16, kind="Internal").ap()
    rs_o = nc.dram_tensor("rs_o", [TPC, D], BF16, kind="Internal").ap()

    from contextlib import ExitStack
    with ExitStack() as es:
        ARENA_KB = 176
        arena_t = es.enter_context(
            nc.sbuf_tensor("arena", [128, ARENA_KB * 1024], mybir.dt.uint8))
        _ar = arena_t.ap()

        _pers = [0]

        def P(dt, cols):
            sz = cols * {F32: 4, BF16: 2, I32: 4}[dt]
            off = (_pers[0] + 63) // 64 * 64
            _pers[0] = off + sz
            assert _pers[0] <= 28 * 1024
            return _ar[:, off:off + sz].bitcast(dt)

        def R(off_kb, dt, cols):
            off = int(off_kb * 1024)
            sz = cols * {F32: 4, BF16: 2, I32: 4}[dt]
            assert 28 * 1024 <= off and off + sz <= ARENA_KB * 1024
            return _ar[:, off:off + sz].bitcast(dt)

        bo_bc = P(BF16, D)
        bf_bc = P(BF16, D)
        bq_bc = P(F32, ES)
        bk_bc = P(F32, ES)
        bv_bc = P(F32, ES)
        gq_bc = P(F32, HD)
        gk_bc = P(F32, HD)
        ones_bf = P(BF16, 32)
        onesr_f = P(F32, 128)
        onescol = P(F32, 16)
        rx1 = P(F32, 16)
        rx2 = P(F32, 16)
        wsrow = P(F32, 16)
        ws_bc = P(F32, 16)
        mwc_bc = P(F32, 16)
        lrow = P(F32, 512)
        st1 = P(F32, 16)
        st2 = P(F32, 16)
        st3 = P(F32, 16)
        stp = [P(F32, 32), P(F32, 32)]
        stw_t = P(F32, 256)
        st3b = [P(F32, 16), P(F32, 16)]


        xt = [R(28, F32, D), R(44, F32, D)]
        lnt = R(60, F32, D)
        scr = R(76, F32, D)
        xa = [R(92, F32, D), R(108, F32, D)]
        tq = R(124, F32, D)
        i32t = R(140, I32, D)
        xqbf = [R(156, BF16, D), R(164, BF16, D)]
        h2st = [R(156, BF16, D), R(164, BF16, D)]
        xqT = R(28, BF16, 32 * 1024)
        at_sb = R(28, BF16, 32 * 1024)
        wring = [R(92 + i, BF16, 512) for i in range(8)]
        yt_t = [R(100, F32, 512), R(102, F32, 512)]
        g_t = [R(104, F32, 512), R(106, F32, 512)]
        xo = [R(108, F32, 512), R(110, F32, 512)]
        xr = [R(112 + 2 * i, F32, 512) for i in range(4)]
        wq_sb = R(28, BF16, 32 * ES)
        wk_sb = R(60, BF16, 32 * ES)
        wv_sb = R(92, BF16, 32 * ES)
        V_sb = R(124, BF16, 32 * ES)
        hring = [R(156 + i, BF16, 512) for i in range(8)]
        qtmp = R(164, F32, 512)
        sqq = R(166, F32, 512)
        qnst = [R(168, BF16, 512), R(169, BF16, 512)]
        atst = [R(170, BF16, 512), R(171, BF16, 512)]
        rbc_sb = R(172, F32, 512)
        qT = R(28, BF16, HPC * N)
        kT = R(60, BF16, HPC * N)
        Eb = [R(92, BF16, 16 * 512), R(108, BF16, 16 * 512)]
        at4 = [R(28 + 16 * e, BF16, B * N) for e in range(4)]
        wo_sb = R(124, BF16, HPC * D)
        stg6 = [R(92, BF16, D), R(100, BF16, D)]
        rsb_t = R(156, BF16, 2048)
        xbt_t = R(160, F32, 2048)
        x3st = R(168, F32, 2048)

        ps2a = es.enter_context(nc.psum_tensor("ps2a", [128, 1024], F32))
        ps2b = es.enter_context(nc.psum_tensor("ps2b", [128, 1024], F32))
        pav = es.enter_context(nc.psum_tensor("pav", [128, 512], F32))
        pl = es.enter_context(nc.psum_tensor("pl", [128, 512], F32))
        prb = es.enter_context(nc.psum_tensor("prb", [128, 512], F32))
        pex = es.enter_context(nc.psum_tensor("pex", [128, 512], F32))
        PS8 = [ps2a.ap()[:, 0:512], ps2a.ap()[:, 512:1024],
               ps2b.ap()[:, 0:512], ps2b.ap()[:, 512:1024],
               pav.ap()[:], pl.ap()[:], prb.ap()[:], pex.ap()[:]]

        S = {}
        for nm in ["prep", "x", "w", "t", "h2", "stx", "stq", "sth", "stw",
                   "cc", "pe", "vs", "sv", "vg", "sg", "sf", "out",
                   "vv", "sc", "wqkv"]:
            S[nm] = Sem(es.enter_context(nc.semaphore("s_" + nm)))
        WR = [Sem(es.enter_context(nc.semaphore(f"s_wr{i}"))) for i in range(8)]
        HR = [Sem(es.enter_context(nc.semaphore(f"s_hr{i}"))) for i in range(8)]

        prog = {e: [] for e in ["sync", "gpsimd", "vector", "scalar", "tensor"]}

        def em(eng, fn):
            prog[eng].append(fn)

        def wt(eng, sem, val):
            if val > 0:
                em(eng, lambda e, s=sem, v=val: e.wait_ge(s.h, v))

        def load(sem, dst, src, transpose=False, serial=True, queue="sync"):
            if serial and sem.n > 0:
                wt(queue, sem, sem.n)
            def fn(e, d=dst, r=src, t=transpose, s=sem):
                e.dma_start(out=d, in_=r, transpose=t).then_inc(s.h, 16)
            em(queue, fn)
            sem.n += 16
            return sem.n

        def store(sem, dst, src, serial=True, queue="gpsimd"):
            if serial and sem.n > 0:
                wt(queue, sem, sem.n)
            def fn(e, d=dst, r=src, s=sem):
                e.dma_start(out=d, in_=r).then_inc(s.h, 16)
            em(queue, fn)
            sem.n += 16
            return sem.n

        SELF = {"vector": "vv", "scalar": "sc"}

        def op(eng, name, *args, inc=None, bar=False, **kw):
            sem = inc if inc is not None else (S[SELF[eng]] if bar else None)
            def fn(e, name=name, args=args, kw=kw, sem=sem):
                ins = getattr(e, name)(*args, **kw)
                if sem is not None:
                    ins.then_inc(sem.h, 1)
            em(eng, fn)
            if sem is not None:
                sem.n += 1
                if bar:
                    wt(eng, sem, sem.n)
                return sem.n
            return None

        def mm(out, lhsT, rhs, start, stop, inc=None):
            return op("tensor", "matmul", out, lhsT, rhs,
                      start=start, stop=stop, inc=inc)

        def cc(kind, alu, i_, o_):
            def fn(e, k=kind, a=alu, i=i_, o=o_):
                e.collective_compute(k, a, replica_groups=RG, ins=[i],
                                     outs=[o]).then_inc(S["cc"].h, 1)
            em("gpsimd", fn)
            S["cc"].n += 1
            return S["cc"].n

        load(S["prep"], bq_bc[0:1, :], bq_in, serial=False)
        load(S["prep"], bk_bc[0:1, :], bk_in, serial=False)
        load(S["prep"], bv_bc[0:1, :], bv_in, serial=False)
        load(S["prep"], gq_bc[0:1, :], gq_in, serial=False)
        load(S["prep"], gk_bc[0:1, :], gk_in, serial=False)
        load(S["prep"], lnt[0:1, 0:D], bo_in, serial=False)
        n_all = load(S["prep"], scr[0:1, 0:D], bf_in, serial=False)
        n_bq = n_bk = n_bv = n_gq = n_gk = n_bo = n_bf = n_all

        em("vector", lambda e: e.memset(ones_bf[:, 0:1], 1.0))
        em("vector", lambda e: e.memset(onesr_f[0:1, :], 1.0))
        em("vector", lambda e: e.memset(onescol[:, 0:1], 1.0))

        wt("vector", S["prep"], n_gq)
        op("vector", "tensor_scalar", gq_bc[0:1, :], gq_bc[0:1, :], ALPHA,
           None, op0=ALU.mult, inc=S["vs"], bar=True)

        def bcast(row_ap, dst_ap, cols, dep_n):
            for j in range((cols + 511) // 512):
                c0, c1 = 512 * j, min(512 * (j + 1), cols)
                wt("tensor", S["prep"], dep_n)
                wt("tensor", S["vs"], S["vs"].n)
                pe_n = mm(prb.ap()[:, 0:c1 - c0], onesr_f[0:1, :],
                          row_ap[0:1, c0:c1], True, True, inc=S["pe"])
                wt("vector", S["pe"], pe_n)
                op("vector", "tensor_copy", dst_ap[:, c0:c1],
                   prb.ap()[:, 0:c1 - c0], inc=S["vs"])

        bcast(bq_bc, bq_bc, ES, n_bq)
        bcast(bk_bc, bk_bc, ES, n_bk)
        bcast(bv_bc, bv_bc, ES, n_bv)
        bcast(gq_bc, gq_bc, HD, n_gq)
        bcast(gk_bc, gk_bc, HD, n_gk)
        bcast(lnt, bo_bc, D, n_bo)
        bcast(scr, bf_bc, D, n_bf)

        p0_xt_g = [0, 0]
        for i in range(4):
            stg = xt[i % 2]
            wt("sync", S["vv"], p0_xt_g[i % 2])
            n_l = load(S["prep"], stg, wfs_in[128 * i:128 * (i + 1), :])
            wt("vector", S["prep"], n_l)
            p0_xt_g[i % 2] = op("vector", "tensor_reduce", st1[:, i:i + 1],
                                stg, AX.X, ALU.add,
                                apply_absolute_value=True, inc=S["vv"],
                                bar=True)
        v_n = op("vector", "tensor_reduce", st1[:, 4:5], st1[:, 0:4], AX.X,
                 ALU.add, inc=S["vs"], bar=True)
        wt("tensor", S["vs"], v_n)
        pe_n = mm(pex.ap()[0:1, 0:1], st1[:, 4:5], onescol[:, 0:1], True, True,
                  inc=S["pe"])
        wt("vector", S["pe"], pe_n)
        op("vector", "tensor_copy", wsrow[0:1, 0:1], pex.ap()[0:1, 0:1])
        vg_n = op("vector", "memset", wsrow[0:1, 1:8], 0.0, inc=S["vg"])
        wt("gpsimd", S["vg"], vg_n)
        st_n = store(S["stw"], ws_i, wsrow[0:1, 0:8])
        wt("gpsimd", S["stw"], st_n)
        cc_ws = cc("AllReduce", ALU.add, ws_i, ws_o)

        store(S["stq"], wqkv_d[:, 0:ES], wq_in)
        store(S["stq"], wqkv_d[:, ES:2 * ES], wk_in)
        st_wqkv = store(S["stq"], wqkv_d[:, 2 * ES:3 * ES], wv_in)
        st_wom = store(S["stw"], wom_d, wos_in)

        wt("sync", S["cc"], cc_ws)
        n_ws = load(S["x"], wsrow[0:1, 8:16], ws_o)
        wt("vector", S["x"], n_ws)
        op("vector", "tensor_scalar", wsrow[0:1, 0:1], wsrow[0:1, 8:9],
           MEAN_W, 1e-5, op0=ALU.mult, op1=ALU.max, bar=True)
        op("vector", "tensor_scalar", wsrow[0:1, 1:2], wsrow[0:1, 0:1],
           float(1.0 / 127.0), None, op0=ALU.mult, bar=True)
        v_n = op("vector", "reciprocal", wsrow[0:1, 2:3], wsrow[0:1, 0:1],
                 inc=S["vs"], bar=True)
        wt("tensor", S["vs"], v_n)
        mm(pex.ap()[:, 0:1], onesr_f[0:1, :], wsrow[0:1, 2:3], True, True)
        pe_n = mm(pex.ap()[:, 1:2], onesr_f[0:1, :], wsrow[0:1, 1:2], True,
                  True, inc=S["pe"])
        wt("vector", S["pe"], pe_n)
        op("vector", "tensor_copy", ws_bc[:, 0:1], pex.ap()[:, 0:1])
        WS_READY = op("vector", "tensor_copy", mwc_bc[:, 0:1], pex.ap()[:, 1:2],
                      inc=S["vs"])

        def ln_quant_phase(x_src, xa_dst, rx_sb, guards, chunks=None):
            cl = list(chunks) if chunks is not None else list(range(NCH))
            c0, ncl = cl[0], len(cl)
            resident = ncl <= 4
            if resident:
                xt_r = [R(28, F32, D), R(44, F32, D), R(60, F32, D),
                        R(76, F32, D)]
                scr_r = R(92, F32, D)
                xa_r = R(108, F32, D)
            C = lambda b: stw_t[:, 8 * b + c0:8 * b + c0 + ncl]
            Cc = lambda b, c: stw_t[:, 8 * b + c:8 * b + c + 1]
            p1v = [0] * 2
            for j, c in enumerate(cl):
                sl = c % 2
                xtile = xt_r[j] if resident else xt[sl]
                sq_t = scr_r if resident else scr
                if not resident:
                    wt("sync", S["vv"], p1v[sl])
                if c == cl[0]:
                    for gs, gv in guards:
                        wt("sync", gs, gv)
                n_x = load(S["x"], xtile, x_src[128 * c:128 * (c + 1), :])
                wt("vector", S["x"], n_x)
                op("vector", "tensor_reduce", Cc(0, c), xtile, AX.X, ALU.add)
                op("vector", "tensor_reduce", Cc(2, c), xtile, AX.X, ALU.max)
                op("vector", "tensor_reduce", Cc(3, c), xtile, AX.X, ALU.min)
                op("vector", "tensor_mul", sq_t, xtile, xtile, bar=True)
                p1v[sl] = op("vector", "tensor_reduce", Cc(1, c), sq_t, AX.X,
                             ALU.add, inc=S["vv"], bar=True)
            op("vector", "tensor_scalar", C(4), C(0), INV_D, None,
               op0=ALU.mult, bar=True)
            op("vector", "tensor_mul", C(5), C(4), C(4), bar=True)
            op("vector", "scalar_tensor_tensor", C(6), C(1), INV_D, C(5),
               op0=ALU.mult, op1=ALU.subtract, bar=True)
            v_n = op("vector", "tensor_scalar", C(5), C(6), 1e-5, None,
                     op0=ALU.add, inc=S["vs"], bar=True)
            wt("scalar", S["vs"], v_n)
            s_n = op("scalar", "activation", C(15), C(5), AF.Sqrt,
                     inc=S["sv"])
            wt("vector", S["sv"], s_n)
            op("vector", "reciprocal", C(7), C(15), bar=True)
            op("vector", "tensor_scalar", C(8), C(7), 0.5, 1.0,
               op0=ALU.mult, op1=ALU.add, bar=True)
            op("vector", "tensor_mul", C(5), C(4), C(7), bar=True)
            op("vector", "tensor_scalar", C(9), C(5), -0.5, None,
               op0=ALU.mult, bar=True)
            op("vector", "tensor_mul", C(15), C(6), C(7), bar=True)
            op("vector", "tensor_mul", C(15), C(15), C(7), bar=True)
            v_n = op("vector", "tensor_scalar", C(15), C(15), 1e-5, None,
                     op0=ALU.add, inc=S["vs"], bar=True)
            wt("scalar", S["vs"], v_n)
            s_n = op("scalar", "activation", C(15), C(15), AF.Sqrt,
                     inc=S["sv"])
            wt("vector", S["sv"], s_n)
            op("vector", "reciprocal", C(10), C(15), bar=True)
            op("vector", "tensor_sub", C(2), C(2), C(4), bar=True)
            op("vector", "tensor_sub", C(3), C(4), C(3), bar=True)
            op("vector", "tensor_max", C(2), C(2), C(3), bar=True)
            op("vector", "tensor_mul", C(2), C(2), C(7), bar=True)
            op("vector", "tensor_mul", C(11), C(2), C(10), bar=True)
            op("vector", "tensor_scalar", C(11), C(11), 1e-5, None,
               op0=ALU.max, bar=True)
            op("vector", "reciprocal", C(12), C(11), bar=True)
            op("vector", "tensor_scalar", C(12), C(12), 127.0, None,
               op0=ALU.mult, bar=True)
            op("vector", "tensor_mul", C(12), C(10), C(12), bar=True)
            op("vector", "tensor_mul", C(13), C(7), C(12), bar=True)
            op("vector", "tensor_mul", C(14), C(5), C(12), bar=True)
            op("vector", "tensor_scalar", C(14), C(14), -1.0, None,
               op0=ALU.mult, bar=True)
            v_st = op("vector", "tensor_scalar", rx_sb[:, c0:c0 + ncl], C(11),
                      mwc_bc[:, 0:1], None, op0=ALU.mult, inc=S["vs"],
                      bar=True)
            xa_g = [0] * 2
            xq_g = [0] * 2
            tq_g = [0, 0]
            p3_sv = [0, 0]
            tq2 = [tq, scr_r if resident else scr]
            for j, c in enumerate(cl):
                sl = c % 2
                if resident:
                    xtile = xt_r[j]
                    xa_t = xa_r
                else:
                    xtile = xt[sl]
                    xa_t = xa[sl]
                    if j < 2:
                        wt("sync", S["vv"], p1v[sl])
                    else:
                        wt("sync", S["sv"], p3_sv[sl])
                    n_x = load(S["x"], xtile, x_src[128 * c:128 * (c + 1), :])
                    wt("scalar", S["x"], n_x)
                if j == 0:
                    wt("scalar", S["vs"], v_st)
                wt("scalar", S["stx"], xa_g[0] if resident else xa_g[sl])
                s_xa = op("scalar", "activation", xa_t, xtile, AF.Identity,
                          bias=Cc(9, c), scale=Cc(8, c), inc=S["sg"])
                wt("gpsimd", S["sg"], s_xa)
                xa_g[0 if resident else sl] = store(
                    S["stx"], xa_dst[128 * c:128 * (c + 1), :], xa_t)
                wt("scalar", S["vv"], tq_g[sl])
                s_t = op("scalar", "activation", tq2[sl], xtile, AF.Identity,
                         bias=Cc(14, c), scale=Cc(13, c), inc=S["sv"])
                p3_sv[sl] = s_t
                wt("vector", S["sv"], s_t)
                tq_g[sl] = op("vector", "tensor_copy", i32t, tq2[sl],
                              inc=S["vv"], bar=True)
                wt("vector", S["stq"], xq_g[sl])
                vg_n = op("vector", "tensor_copy", xqbf[sl], i32t, inc=S["vg"])
                wt("gpsimd", S["vg"], vg_n)
                xq_g[sl] = store(S["stq"], xq_d[128 * c:128 * (c + 1), :],
                                 xqbf[sl])
            return None

        P0_PE = S["pe"].n
        wint_st = 0
        p2_xt_g = [p0_xt_g[0], p0_xt_g[1]]
        p2_tq_g = 0
        for i in range(4):
            sl = i % 2
            if i < 2:
                wt("sync", S["vv"], p0_xt_g[sl])
            else:
                wt("sync", S["sv"], p2_xt_g[sl])
            n_w = load(S["x"], xt[sl], wfs_in[128 * i:128 * (i + 1), :])
            wt("scalar", S["x"], n_w)
            if i == 0:
                wt("scalar", S["vs"], WS_READY)
            wt("scalar", S["vv"], p2_tq_g)
            s_n = op("scalar", "activation", tq, xt[sl], AF.Copy,
                     scale=ws_bc[:, 0:1], inc=S["sv"])
            p2_xt_g[sl] = s_n
            wt("vector", S["sv"], s_n)
            p2_tq_g = op("vector", "tensor_copy", i32t, tq, inc=S["vv"],
                         bar=True)
            wt("vector", S["stw"], wint_st)
            vg_n = op("vector", "tensor_scalar", xqbf[sl], i32t, 1.0, -1.0,
                      op0=ALU.min, op1=ALU.max, inc=S["vg"])
            wt("gpsimd", S["vg"], vg_n)
            wint_st = store(S["stw"], wint_i[128 * i:128 * (i + 1), :],
                            xqbf[sl])
        P2A_SV = S["sv"].n
        P2A_VV = S["vv"].n
        P2A_VG = S["vg"].n
        wtT = [R(60, BF16, ES), R(68, BF16, ES)]
        wt_stl = []
        for k in range(32):
            sl = k % 2
            wt("sync", S["stw"], wint_st if k < 2 else wt_stl[k - 2])
            n_l = load(WR[sl], wtT[sl], wint_i[0:ES, 128 * k:128 * (k + 1)],
                       transpose=True, serial=False)
            wt("gpsimd", WR[sl], n_l)
            wt_stl.append(store(S["stw"], wintT_i[128 * k:128 * (k + 1), :],
                                wtT[sl]))
        wt("gpsimd", S["stw"], wt_stl[-1])
        cc_wint = cc("AllGather", ALU.bypass, wintT_i, wintT_o)
        wintT_flat = wintT_o.rearrange("r p d -> (r p) d")
        P2_STW = S["stw"].n

        g_p1 = [(S["vs"], WS_READY), (S["sv"], P2A_SV), (S["vv"], P2A_VV),
                (S["vg"], P2A_VG), (S["stw"], wint_st), (S["pe"], P0_PE)]
        ln_quant_phase(x_in, xA, rx1, g_p1)
        P1_STX = S["stx"].n
        P1_STQ = S["stq"].n
        P1_SV = S["sv"].n
        P1_VG = S["vg"].n

        wring_g = [0] * 8
        widx = [0]
        xr_g = [0] * 4
        xo_g = {"stx": [0, 0], "out": [0, 0]}

        def ffn_like_phase(dst_sem_name, lhsT_sb, rhs_src_fn, rhs_transpose,
                           res_src, dst, epi_fn, guards_sync, guards_tensor,
                           per_k_guards=None, lhs_k_guards=None,
                           ts_list=(0, 1), og_list=None, guards_issue=()):
            st_sem = S[dst_sem_name]
            bank_g = [S["vs"].n] * 2
            for gs, gv in guards_sync:
                wt("sync", gs, gv)
            for gs, gv in guards_tensor:
                wt("tensor", gs, gv)
            ts0 = list(ts_list)[0]
            ogl = list(og_list) if og_list is not None else list(range(8))
            og0 = ogl[0]
            steps = [(ts, og, k) for ts in ts_list for og in ogl
                     for k in range(32)]
            w0 = widx[0]
            PE0 = S["pe"].n
            n_ws = [0] * len(steps)

            def issue_step(j):
                ts_, og_, k_ = steps[j]
                if j >= 8:
                    wt("scalar", S["pe"], PE0 + j - 7)
                n_ws[j] = load(WR[(w0 + j) % 8], wring[(w0 + j) % 8],
                               rhs_src_fn(og_, k_), serial=False,
                               queue="scalar")

            wt("scalar", S["pe"], PE0)
            for gs, gv in guards_tensor:
                wt("scalar", gs, gv)
            for gs, gv in guards_issue:
                wt("scalar", gs, gv)
            for j in range(min(8, len(steps))):
                issue_step(j)
            for j, (ts, og, k) in enumerate(steps):
                sl = (w0 + j) % 8
                bg = og % 2
                if k == 0:
                    wt("tensor", S["vs"], bank_g[bg])
                if lhs_k_guards is not None and ts == ts0 and og == og0:
                    wt("tensor", *lhs_k_guards[k])
                wt("tensor", WR[sl], n_ws[j])
                for tt in range(4):
                    t0 = 1024 * k + 512 * ts + 128 * tt
                    pe_group = mm(PS8[4 * bg + tt],
                                  lhsT_sb[:, t0:t0 + 128],
                                  wring[sl][:], k == 0, k == 31,
                                  inc=S["pe"] if tt == 3 else None)
                widx[0] += 1
                if j + 8 < len(steps):
                    issue_step(j + 8)
                if k == 31:
                    last_vs = 0
                    for tt in range(4):
                        tg = 4 * ts + tt
                        rows = slice(128 * tg, 128 * (tg + 1))
                        cols = slice(512 * og, 512 * (og + 1))
                        xsl = (2 * og + tt) % 4
                        wt("sync", S["vg"], xr_g[xsl])
                        n_xr = load(S["x"], xr[xsl], res_src[rows, cols])
                        last_vs = epi_fn(bg, tt, tg, rows, cols, xsl, n_xr,
                                         pe_group, st_sem, dst)
                    bank_g[bg] = last_vs

        def make_ffn_epi(rx_sb, st_sem_name):
            g_guard = [0, 0]

            def epi(bg, tt, tg, rows, cols, xsl, n_xr, pe_group, st_sem, dst):
                ysl = tt % 2
                wt("vector", S["pe"], pe_group)
                wt("vector", S["sv"], g_guard[ysl])
                s_n = op("vector", "scalar_tensor_tensor", yt_t[ysl],
                         PS8[4 * bg + tt], rx_sb[:, tg:tg + 1], bf_bc[:, cols],
                         op0=ALU.mult, op1=ALU.add, inc=S["vs"])
                wt("scalar", S["vs"], s_n)
                s_g = op("scalar", "activation", g_t[ysl], yt_t[ysl], AF.Gelu,
                         inc=S["sv"])
                g_guard[ysl] = s_g
                osl = tt % 2
                wt("vector", S["sv"], s_g)
                wt("vector", S["x"], n_xr)
                wt("vector", st_sem, xo_g[st_sem_name][osl])
                vg_n = op("vector", "scalar_tensor_tensor", xo[osl], g_t[ysl],
                          0.5, xr[xsl], op0=ALU.mult, op1=ALU.add, inc=S["vg"])
                xr_g[xsl] = vg_n
                wt("gpsimd", S["vg"], vg_n)
                xo_g[st_sem_name][osl] = store(st_sem, dst[rows, cols],
                                               xo[osl])
                return s_n
            return epi

        def ffn_phase(rx_sb, res_src, dst, st_sem_name, guards):
            for gs, gv in guards:
                wt("sync", gs, gv)
            xqt_k = []
            for k in range(32):
                xqt_k.append((HR[k % 4],
                              load(HR[k % 4], xqT[:, 1024 * k:1024 * (k + 1)],
                                   xq_d[0:TPC, 128 * k:128 * (k + 1)],
                                   transpose=True, serial=True)))
            ffn_like_phase(
                st_sem_name, xqT,
                lambda og, k: wintT_flat[4096 * og + 128 * k:
                                         4096 * og + 128 * (k + 1), :],
                False, res_src, dst, make_ffn_epi(rx_sb, st_sem_name),
                [], [(S["cc"], cc_wint)], lhs_k_guards=xqt_k,
                guards_issue=guards)

        if stage >= 3:
            ffn_phase(rx1, xA, xB, "stx",
                      [(S["stq"], P1_STQ), (S["stx"], P1_STX)])
        P3_PE = S["pe"].n
        P3_STX = S["stx"].n
        P3_SV = S["sv"].n
        P3_VG = S["vg"].n

        cc_h2 = [0, 0]
        P4_STH = 0
        if stage >= 5:
            h2_g = [0] * 2
            CBc = lambda bk, q: stw_t[:, 8 * bk + q:8 * bk + q + 1]
            xt_r4 = [R(28, F32, D), R(44, F32, D), R(60, F32, D),
                     R(76, F32, D)]
            scr4 = R(92, F32, D)
            p4_sg_last = 0
            for half in range(2):
                cls_ = range(4 * half, 4 * half + 4)
                cb0 = 4 * half
                CB = lambda bk: stw_t[:, 8 * bk + cb0:8 * bk + cb0 + 4]
                for j, c in enumerate(cls_):
                    if j == 0:
                        if half == 0:
                            wt("sync", S["pe"], P3_PE)
                            wt("sync", S["stx"], P3_STX)
                            wt("sync", S["sv"], S["sv"].n)
                            wt("sync", S["vg"], S["vg"].n)
                        else:
                            wt("sync", S["sg"], p4_sg_last)
                    n_x = load(S["x"], xt_r4[j], xB[128 * c:128 * (c + 1), :])
                    wt("vector", S["x"], n_x)
                    op("vector", "tensor_reduce", CBc(0, c), xt_r4[j], AX.X,
                       ALU.add)
                    op("vector", "tensor_mul", scr4, xt_r4[j], xt_r4[j],
                       bar=True)
                    op("vector", "tensor_reduce", CBc(1, c), scr4, AX.X,
                       ALU.add, bar=True)
                op("vector", "tensor_scalar", CB(4), CB(0), INV_D, None,
                   op0=ALU.mult, bar=True)
                op("vector", "tensor_mul", CB(5), CB(4), CB(4), bar=True)
                op("vector", "scalar_tensor_tensor", CB(6), CB(1), INV_D,
                   CB(5), op0=ALU.mult, op1=ALU.subtract, bar=True)
                v_n = op("vector", "tensor_scalar", CB(5), CB(6), 1e-5, None,
                         op0=ALU.add, inc=S["vs"], bar=True)
                wt("scalar", S["vs"], v_n)
                s_n = op("scalar", "activation", CB(3), CB(5), AF.Sqrt,
                         inc=S["sv"])
                wt("vector", S["sv"], s_n)
                op("vector", "reciprocal", CB(7), CB(3), bar=True)
                v_st4 = op("vector", "scalar_tensor_tensor", CB(9), CB(4),
                           -1.0, CB(7), op0=ALU.mult, op1=ALU.mult,
                           inc=S["vs"], bar=True)
                for j, c in enumerate(cls_):
                    sl = c % 2
                    if j == 0:
                        wt("scalar", S["vs"], v_st4)
                    wt("scalar", S["sth"], h2_g[sl])
                    sg_n = op("scalar", "activation", h2st[sl], xt_r4[j],
                              AF.Identity, bias=CBc(9, c), scale=CBc(7, c),
                              inc=S["sg"])
                    p4_sg_last = sg_n
                    wt("gpsimd", S["sg"], sg_n)
                    h2_g[sl] = store(
                        S["sth"],
                        h2_i[half][128 * (c % 4):128 * (c % 4) + 128, :],
                        h2st[sl])
                wt("gpsimd", S["sth"], S["sth"].n)
                cc_h2[half] = cc("AllGather", ALU.bypass, h2_i[half],
                                 h2_o[half])
            P4_STH = S["sth"].n

        hring_g = [0] * 8
        qn_g = [0] * 2
        at_g = [0] * 2
        cc_at = [0, 0]
        if stage >= 5:
            Ehalf_g = [0, 0]
            pair_g = [0, 0]
            pl_g = 0
            pavrb_g = 0
            rank1_g = 0
            for b in range(1 if dump == 24 else B):
                wt("sync", S["pe"], S["pe"].n)
                wt("sync", S["sf"], S["sf"].n)
                wt("sync", S["sv"], S["sv"].n)
                wt("sync", S["vg"], S["vg"].n)
                wt("sync", S["stq"], st_wqkv if b == 0 else S["stq"].n)
                if b == 0 and stage >= 6:
                    wt("sync", S["sth"], P4_STH)
                n_wqkv = 0
                for (w_sb, off) in ((wq_sb, 0), (wk_sb, ES), (wv_sb, 2 * ES)):
                    for k in range(32):
                        n_wqkv = load(S["wqkv"], w_sb[:, ES * k:ES * (k + 1)],
                                      wqkv_d[128 * k:128 * (k + 1),
                                             off:off + ES], serial=False)
                wt("tensor", S["wqkv"], n_wqkv)
                wt("sync", S["cc"], cc_h2[b])
                if b == 0:
                    wt("tensor", S["cc"], cc_h2[1])

                qkv_bank_g = S["vs"].n
                v_bank_g = S["vs"].n
                qkv_sq_g = S["sv"].n
                for tc in range(8):
                    r = tc
                    Xh = b
                    wt("tensor", S["vs"], qkv_bank_g)
                    pe_qk = 0
                    for k in range(32):
                        sl = k % 8
                        wt("sync", S["pe"], hring_g[sl])
                        n_h = load(HR[sl], hring[sl],
                                   h2_o[Xh][r][0:512, 128 * k:128 * (k + 1)],
                                   transpose=True, serial=False)
                        wt("tensor", HR[sl], n_h)
                        for tt in range(4):
                            mm(PS8[tt], hring[sl][:, 128 * tt:128 * (tt + 1)],
                               wq_sb[:, ES * k:ES * (k + 1)], k == 0, k == 31)
                        for tt in range(4):
                            pe_qk = mm(PS8[4 + tt],
                                       hring[sl][:, 128 * tt:128 * (tt + 1)],
                                       wk_sb[:, ES * k:ES * (k + 1)],
                                       k == 0, k == 31,
                                       inc=S["pe"] if tt == 3 else None)
                        hring_g[sl] = pe_qk
                    vs_qk = 0
                    vs_q4 = 0
                    sq_guard = qkv_sq_g
                    for qk in range(2):
                        bias_bc = bq_bc if qk == 0 else bk_bc
                        g_bc = gq_bc if qk == 0 else gk_bc
                        dstd = qn_d if qk == 0 else kn_d
                        for tt in range(4):
                            bank = PS8[4 * qk + tt]
                            wt("vector", S["pe"], pe_qk)
                            wt("vector", S["sv"], sq_guard)
                            s_n = op("vector", "scalar_tensor_tensor", qtmp,
                                     bank, 1.0, bias_bc, op0=ALU.mult,
                                     op1=ALU.add, inc=S["vs"], bar=True)
                            vs_qk = s_n
                            wt("scalar", S["vs"], s_n)
                            for hh in range(4):
                                cseg = slice(128 * hh, 128 * (hh + 1))
                                sq_guard = op(
                                    "scalar", "activation", sqq[:, cseg],
                                    qtmp[:, cseg], AF.Square,
                                    accum_out=st3[:, hh:hh + 1],
                                    inc=S["sv"] if hh == 3 else None)
                            wt("vector", S["sv"], sq_guard)
                            v_n = op("vector", "tensor_scalar", st3[:, 0:4],
                                     st3[:, 0:4], float(1.0 / HD), 1e-6,
                                     op0=ALU.mult, op1=ALU.add, inc=S["vs"],
                                     bar=True)
                            wt("scalar", S["vs"], v_n)
                            s_n = op("scalar", "activation", st3[:, 4:8],
                                     st3[:, 0:4], AF.Sqrt, inc=S["sv"])
                            wt("vector", S["sv"], s_n)
                            op("vector", "reciprocal", st3[:, 8:12],
                               st3[:, 4:8], bar=True)
                            qsl = (2 * qk + tt) % 2
                            wt("vector", S["stq"], qn_g[qsl])
                            vg_n = 0
                            for hh in range(4):
                                cseg = slice(128 * hh, 128 * (hh + 1))
                                vg_n = op("vector", "scalar_tensor_tensor",
                                          qnst[qsl][:, cseg], qtmp[:, cseg],
                                          st3[:, 8 + hh:9 + hh], g_bc,
                                          op0=ALU.mult, op1=ALU.mult,
                                          inc=S["vg"] if hh == 3 else None)
                            wt("gpsimd", S["vg"], vg_n)
                            trow = 512 * tc + 128 * tt
                            qn_g[qsl] = store(S["stq"],
                                              dstd[trow:trow + 128, :],
                                              qnst[qsl])
                        if qk == 0:
                            vs_q4 = vs_qk
                    qkv_sq_g = sq_guard
                    wt("tensor", S["vs"], vs_q4)
                    pe_v = 0
                    for k in range(32):
                        sl = k % 8
                        wt("sync", S["pe"], hring_g[sl])
                        n_h = load(HR[sl], hring[sl],
                                   h2_o[Xh][r][0:512, 128 * k:128 * (k + 1)],
                                   transpose=True, serial=False)
                        wt("tensor", HR[sl], n_h)
                        for tt in range(4):
                            pe_v = mm(PS8[tt],
                                      hring[sl][:, 128 * tt:128 * (tt + 1)],
                                      wv_sb[:, ES * k:ES * (k + 1)],
                                      k == 0, k == 31,
                                      inc=S["pe"] if tt == 3 else None)
                        hring_g[sl] = pe_v
                    for tt in range(4):
                        wt("vector", S["pe"], pe_v)
                        cseg = slice(512 * (4 * tc + tt),
                                     512 * (4 * tc + tt + 1))
                        qkv_bank_g = op("vector", "scalar_tensor_tensor",
                                        V_sb[:, cseg], PS8[tt], 1.0, bv_bc,
                                        op0=ALU.mult, op1=ALU.add, inc=S["vs"])
                QKV_STQ = S["stq"].n
                QKV_PE = S["pe"].n
                QKV_VS = S["vs"].n

                wt("sync", S["stq"], QKV_STQ)
                wt("sync", S["pe"], QKV_PE)
                n_qt = 0
                for (dsrc, dsb) in ((qn_d, qT), (kn_d, kT)):
                    for h in range(HPC):
                        n_qt = load(S["t"], dsb[:, N * h:N * (h + 1)],
                                    dsrc[0:N, 128 * h:128 * (h + 1)],
                                    transpose=True, serial=False)
                wt("tensor", S["t"], n_qt)
                wt("tensor", S["vs"], QKV_VS)

                for h in range(HPC):
                    for qc in range(8):
                        qcols = slice(N * h + 512 * qc,
                                      N * h + 512 * (qc + 1))
                        av_n = 0
                        for half in range(2):
                            exp_n = 0
                            for g_ in range(8):
                                pr = g_ % 2
                                ps_pair = ps2a if pr == 0 else ps2b
                                wt("tensor", S["sf"], pair_g[pr])
                                pe_n = 0
                                for j in range(2):
                                    kt = 16 * half + 2 * g_ + j
                                    kcol = slice(N * h + 128 * kt,
                                                 N * h + 128 * (kt + 1))
                                    pe_n = mm(
                                        ps_pair.ap()[:, 512 * j:512 * (j + 1)],
                                        kT[:, kcol], qT[:, qcols], True, True,
                                        inc=S["pe"] if j == 1 else None)
                                wt("scalar", S["pe"], pe_n)
                                wt("scalar", S["pe"], Ehalf_g[half])
                                ecol = slice(1024 * g_, 1024 * (g_ + 1))
                                exp_n = op("scalar", "activation",
                                           Eb[half][:, ecol], ps_pair.ap()[:],
                                           AF.Exp, inc=S["sf"])
                                pair_g[pr] = exp_n
                            wt("tensor", S["sf"], exp_n)
                            if half == 0:
                                wt("tensor", S["vg"], pavrb_g)
                                wt("tensor", S["vs"], pl_g)
                            for kt16 in range(16):
                                kt = 16 * half + kt16
                                erhs = Eb[half][:, 512 * kt16:512 * (kt16 + 1)]
                                vcol = slice(512 * kt + 128 * h,
                                             512 * kt + 128 * (h + 1))
                                mm(pav.ap()[:], V_sb[:, vcol], erhs,
                                   kt == 0, kt == 31)
                            for kt16 in range(16):
                                kt = 16 * half + kt16
                                erhs = Eb[half][:, 512 * kt16:512 * (kt16 + 1)]
                                av_n = mm(pl.ap()[0:1, :], ones_bf[:, 0:1],
                                          erhs, kt == 0, kt == 31,
                                          inc=S["pe"] if kt16 == 15 else None)
                            Ehalf_g[half] = av_n
                        wt("vector", S["pe"], av_n)
                        wt("vector", S["pe"], rank1_g)
                        pl_g = op("vector", "reciprocal", lrow[0:1, :],
                                  pl.ap()[0:1, :], inc=S["vs"])
                        wt("tensor", S["vs"], pl_g)
                        rank1_g = mm(prb.ap()[:], onesr_f[0:1, :],
                                     lrow[0:1, :], True, True, inc=S["pe"])
                        asl = qc % 2
                        wt("vector", S["pe"], rank1_g)
                        op("vector", "tensor_copy", rbc_sb, prb.ap()[:],
                           bar=True)
                        wt("vector", S["sth"], at_g[asl])
                        vg_n = op("vector", "tensor_mul", atst[asl],
                                  pav.ap()[:], rbc_sb, inc=S["vg"])
                        pavrb_g = vg_n
                        wt("gpsimd", S["vg"], vg_n)
                        gcol = 4096 * b + 512 * qc
                        at_g[asl] = store(
                            S["sth"],
                            at_d[128 * h:128 * (h + 1), gcol:gcol + 512],
                            atst[asl])

        cc_rs = [0, 0]
        if stage >= 6:
            wt("sync", S["sth"], S["sth"].n)
            wt("sync", S["pe"], S["pe"].n)
            wt("sync", S["sf"], S["sf"].n)
            wt("sync", S["vg"], S["vg"].n)
            wt("sync", S["stw"], st_wom)
            n_at = 0
            for e in range(4):
                n_at = load(S["t"], at4[e], at_d[128 * e:128 * (e + 1), :],
                            serial=False)
            n_wo = 0
            for e in range(4):
                n_wo = load(S["wqkv"], wo_sb[:, D * e:D * (e + 1)],
                            wom_d[128 * e:128 * (e + 1), :], serial=False)
            wt("tensor", S["t"], n_at)
            wt("tensor", S["wqkv"], n_wo)
            wt("tensor", S["vs"], S["vs"].n)
            rnd_g = [S["vs"].n] * 2
            stg_g = [0, 0]
            rs_st = 0
            for T in range(64):
                ssl = T % 2
                vs_cp = 0
                for rnd in range(2):
                    gr = (2 * T + rnd) % 2
                    wt("tensor", S["vs"], rnd_g[gr])
                    pe_n = 0
                    for e in range(4):
                        for oc in range(4):
                            o = 4 * rnd + oc
                            pe_n = mm(PS8[4 * gr + oc],
                                      at4[e][:, 128 * T:128 * (T + 1)],
                                      wo_sb[:, D * e + 512 * o:D * e + 512 * (o + 1)],
                                      e == 0, e == 3,
                                      inc=S["pe"] if oc == 3 else None)
                    for oc in range(4):
                        o = 4 * rnd + oc
                        wt("vector", S["pe"], pe_n)
                        if rnd == 0 and oc == 0:
                            wt("vector", S["stx"], stg_g[ssl])
                        vs_cp = op("vector", "tensor_copy",
                                   stg6[ssl][:, 512 * o:512 * (o + 1)],
                                   PS8[4 * gr + oc], inc=S["vs"])
                    rnd_g[gr] = vs_cp
                wt("gpsimd", S["vs"], vs_cp)
                stg_g[ssl] = store(S["stx"], rs_i[128 * T:128 * (T + 1), :],
                                   stg6[ssl])
                if T == 31:
                    wt("gpsimd", S["stx"], S["stx"].n)
                    cc_rs[0] = cc("ReduceScatter", ALU.add,
                                  rs_i[0:4096, :], rs_o[0:512, :])
            P6A_STX_RS = S["stx"].n
            P6A_PE = S["pe"].n

            wint8_rhs = lambda og, k: wintT_flat[4096 * og + 128 * k:
                                                 4096 * og + 128 * (k + 1), :]
            wt("sync", S["cc"], cc_rs[0])
            x3_st_g = 0
            for j in range(8):
                c2, hf = divmod(j, 2)
                rows = slice(128 * c2, 128 * (c2 + 1))
                cols = slice(2048 * hf, 2048 * (hf + 1))
                wt("sync", S["vg"], S["vg"].n)
                n_r = load(S["x"], rsb_t, rs_o[rows, cols])
                n_x2 = load(S["x"], xbt_t, xB[rows, cols])
                wt("vector", S["x"], n_x2)
                wt("vector", S["stx"], x3_st_g)
                op("vector", "scalar_tensor_tensor", x3st, rsb_t, 1.0,
                   xbt_t, op0=ALU.mult, op1=ALU.add, bar=True)
                vg_n = op("vector", "tensor_add", x3st, x3st,
                          bo_bc[:, cols], inc=S["vg"])
                wt("gpsimd", S["vg"], vg_n)
                x3_st_g = store(S["stx"], xA[rows, cols], x3st)
            wt("gpsimd", S["stx"], P6A_STX_RS)
            wt("gpsimd", S["x"], S["x"].n)
            cc_rs[1] = cc("ReduceScatter", ALU.add,
                          rs_i[4096:8192, :], rs_o[512:1024, :])
            P6B0_STX = S["stx"].n
            if stage >= 8:
                g_p7a = [(S["stx"], P6B0_STX), (S["pe"], P6A_PE),
                         (S["sv"], S["sv"].n), (S["vg"], S["vg"].n),
                         (S["stq"], S["stq"].n), (S["stw"], S["stw"].n),
                         (S["sth"], S["sth"].n), (S["sf"], S["sf"].n)]
                ln_quant_phase(xA, xB, rx2, g_p7a, chunks=range(0, 4))
                epi8 = make_ffn_epi(rx2, "out")
                g_p8a = [(S["stq"], S["stq"].n), (S["stx"], S["stx"].n)]
                for gs, gv in g_p8a:
                    wt("sync", gs, gv)
                xqt_a = []
                for k in range(32):
                    xqt_a.append((HR[k % 4],
                                  load(HR[k % 4],
                                       xqT[:, 1024 * k:1024 * k + 512],
                                       xq_d[0:512, 128 * k:128 * (k + 1)],
                                       transpose=True, serial=True)))
                ffn_like_phase("out", xqT, wint8_rhs, False, xB, out_t,
                               epi8, [], [], lhs_k_guards=xqt_a,
                               ts_list=(0,), guards_issue=g_p8a)
            wt("sync", S["cc"], cc_rs[1])
            wt("sync", S["stq"], S["stq"].n)
            x3_st_g1 = 0
            for c2 in range(4):
                rows = slice(512 + 128 * c2, 512 + 128 * (c2 + 1))
                for hf in range(2):
                    cols = slice(2048 * hf, 2048 * (hf + 1))
                    wt("sync", S["vg"], S["vg"].n)
                    n_r = load(S["x"], rsb_t, rs_o[rows, cols])
                    n_x2 = load(S["x"], xbt_t, xB[rows, cols])
                    wt("vector", S["x"], n_x2)
                    wt("vector", S["stx"], x3_st_g1)
                    op("vector", "scalar_tensor_tensor", x3st, rsb_t, 1.0,
                       xbt_t, op0=ALU.mult, op1=ALU.add, bar=True)
                    vg_n = op("vector", "tensor_add", x3st, x3st,
                              bo_bc[:, cols], inc=S["vg"])
                    wt("gpsimd", S["vg"], vg_n)
                    x3_st_g1 = store(S["stx"], xA[rows, cols], x3st)
        P6_STX = S["stx"].n
        P6_PE = S["pe"].n

        if stage >= 8:
            g_p7b = [(S["stx"], P6_STX), (S["pe"], P6_PE),
                     (S["sv"], S["sv"].n), (S["vg"], S["vg"].n),
                     (S["stq"], S["stq"].n), (S["stw"], S["stw"].n),
                     (S["sth"], S["sth"].n), (S["sf"], S["sf"].n)]
            ln_quant_phase(xA, xB, rx2, g_p7b, chunks=range(4, 8))
            g_p8b = [(S["stq"], S["stq"].n), (S["stx"], S["stx"].n)]
            for gs, gv in g_p8b:
                wt("sync", gs, gv)
            xqt_b = []
            for k in range(32):
                xqt_b.append((HR[k % 4],
                              load(HR[k % 4],
                                   xqT[:, 1024 * k + 512:1024 * (k + 1)],
                                   xq_d[512:1024, 128 * k:128 * (k + 1)],
                                   transpose=True, serial=True)))
            ffn_like_phase("out", xqT, wint8_rhs, False, xB, out_t, epi8,
                           [], [], lhs_k_guards=xqt_b, ts_list=(1,),
                           guards_issue=g_p8b)

        if dump:
            for nm in ["stx", "stq", "sth", "stw", "cc", "vg", "sg"]:
                wt("gpsimd", S[nm], S[nm].n)
            if dump == 21:
                store(S["out"], out_t[:, 0:D // 2].bitcast(BF16), xq_d)
            elif dump == 22:
                store(S["out"], out_t[0:512, 0:D // 2].bitcast(BF16), wint_i)
            elif dump == 23:
                store(S["out"], out_t[0:512, 0:D // 2].bitcast(BF16)
                      .rearrange("p (a b) -> p a b", a=2)
                      .rearrange("p a b -> (p a) b") if False else
                      out_t.bitcast(BF16)[0:512, 0:B * N], at_d)
            elif dump == 24:
                store(S["out"], out_t.bitcast(BF16)[0:1024, 0:ES],
                      qn_d[1280:2304])
        elif stage < 8:
            src = {1: xA, 3: xB, 6: xA}[stage]
            for nm in ["stx", "stq", "sth", "stw", "cc", "vg", "sg"]:
                wt("gpsimd", S[nm], S[nm].n)
            store(S["out"], out_t, src)

        wt("sync", S["out"], S["out"].n)
        wt("gpsimd", S["out"], S["out"].n)

        block = es.enter_context(nc.Block())
        @block.sync
        def _(e):
            for fn in prog["sync"]:
                fn(e)

        @block.gpsimd
        def _(e):
            for fn in prog["gpsimd"]:
                fn(e)

        @block.vector
        def _(e):
            for fn in prog["vector"]:
                fn(e)

        @block.scalar
        def _(e):
            for fn in prog["scalar"]:
                fn(e)

        @block.tensor
        def _(e):
            for fn in prog["tensor"]:
                fn(e)

    return nc


def kernel(L0_attn_hidden_states, L0_attn_Wq, L0_attn_bq, L0_attn_Wk,
           L0_attn_bk, L0_attn_Wv, L0_attn_bv, L0_attn_Wo, L0_attn_bo,
           L0_attn_gq, L0_attn_gk, L0_ffn_W, L0_ffn_b):
    x = np.ascontiguousarray(
        np.asarray(L0_attn_hidden_states, dtype=np.float32).reshape(B * N, D))
    Wq = np.asarray(L0_attn_Wq, np.float32)
    Wk = np.asarray(L0_attn_Wk, np.float32)
    Wv = np.asarray(L0_attn_Wv, np.float32)
    Wo = np.asarray(L0_attn_Wo, np.float32)
    Wf = np.asarray(L0_ffn_W, np.float32)
    bq = np.asarray(L0_attn_bq, np.float32).reshape(1, D)
    bk = np.asarray(L0_attn_bk, np.float32).reshape(1, D)
    bv = np.asarray(L0_attn_bv, np.float32).reshape(1, D)
    bo = np.asarray(L0_attn_bo, np.float32).reshape(1, D)
    bf_ = np.asarray(L0_ffn_b, np.float32).reshape(1, D)
    gq = np.asarray(L0_attn_gq, np.float32).reshape(1, HD)
    gk = np.asarray(L0_attn_gk, np.float32).reshape(1, HD)

    in_maps = []
    for c in range(NC):
        esl = slice(ES * c, ES * (c + 1))
        myrows = np.r_[512 * c:512 * (c + 1), N + 512 * c:N + 512 * (c + 1)]
        in_maps.append({
            "x_my": np.ascontiguousarray(x[myrows]),
            "wq_my": np.ascontiguousarray(Wq[:, esl]),
            "wk_my": np.ascontiguousarray(Wk[:, esl]),
            "wv_my": np.ascontiguousarray(Wv[:, esl]),
            "bq_my": np.ascontiguousarray(bq[:, esl]),
            "bk_my": np.ascontiguousarray(bk[:, esl]),
            "bv_my": np.ascontiguousarray(bv[:, esl]),
            "gq": gq, "gk": gk,
            "wfs_my": np.ascontiguousarray(Wf[esl, :]),
            "wos_my": np.ascontiguousarray(Wo[esl, :]),
            "bf": bf_, "bo": bo,
        })
    nc = build_nc(_STAGE)
    res = run_bass_kernel_spmd(nc, in_maps, core_ids=list(range(NC)))
    out = np.empty((B * N, D), np.float32)
    for c in range(NC):
        oc = res.results[c]["out"]
        out[512 * c:512 * (c + 1)] = oc[0:512]
        out[N + 512 * c:N + 512 * (c + 1)] = oc[512:1024]
    return out.reshape(B, N, D)


# version 17 (final)
# speedup vs baseline: 1.0801x; 1.0012x over previous
import numpy as np
import concourse.bass as bass
import concourse.mybir as mybir
from concourse.bass_utils import run_bass_kernel_spmd

F32 = mybir.dt.float32
BF16 = mybir.dt.bfloat16
I32 = mybir.dt.int32
AF = mybir.ActivationFunctionType
ALU = mybir.AluOpType
AX = mybir.AxisListType

D = 4096
B = 2
N = 4096
TPC = 1024
NC = 8
HD = 128
HPC = 4
ES = HPC * HD
NCH = TPC // 128
RG = [[0, 1, 2, 3, 4, 5, 6, 7]]
ALPHA = float(1.0 / np.sqrt(HD))
INV_D = float(1.0 / D)
MEAN_W = float(2.0 ** -24)

_STAGE = 8


class Sem:
    def __init__(self, h):
        self.h = h
        self.n = 0


def build_nc(stage):
    dump = stage if stage in (21, 22, 23, 24) else None
    if dump in (21, 22):
        stage = 2
    elif dump in (23, 24):
        stage = 5
    nc = bass.Bass("TRN2", target_bir_lowering=False, debug=False)

    x_in = nc.dram_tensor("x_my", [TPC, D], F32, kind="ExternalInput").ap()
    wq_in = nc.dram_tensor("wq_my", [D, ES], F32, kind="ExternalInput").ap()
    wk_in = nc.dram_tensor("wk_my", [D, ES], F32, kind="ExternalInput").ap()
    wv_in = nc.dram_tensor("wv_my", [D, ES], F32, kind="ExternalInput").ap()
    bq_in = nc.dram_tensor("bq_my", [1, ES], F32, kind="ExternalInput").ap()
    bk_in = nc.dram_tensor("bk_my", [1, ES], F32, kind="ExternalInput").ap()
    bv_in = nc.dram_tensor("bv_my", [1, ES], F32, kind="ExternalInput").ap()
    gq_in = nc.dram_tensor("gq", [1, HD], F32, kind="ExternalInput").ap()
    gk_in = nc.dram_tensor("gk", [1, HD], F32, kind="ExternalInput").ap()
    wfs_in = nc.dram_tensor("wfs_my", [ES, D], F32, kind="ExternalInput").ap()
    wos_in = nc.dram_tensor("wos_my", [ES, D], F32, kind="ExternalInput").ap()
    bf_in = nc.dram_tensor("bf", [1, D], F32, kind="ExternalInput").ap()
    bo_in = nc.dram_tensor("bo", [1, D], F32, kind="ExternalInput").ap()
    out_t = nc.dram_tensor("out", [TPC, D], F32, kind="ExternalOutput").ap()

    xA = nc.dram_tensor("xA", [TPC, D], F32, kind="Internal").ap()
    xB = nc.dram_tensor("xB", [TPC, D], F32, kind="Internal").ap()
    xq_d = nc.dram_tensor("xq_d", [TPC, D], BF16, kind="Internal").ap()
    qn_d = nc.dram_tensor("qn_d", [N, ES], BF16, kind="Internal").ap()
    kn_d = nc.dram_tensor("kn_d", [N, ES], BF16, kind="Internal").ap()
    wqkv_d = nc.dram_tensor("wqkv_d", [D, 3 * ES], BF16, kind="Internal").ap()

    ws_i = nc.dram_tensor("ws_i", [1, 8], F32, kind="Internal").ap()
    ws_o = nc.dram_tensor("ws_o", [1, 8], F32, kind="Internal").ap()
    wint_i = nc.dram_tensor("wint_i", [ES, D], BF16, kind="Internal").ap()
    wintT_i = nc.dram_tensor("wintT_i", [D, ES], BF16, kind="Internal").ap()
    wintT_o = nc.dram_tensor("wintT_o", [NC, D, ES], BF16,
                             kind="Internal").ap()
    wom_d = nc.dram_tensor("wom_d", [ES, D], BF16, kind="Internal").ap()
    h2_i = [nc.dram_tensor(f"h2_i{j}", [512, D], BF16, kind="Internal").ap()
            for j in range(2)]
    h2_o = [nc.dram_tensor(f"h2_o{j}", [NC, 512, D], BF16, kind="Internal").ap()
            for j in range(2)]
    at_d = nc.dram_tensor("at_d", [ES, B * N], BF16, kind="Internal").ap()
    rs_i = nc.dram_tensor("rs_i", [B * N, D], BF16, kind="Internal").ap()
    rs_o = nc.dram_tensor("rs_o", [TPC, D], BF16, kind="Internal").ap()

    from contextlib import ExitStack
    with ExitStack() as es:
        ARENA_KB = 176
        arena_t = es.enter_context(
            nc.sbuf_tensor("arena", [128, ARENA_KB * 1024], mybir.dt.uint8))
        _ar = arena_t.ap()

        _pers = [0]

        def P(dt, cols):
            sz = cols * {F32: 4, BF16: 2, I32: 4}[dt]
            off = (_pers[0] + 63) // 64 * 64
            _pers[0] = off + sz
            assert _pers[0] <= 28 * 1024
            return _ar[:, off:off + sz].bitcast(dt)

        def R(off_kb, dt, cols):
            off = int(off_kb * 1024)
            sz = cols * {F32: 4, BF16: 2, I32: 4}[dt]
            assert 28 * 1024 <= off and off + sz <= ARENA_KB * 1024
            return _ar[:, off:off + sz].bitcast(dt)

        bo_bc = P(BF16, D)
        bf_bc = P(BF16, D)
        bq_bc = P(F32, ES)
        bk_bc = P(F32, ES)
        bv_bc = P(F32, ES)
        gq_bc = P(F32, HD)
        gk_bc = P(F32, HD)
        ones_bf = P(BF16, 32)
        onesr_f = P(F32, 128)
        onescol = P(F32, 16)
        rx1 = P(F32, 16)
        rx2 = P(F32, 16)
        wsrow = P(F32, 16)
        ws_bc = P(F32, 16)
        mwc_bc = P(F32, 16)
        lrow = P(F32, 512)
        st1 = P(F32, 16)
        st2 = P(F32, 16)
        st3 = P(F32, 16)
        stp = [P(F32, 32), P(F32, 32)]
        stw_t = P(F32, 256)
        st3b = [P(F32, 16), P(F32, 16)]


        xt = [R(28, F32, D), R(44, F32, D)]
        lnt = R(60, F32, D)
        scr = R(76, F32, D)
        xa = [R(92, F32, D), R(108, F32, D)]
        tq = R(124, F32, D)
        i32t = R(140, I32, D)
        xqbf = [R(156, BF16, D), R(164, BF16, D)]
        h2st = [R(156, BF16, D), R(164, BF16, D)]
        xqT = R(28, BF16, 32 * 1024)
        at_sb = R(28, BF16, 32 * 1024)
        wring = [R(92 + i, BF16, 512) for i in range(8)]
        yt_t = [R(100, F32, 512), R(102, F32, 512)]
        g_t = [R(104, F32, 512), R(106, F32, 512)]
        xo = [R(108, F32, 512), R(110, F32, 512)]
        xr = [R(112 + 2 * i, F32, 512) for i in range(4)]
        wq_sb = R(28, BF16, 32 * ES)
        wk_sb = R(60, BF16, 32 * ES)
        wv_sb = R(92, BF16, 32 * ES)
        V_sb = R(124, BF16, 32 * ES)
        hring = [R(156 + i, BF16, 512) for i in range(8)]
        qtmp = R(164, F32, 512)
        sqq = R(166, F32, 512)
        qnst = [R(168, BF16, 512), R(169, BF16, 512)]
        atst = [R(170, BF16, 512), R(171, BF16, 512)]
        rbc_sb = R(172, F32, 512)
        qT = R(28, BF16, HPC * N)
        kT = R(60, BF16, HPC * N)
        Eb = [R(92, BF16, 16 * 512), R(108, BF16, 16 * 512)]
        at4 = [R(28 + 16 * e, BF16, B * N) for e in range(4)]
        wo_sb = R(124, BF16, HPC * D)
        stg6 = [R(92, BF16, D), R(100, BF16, D)]
        rsb_t = R(156, BF16, 2048)
        xbt_t = R(160, F32, 2048)
        x3st = R(168, F32, 2048)

        ps2a = es.enter_context(nc.psum_tensor("ps2a", [128, 1024], F32))
        ps2b = es.enter_context(nc.psum_tensor("ps2b", [128, 1024], F32))
        pav = es.enter_context(nc.psum_tensor("pav", [128, 512], F32))
        pl = es.enter_context(nc.psum_tensor("pl", [128, 512], F32))
        prb = es.enter_context(nc.psum_tensor("prb", [128, 512], F32))
        pex = es.enter_context(nc.psum_tensor("pex", [128, 512], F32))
        PS8 = [ps2a.ap()[:, 0:512], ps2a.ap()[:, 512:1024],
               ps2b.ap()[:, 0:512], ps2b.ap()[:, 512:1024],
               pav.ap()[:], pl.ap()[:], prb.ap()[:], pex.ap()[:]]

        S = {}
        for nm in ["prep", "x", "w", "t", "h2", "stx", "stq", "sth", "stw",
                   "cc", "pe", "vs", "sv", "vg", "sg", "sf", "out",
                   "vv", "sc", "wqkv"]:
            S[nm] = Sem(es.enter_context(nc.semaphore("s_" + nm)))
        WR = [Sem(es.enter_context(nc.semaphore(f"s_wr{i}"))) for i in range(8)]
        HR = [Sem(es.enter_context(nc.semaphore(f"s_hr{i}"))) for i in range(8)]

        prog = {e: [] for e in ["sync", "gpsimd", "vector", "scalar", "tensor"]}

        def em(eng, fn):
            prog[eng].append(fn)

        def wt(eng, sem, val):
            if val > 0:
                em(eng, lambda e, s=sem, v=val: e.wait_ge(s.h, v))

        def load(sem, dst, src, transpose=False, serial=True, queue="sync"):
            if serial and sem.n > 0:
                wt(queue, sem, sem.n)
            def fn(e, d=dst, r=src, t=transpose, s=sem):
                e.dma_start(out=d, in_=r, transpose=t).then_inc(s.h, 16)
            em(queue, fn)
            sem.n += 16
            return sem.n

        def store(sem, dst, src, serial=True, queue="gpsimd"):
            if serial and sem.n > 0:
                wt(queue, sem, sem.n)
            def fn(e, d=dst, r=src, s=sem):
                e.dma_start(out=d, in_=r).then_inc(s.h, 16)
            em(queue, fn)
            sem.n += 16
            return sem.n

        SELF = {"vector": "vv", "scalar": "sc"}

        def op(eng, name, *args, inc=None, bar=False, **kw):
            sem = inc if inc is not None else (S[SELF[eng]] if bar else None)
            def fn(e, name=name, args=args, kw=kw, sem=sem):
                ins = getattr(e, name)(*args, **kw)
                if sem is not None:
                    ins.then_inc(sem.h, 1)
            em(eng, fn)
            if sem is not None:
                sem.n += 1
                if bar:
                    wt(eng, sem, sem.n)
                return sem.n
            return None

        def mm(out, lhsT, rhs, start, stop, inc=None):
            return op("tensor", "matmul", out, lhsT, rhs,
                      start=start, stop=stop, inc=inc)

        def cc(kind, alu, i_, o_):
            def fn(e, k=kind, a=alu, i=i_, o=o_):
                e.collective_compute(k, a, replica_groups=RG, ins=[i],
                                     outs=[o]).then_inc(S["cc"].h, 1)
            em("gpsimd", fn)
            S["cc"].n += 1
            return S["cc"].n

        load(S["prep"], bq_bc[0:1, :], bq_in, serial=False)
        load(S["prep"], bk_bc[0:1, :], bk_in, serial=False)
        load(S["prep"], bv_bc[0:1, :], bv_in, serial=False)
        load(S["prep"], gq_bc[0:1, :], gq_in, serial=False)
        load(S["prep"], gk_bc[0:1, :], gk_in, serial=False)
        load(S["prep"], lnt[0:1, 0:D], bo_in, serial=False)
        n_all = load(S["prep"], scr[0:1, 0:D], bf_in, serial=False)
        n_bq = n_bk = n_bv = n_gq = n_gk = n_bo = n_bf = n_all

        em("vector", lambda e: e.memset(ones_bf[:, 0:1], 1.0))
        em("vector", lambda e: e.memset(onesr_f[0:1, :], 1.0))
        em("vector", lambda e: e.memset(onescol[:, 0:1], 1.0))

        wt("vector", S["prep"], n_gq)
        op("vector", "tensor_scalar", gq_bc[0:1, :], gq_bc[0:1, :], ALPHA,
           None, op0=ALU.mult, inc=S["vs"], bar=True)

        def bcast(row_ap, dst_ap, cols, dep_n):
            for j in range((cols + 511) // 512):
                c0, c1 = 512 * j, min(512 * (j + 1), cols)
                wt("tensor", S["prep"], dep_n)
                wt("tensor", S["vs"], S["vs"].n)
                pe_n = mm(prb.ap()[:, 0:c1 - c0], onesr_f[0:1, :],
                          row_ap[0:1, c0:c1], True, True, inc=S["pe"])
                wt("vector", S["pe"], pe_n)
                op("vector", "tensor_copy", dst_ap[:, c0:c1],
                   prb.ap()[:, 0:c1 - c0], inc=S["vs"])

        bcast(bq_bc, bq_bc, ES, n_bq)
        bcast(bk_bc, bk_bc, ES, n_bk)
        bcast(bv_bc, bv_bc, ES, n_bv)
        bcast(gq_bc, gq_bc, HD, n_gq)
        bcast(gk_bc, gk_bc, HD, n_gk)
        bcast(lnt, bo_bc, D, n_bo)
        bcast(scr, bf_bc, D, n_bf)

        p0_xt_g = [0, 0]
        for i in range(4):
            stg = xt[i % 2]
            wt("sync", S["vv"], p0_xt_g[i % 2])
            n_l = load(S["prep"], stg, wfs_in[128 * i:128 * (i + 1), :])
            wt("vector", S["prep"], n_l)
            p0_xt_g[i % 2] = op("vector", "tensor_reduce", st1[:, i:i + 1],
                                stg, AX.X, ALU.add,
                                apply_absolute_value=True, inc=S["vv"],
                                bar=True)
        v_n = op("vector", "tensor_reduce", st1[:, 4:5], st1[:, 0:4], AX.X,
                 ALU.add, inc=S["vs"], bar=True)
        wt("tensor", S["vs"], v_n)
        pe_n = mm(pex.ap()[0:1, 0:1], st1[:, 4:5], onescol[:, 0:1], True, True,
                  inc=S["pe"])
        wt("vector", S["pe"], pe_n)
        op("vector", "tensor_copy", wsrow[0:1, 0:1], pex.ap()[0:1, 0:1])
        vg_n = op("vector", "memset", wsrow[0:1, 1:8], 0.0, inc=S["vg"])
        wt("gpsimd", S["vg"], vg_n)
        st_n = store(S["stw"], ws_i, wsrow[0:1, 0:8])
        wt("gpsimd", S["stw"], st_n)
        cc_ws = cc("AllReduce", ALU.add, ws_i, ws_o)

        store(S["stq"], wqkv_d[:, 0:ES], wq_in)
        store(S["stq"], wqkv_d[:, ES:2 * ES], wk_in)
        st_wqkv = store(S["stq"], wqkv_d[:, 2 * ES:3 * ES], wv_in)
        st_wom = store(S["stw"], wom_d, wos_in)

        wt("sync", S["cc"], cc_ws)
        n_ws = load(S["x"], wsrow[0:1, 8:16], ws_o)
        wt("vector", S["x"], n_ws)
        op("vector", "tensor_scalar", wsrow[0:1, 0:1], wsrow[0:1, 8:9],
           MEAN_W, 1e-5, op0=ALU.mult, op1=ALU.max, bar=True)
        op("vector", "tensor_scalar", wsrow[0:1, 1:2], wsrow[0:1, 0:1],
           float(1.0 / 127.0), None, op0=ALU.mult, bar=True)
        v_n = op("vector", "reciprocal", wsrow[0:1, 2:3], wsrow[0:1, 0:1],
                 inc=S["vs"], bar=True)
        wt("tensor", S["vs"], v_n)
        mm(pex.ap()[:, 0:1], onesr_f[0:1, :], wsrow[0:1, 2:3], True, True)
        pe_n = mm(pex.ap()[:, 1:2], onesr_f[0:1, :], wsrow[0:1, 1:2], True,
                  True, inc=S["pe"])
        wt("vector", S["pe"], pe_n)
        op("vector", "tensor_copy", ws_bc[:, 0:1], pex.ap()[:, 0:1])
        WS_READY = op("vector", "tensor_copy", mwc_bc[:, 0:1], pex.ap()[:, 1:2],
                      inc=S["vs"])

        def ln_quant_phase(x_src, xa_dst, rx_sb, guards, chunks=None):
            cl = list(chunks) if chunks is not None else list(range(NCH))
            c0, ncl = cl[0], len(cl)
            resident = ncl <= 4
            if resident:
                xt_r = [R(28, F32, D), R(44, F32, D), R(60, F32, D),
                        R(76, F32, D)]
                scr_r = R(92, F32, D)
                xa_r = R(108, F32, D)
            C = lambda b: stw_t[:, 8 * b + c0:8 * b + c0 + ncl]
            Cc = lambda b, c: stw_t[:, 8 * b + c:8 * b + c + 1]
            p1v = [0] * 2
            for j, c in enumerate(cl):
                sl = c % 2
                xtile = xt_r[j] if resident else xt[sl]
                sq_t = scr_r if resident else scr
                if not resident:
                    wt("sync", S["vv"], p1v[sl])
                if c == cl[0]:
                    for gs, gv in guards:
                        wt("sync", gs, gv)
                n_x = load(S["x"], xtile, x_src[128 * c:128 * (c + 1), :])
                wt("vector", S["x"], n_x)
                op("vector", "tensor_reduce", Cc(0, c), xtile, AX.X, ALU.add)
                op("vector", "tensor_reduce", Cc(2, c), xtile, AX.X, ALU.max)
                op("vector", "tensor_reduce", Cc(3, c), xtile, AX.X, ALU.min)
                op("vector", "tensor_mul", sq_t, xtile, xtile, bar=True)
                p1v[sl] = op("vector", "tensor_reduce", Cc(1, c), sq_t, AX.X,
                             ALU.add, inc=S["vv"], bar=True)
            op("vector", "tensor_scalar", C(4), C(0), INV_D, None,
               op0=ALU.mult, bar=True)
            op("vector", "tensor_mul", C(5), C(4), C(4), bar=True)
            op("vector", "scalar_tensor_tensor", C(6), C(1), INV_D, C(5),
               op0=ALU.mult, op1=ALU.subtract, bar=True)
            v_n = op("vector", "tensor_scalar", C(5), C(6), 1e-5, None,
                     op0=ALU.add, inc=S["vs"], bar=True)
            wt("scalar", S["vs"], v_n)
            s_n = op("scalar", "activation", C(15), C(5), AF.Sqrt,
                     inc=S["sv"])
            wt("vector", S["sv"], s_n)
            op("vector", "reciprocal", C(7), C(15), bar=True)
            op("vector", "tensor_scalar", C(8), C(7), 0.5, 1.0,
               op0=ALU.mult, op1=ALU.add, bar=True)
            op("vector", "tensor_mul", C(5), C(4), C(7), bar=True)
            op("vector", "tensor_scalar", C(9), C(5), -0.5, None,
               op0=ALU.mult, bar=True)
            op("vector", "tensor_mul", C(15), C(6), C(7), bar=True)
            op("vector", "tensor_mul", C(15), C(15), C(7), bar=True)
            v_n = op("vector", "tensor_scalar", C(15), C(15), 1e-5, None,
                     op0=ALU.add, inc=S["vs"], bar=True)
            wt("scalar", S["vs"], v_n)
            s_n = op("scalar", "activation", C(15), C(15), AF.Sqrt,
                     inc=S["sv"])
            wt("vector", S["sv"], s_n)
            op("vector", "reciprocal", C(10), C(15), bar=True)
            op("vector", "tensor_sub", C(2), C(2), C(4), bar=True)
            op("vector", "tensor_sub", C(3), C(4), C(3), bar=True)
            op("vector", "tensor_max", C(2), C(2), C(3), bar=True)
            op("vector", "tensor_mul", C(2), C(2), C(7), bar=True)
            op("vector", "tensor_mul", C(11), C(2), C(10), bar=True)
            op("vector", "tensor_scalar", C(11), C(11), 1e-5, None,
               op0=ALU.max, bar=True)
            op("vector", "reciprocal", C(12), C(11), bar=True)
            op("vector", "tensor_scalar", C(12), C(12), 127.0, None,
               op0=ALU.mult, bar=True)
            op("vector", "tensor_mul", C(12), C(10), C(12), bar=True)
            op("vector", "tensor_mul", C(13), C(7), C(12), bar=True)
            op("vector", "tensor_mul", C(14), C(5), C(12), bar=True)
            op("vector", "tensor_scalar", C(14), C(14), -1.0, None,
               op0=ALU.mult, bar=True)
            v_st = op("vector", "tensor_scalar", rx_sb[:, c0:c0 + ncl], C(11),
                      mwc_bc[:, 0:1], None, op0=ALU.mult, inc=S["vs"],
                      bar=True)
            xa_g = [0] * 2
            xq_g = [0] * 2
            tq_g = [0, 0]
            p3_sv = [0, 0]
            tq2 = [tq, scr_r if resident else scr]
            for j, c in enumerate(cl):
                sl = c % 2
                if resident:
                    xtile = xt_r[j]
                    xa_t = xa_r
                else:
                    xtile = xt[sl]
                    xa_t = xa[sl]
                    if j < 2:
                        wt("sync", S["vv"], p1v[sl])
                    else:
                        wt("sync", S["sv"], p3_sv[sl])
                    n_x = load(S["x"], xtile, x_src[128 * c:128 * (c + 1), :])
                    wt("scalar", S["x"], n_x)
                if j == 0:
                    wt("scalar", S["vs"], v_st)
                wt("scalar", S["stx"], xa_g[0] if resident else xa_g[sl])
                s_xa = op("scalar", "activation", xa_t, xtile, AF.Identity,
                          bias=Cc(9, c), scale=Cc(8, c), inc=S["sg"])
                wt("gpsimd", S["sg"], s_xa)
                xa_g[0 if resident else sl] = store(
                    S["stx"], xa_dst[128 * c:128 * (c + 1), :], xa_t)
                wt("scalar", S["vv"], tq_g[sl])
                s_t = op("scalar", "activation", tq2[sl], xtile, AF.Identity,
                         bias=Cc(14, c), scale=Cc(13, c), inc=S["sv"])
                p3_sv[sl] = s_t
                wt("vector", S["sv"], s_t)
                tq_g[sl] = op("vector", "tensor_copy", i32t, tq2[sl],
                              inc=S["vv"], bar=True)
                wt("vector", S["stq"], xq_g[sl])
                vg_n = op("vector", "tensor_copy", xqbf[sl], i32t, inc=S["vg"])
                wt("gpsimd", S["vg"], vg_n)
                xq_g[sl] = store(S["stq"], xq_d[128 * c:128 * (c + 1), :],
                                 xqbf[sl])
            return None

        P0_PE = S["pe"].n
        wint_st = 0
        p2_xt_g = [p0_xt_g[0], p0_xt_g[1]]
        p2_tq_g = 0
        for i in range(4):
            sl = i % 2
            if i < 2:
                wt("sync", S["vv"], p0_xt_g[sl])
            else:
                wt("sync", S["sv"], p2_xt_g[sl])
            n_w = load(S["x"], xt[sl], wfs_in[128 * i:128 * (i + 1), :])
            wt("scalar", S["x"], n_w)
            if i == 0:
                wt("scalar", S["vs"], WS_READY)
            wt("scalar", S["vv"], p2_tq_g)
            s_n = op("scalar", "activation", tq, xt[sl], AF.Copy,
                     scale=ws_bc[:, 0:1], inc=S["sv"])
            p2_xt_g[sl] = s_n
            wt("vector", S["sv"], s_n)
            p2_tq_g = op("vector", "tensor_copy", i32t, tq, inc=S["vv"],
                         bar=True)
            wt("vector", S["stw"], wint_st)
            vg_n = op("vector", "tensor_scalar", xqbf[sl], i32t, 1.0, -1.0,
                      op0=ALU.min, op1=ALU.max, inc=S["vg"])
            wt("gpsimd", S["vg"], vg_n)
            wint_st = store(S["stw"], wint_i[128 * i:128 * (i + 1), :],
                            xqbf[sl])
        P2A_SV = S["sv"].n
        P2A_VV = S["vv"].n
        P2A_VG = S["vg"].n
        wtT = [R(60, BF16, ES), R(68, BF16, ES)]
        wt_stl = []
        for k in range(32):
            sl = k % 2
            wt("sync", S["stw"], wint_st if k < 2 else wt_stl[k - 2])
            n_l = load(WR[sl], wtT[sl], wint_i[0:ES, 128 * k:128 * (k + 1)],
                       transpose=True, serial=False)
            wt("gpsimd", WR[sl], n_l)
            wt_stl.append(store(S["stw"], wintT_i[128 * k:128 * (k + 1), :],
                                wtT[sl]))
        wt("gpsimd", S["stw"], wt_stl[-1])
        cc_wint = cc("AllGather", ALU.bypass, wintT_i, wintT_o)
        wintT_flat = wintT_o.rearrange("r p d -> (r p) d")
        P2_STW = S["stw"].n

        g_p1 = [(S["vs"], WS_READY), (S["sv"], P2A_SV), (S["vv"], P2A_VV),
                (S["vg"], P2A_VG), (S["stw"], wint_st), (S["pe"], P0_PE)]
        ln_quant_phase(x_in, xA, rx1, g_p1)
        P1_STX = S["stx"].n
        P1_STQ = S["stq"].n
        P1_SV = S["sv"].n
        P1_VG = S["vg"].n

        wring_g = [0] * 8
        widx = [0]
        xr_g = [0] * 4
        xo_g = {"stx": [0, 0], "out": [0, 0]}

        def ffn_like_phase(dst_sem_name, lhsT_sb, rhs_src_fn, rhs_transpose,
                           res_src, dst, epi_fn, guards_sync, guards_tensor,
                           per_k_guards=None, lhs_k_guards=None,
                           ts_list=(0, 1), og_list=None, guards_issue=()):
            st_sem = S[dst_sem_name]
            bank_g = [S["vs"].n] * 2
            for gs, gv in guards_sync:
                wt("sync", gs, gv)
            for gs, gv in guards_tensor:
                wt("tensor", gs, gv)
            ts0 = list(ts_list)[0]
            ogl = list(og_list) if og_list is not None else list(range(8))
            og0 = ogl[0]
            steps = [(ts, og, k) for ts in ts_list for og in ogl
                     for k in range(32)]
            w0 = widx[0]
            PE0 = S["pe"].n
            n_ws = [0] * len(steps)

            def issue_step(j):
                ts_, og_, k_ = steps[j]
                if j >= 8:
                    wt("scalar", S["pe"], PE0 + j - 7)
                n_ws[j] = load(WR[(w0 + j) % 8], wring[(w0 + j) % 8],
                               rhs_src_fn(og_, k_), serial=False,
                               queue="scalar")

            wt("scalar", S["pe"], PE0)
            for gs, gv in guards_tensor:
                wt("scalar", gs, gv)
            for gs, gv in guards_issue:
                wt("scalar", gs, gv)
            for j in range(min(8, len(steps))):
                issue_step(j)
            for j, (ts, og, k) in enumerate(steps):
                sl = (w0 + j) % 8
                bg = og % 2
                if k == 0:
                    wt("tensor", S["vs"], bank_g[bg])
                if lhs_k_guards is not None and ts == ts0 and og == og0:
                    wt("tensor", *lhs_k_guards[k])
                wt("tensor", WR[sl], n_ws[j])
                for tt in range(4):
                    t0 = 1024 * k + 512 * ts + 128 * tt
                    pe_group = mm(PS8[4 * bg + tt],
                                  lhsT_sb[:, t0:t0 + 128],
                                  wring[sl][:], k == 0, k == 31,
                                  inc=S["pe"] if tt == 3 else None)
                widx[0] += 1
                if j + 8 < len(steps):
                    issue_step(j + 8)
                if k == 31:
                    last_vs = 0
                    for tt in range(4):
                        tg = 4 * ts + tt
                        rows = slice(128 * tg, 128 * (tg + 1))
                        cols = slice(512 * og, 512 * (og + 1))
                        xsl = (2 * og + tt) % 4
                        wt("sync", S["vg"], xr_g[xsl])
                        n_xr = load(S["x"], xr[xsl], res_src[rows, cols])
                        last_vs = epi_fn(bg, tt, tg, rows, cols, xsl, n_xr,
                                         pe_group, st_sem, dst)
                    bank_g[bg] = last_vs

        def make_ffn_epi(rx_sb, st_sem_name):
            g_guard = [0, 0]

            def epi(bg, tt, tg, rows, cols, xsl, n_xr, pe_group, st_sem, dst):
                ysl = tt % 2
                wt("vector", S["pe"], pe_group)
                wt("vector", S["sv"], g_guard[ysl])
                s_n = op("vector", "scalar_tensor_tensor", yt_t[ysl],
                         PS8[4 * bg + tt], rx_sb[:, tg:tg + 1], bf_bc[:, cols],
                         op0=ALU.mult, op1=ALU.add, inc=S["vs"])
                wt("scalar", S["vs"], s_n)
                s_g = op("scalar", "activation", g_t[ysl], yt_t[ysl], AF.Gelu,
                         inc=S["sv"])
                g_guard[ysl] = s_g
                osl = tt % 2
                wt("vector", S["sv"], s_g)
                wt("vector", S["x"], n_xr)
                wt("vector", st_sem, xo_g[st_sem_name][osl])
                vg_n = op("vector", "scalar_tensor_tensor", xo[osl], g_t[ysl],
                          0.5, xr[xsl], op0=ALU.mult, op1=ALU.add, inc=S["vg"])
                xr_g[xsl] = vg_n
                wt("gpsimd", S["vg"], vg_n)
                xo_g[st_sem_name][osl] = store(st_sem, dst[rows, cols],
                                               xo[osl])
                return s_n
            return epi

        def ffn_phase(rx_sb, res_src, dst, st_sem_name, guards):
            for gs, gv in guards:
                wt("sync", gs, gv)
            xqt_k = []
            for k in range(32):
                xqt_k.append((HR[k % 4],
                              load(HR[k % 4], xqT[:, 1024 * k:1024 * (k + 1)],
                                   xq_d[0:TPC, 128 * k:128 * (k + 1)],
                                   transpose=True, serial=True)))
            ffn_like_phase(
                st_sem_name, xqT,
                lambda og, k: wintT_flat[4096 * og + 128 * k:
                                         4096 * og + 128 * (k + 1), :],
                False, res_src, dst, make_ffn_epi(rx_sb, st_sem_name),
                [], [(S["cc"], cc_wint)], lhs_k_guards=xqt_k,
                guards_issue=guards)

        if stage >= 3:
            ffn_phase(rx1, xA, xB, "stx",
                      [(S["stq"], P1_STQ), (S["stx"], P1_STX)])
        P3_PE = S["pe"].n
        P3_STX = S["stx"].n
        P3_SV = S["sv"].n
        P3_VG = S["vg"].n

        cc_h2 = [0, 0]
        P4_STH = 0
        if stage >= 5:
            h2_g = [0] * 2
            CBc = lambda bk, q: stw_t[:, 8 * bk + q:8 * bk + q + 1]
            xt_r4 = [R(28, F32, D), R(44, F32, D), R(60, F32, D),
                     R(76, F32, D)]
            scr4 = R(92, F32, D)
            p4_sg_last = 0
            for half in range(2):
                cls_ = range(4 * half, 4 * half + 4)
                cb0 = 4 * half
                CB = lambda bk: stw_t[:, 8 * bk + cb0:8 * bk + cb0 + 4]
                for j, c in enumerate(cls_):
                    if j == 0:
                        if half == 0:
                            wt("sync", S["pe"], P3_PE)
                            wt("sync", S["stx"], P3_STX)
                            wt("sync", S["sv"], S["sv"].n)
                            wt("sync", S["vg"], S["vg"].n)
                        else:
                            wt("sync", S["sg"], p4_sg_last)
                    n_x = load(S["x"], xt_r4[j], xB[128 * c:128 * (c + 1), :])
                    wt("vector", S["x"], n_x)
                    op("vector", "tensor_reduce", CBc(0, c), xt_r4[j], AX.X,
                       ALU.add)
                    op("vector", "tensor_mul", scr4, xt_r4[j], xt_r4[j],
                       bar=True)
                    op("vector", "tensor_reduce", CBc(1, c), scr4, AX.X,
                       ALU.add, bar=True)
                op("vector", "tensor_scalar", CB(4), CB(0), INV_D, None,
                   op0=ALU.mult, bar=True)
                op("vector", "tensor_mul", CB(5), CB(4), CB(4), bar=True)
                op("vector", "scalar_tensor_tensor", CB(6), CB(1), INV_D,
                   CB(5), op0=ALU.mult, op1=ALU.subtract, bar=True)
                v_n = op("vector", "tensor_scalar", CB(5), CB(6), 1e-5, None,
                         op0=ALU.add, inc=S["vs"], bar=True)
                wt("scalar", S["vs"], v_n)
                s_n = op("scalar", "activation", CB(3), CB(5), AF.Sqrt,
                         inc=S["sv"])
                wt("vector", S["sv"], s_n)
                op("vector", "reciprocal", CB(7), CB(3), bar=True)
                v_st4 = op("vector", "scalar_tensor_tensor", CB(9), CB(4),
                           -1.0, CB(7), op0=ALU.mult, op1=ALU.mult,
                           inc=S["vs"], bar=True)
                for j, c in enumerate(cls_):
                    sl = c % 2
                    if j == 0:
                        wt("scalar", S["vs"], v_st4)
                    wt("scalar", S["sth"], h2_g[sl])
                    sg_n = op("scalar", "activation", h2st[sl], xt_r4[j],
                              AF.Identity, bias=CBc(9, c), scale=CBc(7, c),
                              inc=S["sg"])
                    p4_sg_last = sg_n
                    wt("gpsimd", S["sg"], sg_n)
                    h2_g[sl] = store(
                        S["sth"],
                        h2_i[half][128 * (c % 4):128 * (c % 4) + 128, :],
                        h2st[sl])
                wt("gpsimd", S["sth"], S["sth"].n)
                cc_h2[half] = cc("AllGather", ALU.bypass, h2_i[half],
                                 h2_o[half])
            P4_STH = S["sth"].n

        hring_g = [0] * 8
        qn_g = [0] * 2
        at_g = [0] * 2
        cc_at = [0, 0]
        if stage >= 5:
            Ehalf_g = [0, 0]
            pair_g = [0, 0]
            pl_g = 0
            pavrb_g = 0
            rank1_g = 0
            for b in range(1 if dump == 24 else B):
                wt("sync", S["pe"], S["pe"].n)
                wt("sync", S["sf"], S["sf"].n)
                wt("sync", S["sv"], S["sv"].n)
                wt("sync", S["vg"], S["vg"].n)
                wt("sync", S["stq"], st_wqkv if b == 0 else S["stq"].n)
                if b == 0 and stage >= 6:
                    wt("sync", S["sth"], P4_STH)
                n_wqkv = 0
                for (w_sb, off) in ((wq_sb, 0), (wk_sb, ES), (wv_sb, 2 * ES)):
                    for k in range(32):
                        n_wqkv = load(S["wqkv"], w_sb[:, ES * k:ES * (k + 1)],
                                      wqkv_d[128 * k:128 * (k + 1),
                                             off:off + ES], serial=False)
                wt("tensor", S["wqkv"], n_wqkv)
                wt("sync", S["cc"], cc_h2[b])
                if b == 0:
                    wt("tensor", S["cc"], cc_h2[1])

                qkv_bank_g = S["vs"].n
                v_bank_g = S["vs"].n
                qkv_sq_g = S["sv"].n
                for tc in range(8):
                    r = tc
                    Xh = b
                    wt("tensor", S["vs"], qkv_bank_g)
                    pe_qk = 0
                    for k in range(32):
                        sl = k % 8
                        wt("sync", S["pe"], hring_g[sl])
                        n_h = load(HR[sl], hring[sl],
                                   h2_o[Xh][r][0:512, 128 * k:128 * (k + 1)],
                                   transpose=True, serial=False)
                        wt("tensor", HR[sl], n_h)
                        for tt in range(4):
                            mm(PS8[tt], hring[sl][:, 128 * tt:128 * (tt + 1)],
                               wq_sb[:, ES * k:ES * (k + 1)], k == 0, k == 31)
                            pe_qk = mm(PS8[4 + tt],
                                       hring[sl][:, 128 * tt:128 * (tt + 1)],
                                       wk_sb[:, ES * k:ES * (k + 1)],
                                       k == 0, k == 31,
                                       inc=S["pe"] if tt == 3 else None)
                        hring_g[sl] = pe_qk
                    vs_qk = 0
                    vs_q4 = 0
                    sq_guard = qkv_sq_g
                    for qk in range(2):
                        bias_bc = bq_bc if qk == 0 else bk_bc
                        g_bc = gq_bc if qk == 0 else gk_bc
                        dstd = qn_d if qk == 0 else kn_d
                        for tt in range(4):
                            bank = PS8[4 * qk + tt]
                            wt("vector", S["pe"], pe_qk)
                            wt("vector", S["sv"], sq_guard)
                            s_n = op("vector", "scalar_tensor_tensor", qtmp,
                                     bank, 1.0, bias_bc, op0=ALU.mult,
                                     op1=ALU.add, inc=S["vs"], bar=True)
                            vs_qk = s_n
                            wt("scalar", S["vs"], s_n)
                            for hh in range(4):
                                cseg = slice(128 * hh, 128 * (hh + 1))
                                sq_guard = op(
                                    "scalar", "activation", sqq[:, cseg],
                                    qtmp[:, cseg], AF.Square,
                                    accum_out=st3[:, hh:hh + 1],
                                    inc=S["sv"] if hh == 3 else None)
                            wt("vector", S["sv"], sq_guard)
                            v_n = op("vector", "tensor_scalar", st3[:, 0:4],
                                     st3[:, 0:4], float(1.0 / HD), 1e-6,
                                     op0=ALU.mult, op1=ALU.add, inc=S["vs"],
                                     bar=True)
                            wt("scalar", S["vs"], v_n)
                            s_n = op("scalar", "activation", st3[:, 4:8],
                                     st3[:, 0:4], AF.Sqrt, inc=S["sv"])
                            wt("vector", S["sv"], s_n)
                            op("vector", "reciprocal", st3[:, 8:12],
                               st3[:, 4:8], bar=True)
                            qsl = (2 * qk + tt) % 2
                            wt("vector", S["stq"], qn_g[qsl])
                            vg_n = 0
                            for hh in range(4):
                                cseg = slice(128 * hh, 128 * (hh + 1))
                                vg_n = op("vector", "scalar_tensor_tensor",
                                          qnst[qsl][:, cseg], qtmp[:, cseg],
                                          st3[:, 8 + hh:9 + hh], g_bc,
                                          op0=ALU.mult, op1=ALU.mult,
                                          inc=S["vg"] if hh == 3 else None)
                            wt("gpsimd", S["vg"], vg_n)
                            trow = 512 * tc + 128 * tt
                            qn_g[qsl] = store(S["stq"],
                                              dstd[trow:trow + 128, :],
                                              qnst[qsl])
                        if qk == 0:
                            vs_q4 = vs_qk
                    qkv_sq_g = sq_guard
                    wt("tensor", S["vs"], vs_q4)
                    pe_v = 0
                    for k in range(32):
                        sl = k % 8
                        wt("sync", S["pe"], hring_g[sl])
                        n_h = load(HR[sl], hring[sl],
                                   h2_o[Xh][r][0:512, 128 * k:128 * (k + 1)],
                                   transpose=True, serial=False)
                        wt("tensor", HR[sl], n_h)
                        for tt in range(4):
                            pe_v = mm(PS8[tt],
                                      hring[sl][:, 128 * tt:128 * (tt + 1)],
                                      wv_sb[:, ES * k:ES * (k + 1)],
                                      k == 0, k == 31,
                                      inc=S["pe"] if tt == 3 else None)
                        hring_g[sl] = pe_v
                    for tt in range(4):
                        wt("vector", S["pe"], pe_v)
                        cseg = slice(512 * (4 * tc + tt),
                                     512 * (4 * tc + tt + 1))
                        qkv_bank_g = op("vector", "scalar_tensor_tensor",
                                        V_sb[:, cseg], PS8[tt], 1.0, bv_bc,
                                        op0=ALU.mult, op1=ALU.add, inc=S["vs"])
                QKV_STQ = S["stq"].n
                QKV_PE = S["pe"].n
                QKV_VS = S["vs"].n

                wt("sync", S["stq"], QKV_STQ)
                wt("sync", S["pe"], QKV_PE)
                n_qt = 0
                for (dsrc, dsb) in ((qn_d, qT), (kn_d, kT)):
                    for h in range(HPC):
                        n_qt = load(S["t"], dsb[:, N * h:N * (h + 1)],
                                    dsrc[0:N, 128 * h:128 * (h + 1)],
                                    transpose=True, serial=False)
                wt("tensor", S["t"], n_qt)
                wt("tensor", S["vs"], QKV_VS)

                for h in range(HPC):
                    for qc in range(8):
                        qcols = slice(N * h + 512 * qc,
                                      N * h + 512 * (qc + 1))
                        av_n = 0
                        for half in range(2):
                            exp_n = 0
                            for g_ in range(8):
                                pr = g_ % 2
                                ps_pair = ps2a if pr == 0 else ps2b
                                wt("tensor", S["sf"], pair_g[pr])
                                pe_n = 0
                                for j in range(2):
                                    kt = 16 * half + 2 * g_ + j
                                    kcol = slice(N * h + 128 * kt,
                                                 N * h + 128 * (kt + 1))
                                    pe_n = mm(
                                        ps_pair.ap()[:, 512 * j:512 * (j + 1)],
                                        kT[:, kcol], qT[:, qcols], True, True,
                                        inc=S["pe"] if j == 1 else None)
                                wt("scalar", S["pe"], pe_n)
                                wt("scalar", S["pe"], Ehalf_g[half])
                                ecol = slice(1024 * g_, 1024 * (g_ + 1))
                                exp_n = op("scalar", "activation",
                                           Eb[half][:, ecol], ps_pair.ap()[:],
                                           AF.Exp, inc=S["sf"])
                                pair_g[pr] = exp_n
                            wt("tensor", S["sf"], exp_n)
                            if half == 0:
                                wt("tensor", S["vg"], pavrb_g)
                                wt("tensor", S["vs"], pl_g)
                            for kt16 in range(16):
                                kt = 16 * half + kt16
                                erhs = Eb[half][:, 512 * kt16:512 * (kt16 + 1)]
                                vcol = slice(512 * kt + 128 * h,
                                             512 * kt + 128 * (h + 1))
                                mm(pav.ap()[:], V_sb[:, vcol], erhs,
                                   kt == 0, kt == 31)
                            for kt16 in range(16):
                                kt = 16 * half + kt16
                                erhs = Eb[half][:, 512 * kt16:512 * (kt16 + 1)]
                                av_n = mm(pl.ap()[0:1, :], ones_bf[:, 0:1],
                                          erhs, kt == 0, kt == 31,
                                          inc=S["pe"] if kt16 == 15 else None)
                            Ehalf_g[half] = av_n
                        wt("vector", S["pe"], av_n)
                        wt("vector", S["pe"], rank1_g)
                        pl_g = op("vector", "reciprocal", lrow[0:1, :],
                                  pl.ap()[0:1, :], inc=S["vs"])
                        wt("tensor", S["vs"], pl_g)
                        rank1_g = mm(prb.ap()[:], onesr_f[0:1, :],
                                     lrow[0:1, :], True, True, inc=S["pe"])
                        asl = qc % 2
                        wt("vector", S["pe"], rank1_g)
                        op("vector", "tensor_copy", rbc_sb, prb.ap()[:],
                           bar=True)
                        wt("vector", S["sth"], at_g[asl])
                        vg_n = op("vector", "tensor_mul", atst[asl],
                                  pav.ap()[:], rbc_sb, inc=S["vg"])
                        pavrb_g = vg_n
                        wt("gpsimd", S["vg"], vg_n)
                        gcol = 4096 * b + 512 * qc
                        at_g[asl] = store(
                            S["sth"],
                            at_d[128 * h:128 * (h + 1), gcol:gcol + 512],
                            atst[asl])

        cc_rs = [0, 0]
        if stage >= 6:
            wt("sync", S["sth"], S["sth"].n)
            wt("sync", S["pe"], S["pe"].n)
            wt("sync", S["sf"], S["sf"].n)
            wt("sync", S["vg"], S["vg"].n)
            wt("sync", S["stw"], st_wom)
            n_at = 0
            for e in range(4):
                n_at = load(S["t"], at4[e], at_d[128 * e:128 * (e + 1), :],
                            serial=False)
            n_wo = 0
            for e in range(4):
                n_wo = load(S["wqkv"], wo_sb[:, D * e:D * (e + 1)],
                            wom_d[128 * e:128 * (e + 1), :], serial=False)
            wt("tensor", S["t"], n_at)
            wt("tensor", S["wqkv"], n_wo)
            wt("tensor", S["vs"], S["vs"].n)
            rnd_g = [S["vs"].n] * 2
            stg_g = [0, 0]
            rs_st = 0
            for T in range(64):
                ssl = T % 2
                vs_cp = 0
                for rnd in range(2):
                    gr = (2 * T + rnd) % 2
                    wt("tensor", S["vs"], rnd_g[gr])
                    pe_n = 0
                    for e in range(4):
                        for oc in range(4):
                            o = 4 * rnd + oc
                            pe_n = mm(PS8[4 * gr + oc],
                                      at4[e][:, 128 * T:128 * (T + 1)],
                                      wo_sb[:, D * e + 512 * o:D * e + 512 * (o + 1)],
                                      e == 0, e == 3,
                                      inc=S["pe"] if oc == 3 else None)
                    for oc in range(4):
                        o = 4 * rnd + oc
                        wt("vector", S["pe"], pe_n)
                        if rnd == 0 and oc == 0:
                            wt("vector", S["stx"], stg_g[ssl])
                        vs_cp = op("vector", "tensor_copy",
                                   stg6[ssl][:, 512 * o:512 * (o + 1)],
                                   PS8[4 * gr + oc], inc=S["vs"])
                    rnd_g[gr] = vs_cp
                wt("gpsimd", S["vs"], vs_cp)
                stg_g[ssl] = store(S["stx"], rs_i[128 * T:128 * (T + 1), :],
                                   stg6[ssl])
                if T == 31:
                    wt("gpsimd", S["stx"], S["stx"].n)
                    cc_rs[0] = cc("ReduceScatter", ALU.add,
                                  rs_i[0:4096, :], rs_o[0:512, :])
            P6A_STX_RS = S["stx"].n
            P6A_PE = S["pe"].n

            wint8_rhs = lambda og, k: wintT_flat[4096 * og + 128 * k:
                                                 4096 * og + 128 * (k + 1), :]
            wt("sync", S["cc"], cc_rs[0])
            x3_st_g = 0
            for j in range(8):
                c2, hf = divmod(j, 2)
                rows = slice(128 * c2, 128 * (c2 + 1))
                cols = slice(2048 * hf, 2048 * (hf + 1))
                wt("sync", S["vg"], S["vg"].n)
                n_r = load(S["x"], rsb_t, rs_o[rows, cols])
                n_x2 = load(S["x"], xbt_t, xB[rows, cols])
                wt("vector", S["x"], n_x2)
                wt("vector", S["stx"], x3_st_g)
                op("vector", "scalar_tensor_tensor", x3st, rsb_t, 1.0,
                   xbt_t, op0=ALU.mult, op1=ALU.add, bar=True)
                vg_n = op("vector", "tensor_add", x3st, x3st,
                          bo_bc[:, cols], inc=S["vg"])
                wt("gpsimd", S["vg"], vg_n)
                x3_st_g = store(S["stx"], xA[rows, cols], x3st)
            wt("gpsimd", S["stx"], P6A_STX_RS)
            wt("gpsimd", S["x"], S["x"].n)
            cc_rs[1] = cc("ReduceScatter", ALU.add,
                          rs_i[4096:8192, :], rs_o[512:1024, :])
            P6B0_STX = S["stx"].n
            if stage >= 8:
                g_p7a = [(S["stx"], P6B0_STX), (S["pe"], P6A_PE),
                         (S["sv"], S["sv"].n), (S["vg"], S["vg"].n),
                         (S["stq"], S["stq"].n), (S["stw"], S["stw"].n),
                         (S["sth"], S["sth"].n), (S["sf"], S["sf"].n)]
                ln_quant_phase(xA, xB, rx2, g_p7a, chunks=range(0, 4))
                epi8 = make_ffn_epi(rx2, "out")
                g_p8a = [(S["stq"], S["stq"].n), (S["stx"], S["stx"].n)]
                for gs, gv in g_p8a:
                    wt("sync", gs, gv)
                xqt_a = []
                for k in range(32):
                    xqt_a.append((HR[k % 4],
                                  load(HR[k % 4],
                                       xqT[:, 1024 * k:1024 * k + 512],
                                       xq_d[0:512, 128 * k:128 * (k + 1)],
                                       transpose=True, serial=True)))
                ffn_like_phase("out", xqT, wint8_rhs, False, xB, out_t,
                               epi8, [], [], lhs_k_guards=xqt_a,
                               ts_list=(0,), guards_issue=g_p8a)
            wt("sync", S["cc"], cc_rs[1])
            wt("sync", S["stq"], S["stq"].n)
            x3_st_g1 = 0
            for c2 in range(4):
                rows = slice(512 + 128 * c2, 512 + 128 * (c2 + 1))
                for hf in range(2):
                    cols = slice(2048 * hf, 2048 * (hf + 1))
                    wt("sync", S["vg"], S["vg"].n)
                    n_r = load(S["x"], rsb_t, rs_o[rows, cols])
                    n_x2 = load(S["x"], xbt_t, xB[rows, cols])
                    wt("vector", S["x"], n_x2)
                    wt("vector", S["stx"], x3_st_g1)
                    op("vector", "scalar_tensor_tensor", x3st, rsb_t, 1.0,
                       xbt_t, op0=ALU.mult, op1=ALU.add, bar=True)
                    vg_n = op("vector", "tensor_add", x3st, x3st,
                              bo_bc[:, cols], inc=S["vg"])
                    wt("gpsimd", S["vg"], vg_n)
                    x3_st_g1 = store(S["stx"], xA[rows, cols], x3st)
        P6_STX = S["stx"].n
        P6_PE = S["pe"].n

        if stage >= 8:
            g_p7b = [(S["stx"], P6_STX), (S["pe"], P6_PE),
                     (S["sv"], S["sv"].n), (S["vg"], S["vg"].n),
                     (S["stq"], S["stq"].n), (S["stw"], S["stw"].n),
                     (S["sth"], S["sth"].n), (S["sf"], S["sf"].n)]
            ln_quant_phase(xA, xB, rx2, g_p7b, chunks=range(4, 8))
            g_p8b = [(S["stq"], S["stq"].n), (S["stx"], S["stx"].n)]
            for gs, gv in g_p8b:
                wt("sync", gs, gv)
            xqt_b = []
            for k in range(32):
                xqt_b.append((HR[k % 4],
                              load(HR[k % 4],
                                   xqT[:, 1024 * k + 512:1024 * (k + 1)],
                                   xq_d[512:1024, 128 * k:128 * (k + 1)],
                                   transpose=True, serial=True)))
            ffn_like_phase("out", xqT, wint8_rhs, False, xB, out_t, epi8,
                           [], [], lhs_k_guards=xqt_b, ts_list=(1,),
                           guards_issue=g_p8b)

        if dump:
            for nm in ["stx", "stq", "sth", "stw", "cc", "vg", "sg"]:
                wt("gpsimd", S[nm], S[nm].n)
            if dump == 21:
                store(S["out"], out_t[:, 0:D // 2].bitcast(BF16), xq_d)
            elif dump == 22:
                store(S["out"], out_t[0:512, 0:D // 2].bitcast(BF16), wint_i)
            elif dump == 23:
                store(S["out"], out_t[0:512, 0:D // 2].bitcast(BF16)
                      .rearrange("p (a b) -> p a b", a=2)
                      .rearrange("p a b -> (p a) b") if False else
                      out_t.bitcast(BF16)[0:512, 0:B * N], at_d)
            elif dump == 24:
                store(S["out"], out_t.bitcast(BF16)[0:1024, 0:ES],
                      qn_d[1280:2304])
        elif stage < 8:
            src = {1: xA, 3: xB, 6: xA}[stage]
            for nm in ["stx", "stq", "sth", "stw", "cc", "vg", "sg"]:
                wt("gpsimd", S[nm], S[nm].n)
            store(S["out"], out_t, src)

        wt("sync", S["out"], S["out"].n)
        wt("gpsimd", S["out"], S["out"].n)

        block = es.enter_context(nc.Block())
        @block.sync
        def _(e):
            for fn in prog["sync"]:
                fn(e)

        @block.gpsimd
        def _(e):
            for fn in prog["gpsimd"]:
                fn(e)

        @block.vector
        def _(e):
            for fn in prog["vector"]:
                fn(e)

        @block.scalar
        def _(e):
            for fn in prog["scalar"]:
                fn(e)

        @block.tensor
        def _(e):
            for fn in prog["tensor"]:
                fn(e)

    return nc


def kernel(L0_attn_hidden_states, L0_attn_Wq, L0_attn_bq, L0_attn_Wk,
           L0_attn_bk, L0_attn_Wv, L0_attn_bv, L0_attn_Wo, L0_attn_bo,
           L0_attn_gq, L0_attn_gk, L0_ffn_W, L0_ffn_b):
    x = np.ascontiguousarray(
        np.asarray(L0_attn_hidden_states, dtype=np.float32).reshape(B * N, D))
    Wq = np.asarray(L0_attn_Wq, np.float32)
    Wk = np.asarray(L0_attn_Wk, np.float32)
    Wv = np.asarray(L0_attn_Wv, np.float32)
    Wo = np.asarray(L0_attn_Wo, np.float32)
    Wf = np.asarray(L0_ffn_W, np.float32)
    bq = np.asarray(L0_attn_bq, np.float32).reshape(1, D)
    bk = np.asarray(L0_attn_bk, np.float32).reshape(1, D)
    bv = np.asarray(L0_attn_bv, np.float32).reshape(1, D)
    bo = np.asarray(L0_attn_bo, np.float32).reshape(1, D)
    bf_ = np.asarray(L0_ffn_b, np.float32).reshape(1, D)
    gq = np.asarray(L0_attn_gq, np.float32).reshape(1, HD)
    gk = np.asarray(L0_attn_gk, np.float32).reshape(1, HD)

    in_maps = []
    for c in range(NC):
        esl = slice(ES * c, ES * (c + 1))
        myrows = np.r_[512 * c:512 * (c + 1), N + 512 * c:N + 512 * (c + 1)]
        in_maps.append({
            "x_my": np.ascontiguousarray(x[myrows]),
            "wq_my": np.ascontiguousarray(Wq[:, esl]),
            "wk_my": np.ascontiguousarray(Wk[:, esl]),
            "wv_my": np.ascontiguousarray(Wv[:, esl]),
            "bq_my": np.ascontiguousarray(bq[:, esl]),
            "bk_my": np.ascontiguousarray(bk[:, esl]),
            "bv_my": np.ascontiguousarray(bv[:, esl]),
            "gq": gq, "gk": gk,
            "wfs_my": np.ascontiguousarray(Wf[esl, :]),
            "wos_my": np.ascontiguousarray(Wo[esl, :]),
            "bf": bf_, "bo": bo,
        })
    nc = build_nc(_STAGE)
    res = run_bass_kernel_spmd(nc, in_maps, core_ids=list(range(NC)))
    out = np.empty((B * N, D), np.float32)
    for c in range(NC):
        oc = res.results[c]["out"]
        out[512 * c:512 * (c + 1)] = oc[0:512]
        out[N + 512 * c:N + 512 * (c + 1)] = oc[512:1024]
    return out.reshape(B, N, D)
